# version 09
# baseline (speedup 1.0000x reference)
import contextlib
import numpy as np
import ml_dtypes
import concourse.bass as bass
import concourse.mybir as mybir
from concourse.bass_utils import run_bass_kernel_spmd
from concourse.bass_core_rust import IndirectOffsetOnAxis

F32 = mybir.dt.float32
BF16 = mybir.dt.bfloat16
AX = mybir.AxisListType.X
AF = mybir.ActivationFunctionType
ALU = mybir.AluOpType

B, S, D, H, DH, E = 4, 4096, 1024, 16, 64, 8
NT = S // 128
NOWN = NT // 2
NPAIR = H // 2
EXP_SHIFT = 20.0
LN_EPS = 1e-5
NEG_BIG = -1e30
G1 = 4
CAP1 = 768
CAP2 = 512
NSORT1 = 8 * CAP1 // 128
NSORT2 = 8 * CAP2 // 128

ENGINES = ("sync", "vector", "scalar", "tensor", "gpsimd")


class Op:
    __slots__ = ("engine", "fn", "seq", "deps", "small", "is_dma", "stream",
                 "inc_val")

    def __init__(self, engine, fn, seq, deps, small, is_dma, stream):
        self.engine, self.fn, self.seq = engine, fn, seq
        self.deps, self.small, self.is_dma = deps, small, is_dma
        self.stream = stream
        self.inc_val = None


class Sch:

    def __init__(self, nc):
        self.nc = nc
        self.all = []
        self.last_writer = {}
        self.readers = {}
        self.stream_last = {}
        self.last_real = {e: None for e in ENGINES}

    def _reduce(self, deps):
        best = {}
        for t in deps:
            p = self.all[t]
            key = ("s", p.stream) if p.is_dma else ("e", p.engine)
            if key not in best or best[key] < t:
                best[key] = t
        return sorted(best.values())

    def _mkop(self, engine, fn, reads, writes, extra, small, is_dma, stream):
        deps = set()
        for k in reads:
            w = self.last_writer.get(k)
            if w is not None:
                deps.add(w)
        for k in writes:
            w = self.last_writer.get(k)
            if w is not None:
                deps.add(w)
            for rd in self.readers.get(k, ()):
                deps.add(rd)
        for t in extra:
            if t is not None:
                deps.add(t)
        if is_dma:
            prev = self.stream_last.get(stream)
            if prev is not None:
                deps.add(prev)
        op = Op(engine, fn, len(self.all), self._reduce(deps), small, is_dma, stream)
        self.all.append(op)
        tok = op.seq
        if is_dma:
            self.stream_last[stream] = tok
        if fn is not None:
            self.last_real[engine] = tok
        for k in writes:
            self.last_writer[k] = tok
            self.readers[k] = []
        for k in reads:
            self.readers.setdefault(k, []).append(tok)
        return tok

    def op(self, engine, fn, reads=(), writes=(), after=(), small=False):
        return self._mkop(engine, fn, reads, writes, after, small, False, None)

    def dma(self, engine, stream, out, in_, reads=(), writes=(), after=()):
        fn = lambda e, o=out, i=in_: e.dma_start(out=o, in_=i)
        return self._mkop(engine, fn, reads, writes, after, False, True, stream)

    def phase(self):
        lasts = [t for t in self.last_real.values() if t is not None]
        lasts += list(self.stream_last.values())
        for e in ENGINES:
            self._mkop(e, None, (), (), lasts, False, False, None)

    def finish(self, engine, tokens):
        self._mkop(engine, None, (), (), tokens, False, False, None)

    def emit(self):
        nc = self.nc
        ops = self.all
        def needs_sync(p, op):
            if p.is_dma:
                return False
            if p.engine == "tensor" and op.engine == "tensor":
                return False
            return True

        for op in ops:
            for d in op.deps:
                p = ops[d]
                if needs_sync(p, op) and p.inc_val is None:
                    p.inc_val = ("prog", p.engine, 0)
        prog_sem = {e: nc.alloc_semaphore(f"prog_{e}") for e in ENGINES}
        prog_count = {e: 0 for e in ENGINES}
        dma_sems, dma_count = {}, {}
        for op in ops:
            if op.is_dma:
                s = op.stream
                if s not in dma_sems:
                    dma_sems[s] = nc.alloc_semaphore(f"dm_{s}")
                    dma_count[s] = 0
                dma_count[s] += 16
                op.inc_val = ("dma", s, dma_count[s])
            elif op.inc_val is not None:
                prog_count[op.engine] += 1
                op.inc_val = ("prog", op.engine, prog_count[op.engine])

        with nc.Block() as block:
            for ename in ENGINES:
                eops = [o for o in ops if o.engine == ename]

                def make(eops=eops, ename=ename):
                    def run(eng):
                        hi = {}

                        def wait(sem, key, val):
                            if hi.get(key, 0) < val:
                                eng.wait_ge(sem, val)
                                hi[key] = val

                        for op in eops:
                            for d in op.deps:
                                p = ops[d]
                                if p.inc_val is None:
                                    continue
                                kind, key, val = p.inc_val
                                if kind == "dma":
                                    wait(dma_sems[key], ("d", key), val)
                                elif needs_sync(p, op):
                                    wait(prog_sem[key], ("p", key), val)
                            ins = op.fn(eng) if op.fn is not None else None
                            if op.inc_val is not None and ins is not None:
                                kind, key, _ = op.inc_val
                                if kind == "dma":
                                    ins.then_inc(dma_sems[key], 16)
                                else:
                                    ins.then_inc(prog_sem[key], 1)
                            elif op.inc_val is not None and not op.is_dma:
                                raise RuntimeError("inc on fn-less op")
                    return run

                getattr(block, ename)(make())
        n_sems = len(prog_sem) + len(dma_sems)
        return n_sems


def head_col_perm():
    perm = []
    for p in range(NPAIR):
        for hp in range(2):
            h = p + NPAIR * hp
            perm.extend(range(h * DH, (h + 1) * DH))
    return np.array(perm)


def build_masks(role):
    k = np.arange(128)[:, None]
    q = np.arange(128)[None, :]
    tri = np.where(k > q, NEG_BIG, 0.0).astype(np.float32)
    zero = np.zeros((128, 128), np.float32)
    fm = np.full((128, 128), NEG_BIG, np.float32)
    blk = fm if role == 0 else zero
    m0 = np.concatenate([tri, zero], axis=1)
    m1 = np.concatenate([blk, zero], axis=1)
    m2 = np.concatenate([fm, tri], axis=1)
    m3 = np.concatenate([fm, blk], axis=1)
    return np.ascontiguousarray(np.concatenate([m0, m1, m2, m3], axis=1))


def build_program(stages=("p1a", "p1b", "p2", "p3a", "p3b"), dev=False, p2_input=False):
    nc = bass.Bass("TRN2", target_bir_lowering=False, debug=False)
    scratch_kind = "ExternalOutput" if dev else "Internal"

    x_in = nc.dram_tensor("x_in", [S, D], F32, kind="ExternalInput").ap()
    wg_in = nc.dram_tensor("wg_in", [D, D], F32, kind="ExternalInput").ap()
    bg_b = nc.dram_tensor("bg_b", [128, D], F32, kind="ExternalInput").ap()
    we_in = nc.dram_tensor("we_in", [E, D, D], F32, kind="ExternalInput").ap()
    be_b = nc.dram_tensor("be_b", [128, E * D], F32, kind="ExternalInput").ap()
    wq_in = nc.dram_tensor("wq_in", [D, D], F32, kind="ExternalInput").ap()
    wk_in = nc.dram_tensor("wk_in", [D, D], F32, kind="ExternalInput").ap()
    wv_in = nc.dram_tensor("wv_in", [D, D], F32, kind="ExternalInput").ap()
    wo_in = nc.dram_tensor("wo_in", [D, D], F32, kind="ExternalInput").ap()
    maskc_in = nc.dram_tensor("maskc_in", [128, 4 * 256], F32, kind="ExternalInput").ap()
    out_d = nc.dram_tensor("out_d", [NOWN * 128, D], F32, kind="ExternalOutput").ap()

    x1_d = nc.dram_tensor("x1_d", [S, D], F32, kind=scratch_kind).ap()
    xln_d = nc.dram_tensor("xln_d", [S, D], F32, kind="Internal").ap()
    sorted_d = nc.dram_tensor("sorted_d", [NSORT1 * 128, D], F32, kind="Internal").ap()
    hsort_d = nc.dram_tensor("hsort_d", [NSORT1 * 128, D], F32, kind="Internal").ap()
    slots_d = nc.dram_tensor("slots_d", [1, S], F32, kind="Internal").ap()
    x2_d = nc.dram_tensor("x2_d", [NOWN * 128, D], F32, kind=scratch_kind).ap()
    weh_d = nc.dram_tensor("weh_d", [E, 128, 8 * D], BF16, kind="Internal").ap()
    wel_d = nc.dram_tensor("wel_d", [E, 128, 8 * D], BF16, kind="Internal").ap()
    wsp_d = {}
    for w in ("wq", "wk", "wv"):
        wsp_d[w + "h"] = nc.dram_tensor(w + "h_d", [128, 8 * D], BF16, kind="Internal").ap()
        wsp_d[w + "l"] = nc.dram_tensor(w + "l_d", [128, 8 * D], BF16, kind="Internal").ap()
    kvq_kind = "ExternalInput" if p2_input else scratch_kind
    k_d = nc.dram_tensor("k_d", [D, S], F32, kind=kvq_kind).ap()
    q_d = nc.dram_tensor("q_d", [D, NOWN * 128], F32, kind=kvq_kind).ap()
    v_d = nc.dram_tensor("v_d", [NPAIR, S, 130], F32, kind=kvq_kind).ap()
    attn_d = nc.dram_tensor("attn_d", [128, NPAIR * 2048], F32,
                            kind=("ExternalOutput" if dev else "Internal")).ap()

    ident32 = nc.inline_tensor(np.eye(128, dtype=np.float32), name="ident32").ap()
    desc_np = np.broadcast_to((D - np.arange(D)).astype(np.float32), (128, D)).copy()
    desc_c = nc.inline_tensor(desc_np, name="desc_c").ap()
    mod8_np = np.broadcast_to((np.arange(D) % 8).astype(np.float32), (128, D)).copy()
    mod8_c = nc.inline_tensor(mod8_np, name="mod8_c").ap()
    sconst_np = np.stack([np.full(128, LN_EPS, np.float32),
                          np.full(128, -EXP_SHIFT, np.float32)], axis=1).copy()
    sconst_c = nc.inline_tensor(sconst_np, name="sconst_c").ap()
    ones64_c = nc.inline_tensor(np.ones((1, 64), np.float32), name="ones64_c").ap()
    iota8_c = nc.inline_tensor(
        np.broadcast_to(np.arange(8, dtype=np.float32), (128, 8)).copy(), name="iota8_c").ap()
    ones8_c = nc.inline_tensor(np.ones((8, 1), np.float32), name="ones8_c").ap()
    base1_c = nc.inline_tensor((np.arange(8, dtype=np.float32) * CAP1).reshape(8, 1).copy(),
                               name="base1_c").ap()
    base2_c = nc.inline_tensor((np.arange(8, dtype=np.float32) * CAP2).reshape(8, 1).copy(),
                               name="base2_c").ap()

    A = nc.alloc_sbuf_tensor
    sconst = A("sconst", [128, 2], F32)
    desc_sb = A("desc_sb", [128, D], F32)
    mod8_sb = A("mod8_sb", [128, D], F32)
    id32 = A("id32", [128, 128], F32)
    maskc = A("maskc", [128, 4 * 256], F32)
    ones64 = A("ones64", [1, 64], F32)
    bg_sb = A("bg_sb", [128, D], F32)
    wgh = A("wgh", [128, 8 * D], BF16)
    wgl = A("wgl", [128, 8 * D], BF16)
    wgq = A("wgq", [128, 8 * D], BF16)
    iota8 = A("iota8", [128, 8], F32)
    ones8 = A("ones8", [8, 1], F32)
    base1 = A("base1", [8, 1], F32)
    base2 = A("base2", [8, 1], F32)
    NSLOT = 2
    msum = A("msum", [128, NSLOT], F32)
    negm = A("negm", [128, NSLOT], F32)
    vsum = A("vsum", [128, NSLOT], F32)
    lnv = A("lnv", [128, NSLOT], F32)
    rstd = A("rstd", [128, NSLOT], F32)
    mxv = A("mxv", [128, 1], F32)
    negmax = A("negmax", [128, 1], F32)
    rmax = A("rmax", [128, 1], F32)
    evec = A("evec", [128, 1], F32)
    sumexp = A("sumexp", [128, 1], F32)
    pd = A("pd", [128, 1], F32)
    phalf = A("phalf", [128, 1], F32)
    gs = [A(f"gs{i}", [128, E], F32) for i in range(G1)]

    PS = [nc.alloc_psum_tensor(f"ps{i}", [128, 1024], F32) for i in range(4)]

    sch = Sch(nc)
    eps_ap = sconst[:, 0:1]
    negC_ap = sconst[:, 1:2]

    sch.dma("sync", "cst", id32[:], ident32, writes=[("id32",)])
    sch.dma("sync", "cst", sconst[:], sconst_c, writes=[("sconst",)])
    sch.dma("sync", "cst", desc_sb[:], desc_c, writes=[("desc",)])
    sch.dma("sync", "cst", mod8_sb[:], mod8_c, writes=[("mod8",)])
    sch.dma("sync", "cst", bg_sb[:], bg_b, writes=[("bg",)])
    sch.dma("sync", "cst", maskc[:], maskc_in, writes=[("maskc",)])
    sch.dma("sync", "cst", ones64[:], ones64_c, writes=[("ones64",)])
    sch.dma("sync", "cst", iota8[:], iota8_c, writes=[("iota8",)])
    sch.dma("sync", "cst", ones8[:], ones8_c, writes=[("ones8",)])
    sch.dma("sync", "cst", base1[:], base1_c, writes=[("base1",)])
    sch.dma("sync", "cst", base2[:], base2_c, writes=[("base2",)])

    p0 = contextlib.ExitStack()
    stage32 = p0.enter_context(nc.sbuf_tensor("stage32", [128, 8 * D], F32))
    sph = p0.enter_context(nc.sbuf_tensor("sph", [128, 8 * D], BF16))
    spl = p0.enter_context(nc.sbuf_tensor("spl", [128, 8 * D], BF16))

    def split_to(dram_src_view, hi_buf, lo_buf):
        sch.dma("sync", "wld", stage32[:].rearrange("p (c d) -> p c d", c=8),
                dram_src_view, writes=[("stage32",)])
        sch.op("vector", lambda e, h=hi_buf: e.tensor_copy(out=h[:], in_=stage32[:]),
               reads=[("stage32",)], writes=[("sph",)])
        sch.op("vector", lambda e, h=hi_buf, l=lo_buf: e.tensor_sub(
               out=l[:], in0=stage32[:], in1=h[:]),
               reads=[("stage32",), ("sph",)], writes=[("spl",)])

    wg_view = wg_in.rearrange("(c p) d -> p c d", p=128)
    sch.dma("sync", "wld", stage32[:], wg_view, writes=[("stage32",)])
    sch.op("vector", lambda e: e.tensor_copy(out=wgh[:], in_=stage32[:]),
           reads=[("stage32",)], writes=[("wg", "h")])
    sch.op("vector", lambda e: e.tensor_sub(out=wgl[:], in0=stage32[:], in1=wgh[:]),
           reads=[("stage32",), ("wg", "h")], writes=[("wg", "l")])
    sch.op("vector", lambda e: e.tensor_sub(out=sph[:].bitcast(F32),
               in0=stage32[:, 0:4096], in1=wgh[:, 0:4096]),
           reads=[("stage32",), ("wg", "h")], writes=[("sph",)])
    sch.op("vector", lambda e: e.tensor_sub(out=spl[:].bitcast(F32),
               in0=stage32[:, 4096:8192], in1=wgh[:, 4096:8192]),
           reads=[("stage32",), ("wg", "h")], writes=[("spl",)])
    sch.op("vector", lambda e: e.tensor_sub(out=wgq[:, 0:4096],
               in0=sph[:].bitcast(F32), in1=wgl[:, 0:4096]),
           reads=[("sph",), ("wg", "l")], writes=[("wg", "q")])
    sch.op("vector", lambda e: e.tensor_sub(out=wgq[:, 4096:8192],
               in0=spl[:].bitcast(F32), in1=wgl[:, 4096:8192]),
           reads=[("spl",), ("wg", "l")], writes=[("wg", "q")])
    for e_i in range(E):
        split_to(we_in[e_i].rearrange("(c p) d -> p c d", p=128), sph, spl)
        sch.dma("gpsimd", "wesph", weh_d[e_i], sph[:], reads=[("sph",)],
                writes=[("weh_d", e_i)])
        sch.dma("gpsimd", "wespl", wel_d[e_i], spl[:], reads=[("spl",)],
                writes=[("wel_d", e_i)])
    for w, w_inp in (("wq", wq_in), ("wk", wk_in), ("wv", wv_in)):
        split_to(w_inp.rearrange("(c p) d -> p c d", p=128), sph, spl)
        sch.dma("gpsimd", "wesph", wsp_d[w + "h"], sph[:], reads=[("sph",)],
                writes=[("wspd", w, "h")])
        sch.dma("gpsimd", "wespl", wsp_d[w + "l"], spl[:], reads=[("spl",)],
                writes=[("wspd", w, "l")])
    zz0 = p0.enter_context(nc.sbuf_tensor("zz0", [128, D], F32))
    sch.op("vector", lambda e: e.memset(zz0[:], 0.0), writes=[("zz0",)])
    for s_i in range(NSORT1):
        sch.dma("gpsimd", "zinit", sorted_d[s_i * 128:(s_i + 1) * 128, :], zz0[:],
                reads=[("zz0",)])
    p0.close()
    sch.phase()

    ln_ctr = [0]

    def emit_ln(src_ap, src_keys, dst_ap, dst_key, sq_buf, sq_key):
        sl = ln_ctr[0] % NSLOT
        ln_ctr[0] += 1
        ms, nm = msum[:, sl:sl + 1], negm[:, sl:sl + 1]
        vv, lv, rs = vsum[:, sl:sl + 1], lnv[:, sl:sl + 1], rstd[:, sl:sl + 1]
        K = ("lnst", sl)
        sch.op("vector", lambda e: e.reduce_sum(out=ms, in_=src_ap, axis=AX),
               reads=list(src_keys), writes=[K + ("ms",)], small=True)
        sch.op("vector", lambda e: e.tensor_scalar_mul(out=nm, in0=ms, scalar1=-1.0 / D),
               reads=[K + ("ms",)], writes=[K + ("nm",)], small=True)
        sch.op("scalar", lambda e: e.activation(out=sq_buf[:], in_=src_ap,
                                                func=AF.Square, bias=nm, accum_out=vv),
               reads=list(src_keys) + [K + ("nm",), ("sconst",)],
               writes=[sq_key, K + ("vv",)], small=True)
        sch.op("scalar", lambda e: e.activation(out=lv, in_=vv, func=AF.Ln,
                                                bias=eps_ap, scale=1.0 / D),
               reads=[K + ("vv",), ("sconst",)], writes=[K + ("lv",)], small=True)
        sch.op("scalar", lambda e: e.activation(out=rs, in_=lv, func=AF.Exp, scale=-0.5),
               reads=[K + ("lv",)], writes=[K + ("rs",)], small=True)
        sch.op("vector", lambda e: e.tensor_scalar(out=dst_ap, in0=src_ap, scalar1=nm,
                                                   scalar2=rs, op0=ALU.add, op1=ALU.mult),
               reads=list(src_keys) + [K + ("nm",), K + ("rs",)], writes=[dst_key])

    def emit_tsplit(xln_ap, xln_key, ps, ps_key, hi_ap, lo_ap, hi_key, lo_key):
        for c in range(8):
            sch.op("tensor",
                   lambda e, c=c: e.transpose(out=ps[:, c * 128:(c + 1) * 128],
                                              in_=xln_ap[:, c * 128:(c + 1) * 128],
                                              identity=id32[:]),
                   reads=[xln_key, ("id32",)],
                   writes=([ps_key] if c in (0, 7) else []))
        sch.op("scalar", lambda e: e.activation(out=hi_ap, in_=ps[:], func=AF.Copy),
               reads=[ps_key], writes=[hi_key])
        sch.op("vector", lambda e: e.tensor_sub(out=lo_ap, in0=ps[:], in1=hi_ap),
               reads=[ps_key, hi_key], writes=[lo_key])

    def emit_mm3(ps, ps_key, lhs_pairs, rhs_of, read_keys, nhalves=2, hw=512):
        for n in range(nhalves):
            plan = []
            for c, (xh, xl) in enumerate(lhs_pairs):
                rh = rhs_of(c, n, "h")
                rl = rhs_of(c, n, "l")
                plan += [(xh, rh), (xh, rl), (xl, rh)]
            for i, (xs, ws) in enumerate(plan):
                st, sp = (i == 0), (i == len(plan) - 1)
                sch.op("tensor",
                       lambda e, xs=xs, ws=ws, st=st, sp=sp, n=n:
                       e.matmul(ps[:, n * hw:(n + 1) * hw], xs, ws, start=st, stop=sp),
                       reads=list(read_keys),
                       writes=([ps_key] if (st or sp) else []))

    def emit_ffn(tile_srcs, tile_dsts, phase_tag):
        fs = contextlib.ExitStack()
        weh = [fs.enter_context(nc.sbuf_tensor(f"{phase_tag}weh{i}", [128, 8 * D], BF16))
               for i in range(2)]
        wel = [fs.enter_context(nc.sbuf_tensor(f"{phase_tag}wel{i}", [128, 8 * D], BF16))
               for i in range(2)]
        xt = [fs.enter_context(nc.sbuf_tensor(f"{phase_tag}xt{i}", [128, D], F32))
              for i in range(2)]
        xln = [fs.enter_context(nc.sbuf_tensor(f"{phase_tag}xln{i}", [128, D], F32))
               for i in range(2)]
        xfm_h = fs.enter_context(nc.sbuf_tensor(f"{phase_tag}xfh", [128, G1 * D], BF16))
        xfm_l = fs.enter_context(nc.sbuf_tensor(f"{phase_tag}xfl", [128, G1 * D], BF16))
        sq_scr = fs.enter_context(nc.sbuf_tensor(f"{phase_tag}sq", [128, D], BF16))
        logits = fs.enter_context(nc.sbuf_tensor(f"{phase_tag}lg", [128, D], F32))
        descr = fs.enter_context(nc.sbuf_tensor(f"{phase_tag}dsc", [128, D], F32))
        tmp_relu = [fs.enter_context(nc.sbuf_tensor(f"{phase_tag}tr{i}", [128, D], F32))
                    for i in range(2)]
        xacc = [fs.enter_context(nc.sbuf_tensor(f"{phase_tag}xa{i}", [128, D], F32))
                for i in range(G1)]
        be2 = [fs.enter_context(nc.sbuf_tensor(f"{phase_tag}be{i}", [128, D], F32))
               for i in range(2)]
        ps_t = PS[0]
        ps_h = [PS[2], PS[3]]

        nt = len(tile_srcs)
        ngroups = (nt + G1 - 1) // G1
        for g in range(ngroups):
            idxs = list(range(g * G1, min((g + 1) * G1, nt)))
            for i, t in enumerate(idxs):
                bs = i % 2
                sch.dma("sync", f"xld{bs}", xt[bs][:], tile_srcs[t],
                        writes=[("xt", bs)])
                emit_ln(xt[bs][:], [("xt", bs)], xln[bs][:], ("xln", bs),
                        sq_scr, ("sq",))
                hi_ap = xfm_h[:, i * 1024:(i + 1) * 1024]
                lo_ap = xfm_l[:, i * 1024:(i + 1) * 1024]
                hk, lk = ("xfm", i, "h"), ("xfm", i, "l")
                emit_tsplit(xln[bs][:], ("xln", bs), ps_t, ("pst",), hi_ap, lo_ap, hk, lk)
                sch.op("vector", lambda e, i=i, bs=bs: e.tensor_copy(out=xacc[i][:],
                                                                     in_=xt[bs][:]),
                       reads=[("xt", bs)], writes=[("xacc", i)])
                lhs_pairs = [(xfm_h[:, i * 1024 + c * 128: i * 1024 + (c + 1) * 128],
                              xfm_l[:, i * 1024 + c * 128: i * 1024 + (c + 1) * 128])
                             for c in range(8)]
                emit_mm3(ps_g, ("psg",), lhs_pairs,
                         lambda c, n, which: (wgh if which == "h" else wgl)[
                             :, c * 1024 + n * 512: c * 1024 + n * 512 + 512],
                         [hk, lk, ("wg", "h"), ("wg", "l")])
                sch.op("vector", lambda e: e.tensor_add(out=logits[:], in0=ps_g[:],
                                                        in1=bg_sb[:]),
                       reads=[("psg",), ("bg",)], writes=[("lgt",)])
                sch.op("vector", lambda e: e.reduce_max(out=mxv[:], in_=logits[:],
                                                        axis=AX),
                       reads=[("lgt",)], writes=[("mxv",)], small=True)
                sch.op("vector", lambda e: e.tensor_scalar_mul(out=negmax[:], in0=mxv[:],
                                                               scalar1=-1.0),
                       reads=[("mxv",)], writes=[("negmax",)], small=True)
                sch.op("vector", lambda e: e.scalar_tensor_tensor(
                       out=descr[:], in0=logits[:], scalar=mxv[:], in1=desc_sb[:],
                       op0=ALU.is_equal, op1=ALU.mult),
                       reads=[("lgt",), ("mxv",), ("desc",)], writes=[("dscr",)])
                sch.op("vector", lambda e: e.reduce_max(out=rmax[:], in_=descr[:],
                                                        axis=AX),
                       reads=[("dscr",)], writes=[("rmax",)], small=True)
                sch.op("vector", lambda e: e.scalar_tensor_tensor(
                       out=descr[:], in0=descr[:], scalar=rmax[:], in1=mod8_sb[:],
                       op0=ALU.is_equal, op1=ALU.mult),
                       reads=[("dscr",), ("rmax",), ("mod8",)], writes=[("dscr",)])
                sch.op("vector", lambda e: e.reduce_sum(out=evec[:], in_=descr[:],
                                                        axis=AX),
                       reads=[("dscr",)], writes=[("evec",)], small=True)
                sch.op("scalar", lambda e: e.activation(out=sq_scr[:], in_=logits[:],
                                                        func=AF.Exp, bias=negmax[:],
                                                        accum_out=sumexp[:]),
                       reads=[("lgt",), ("negmax",)], writes=[("sq",), ("sume",)],
                       small=True)
                sch.op("vector", lambda e: e.reciprocal(out=pd[:], in_=sumexp[:]),
                       reads=[("sume",)], writes=[("pd",)], small=True)
                sch.op("vector", lambda e: e.tensor_scalar_mul(out=phalf[:], in0=pd[:],
                                                               scalar1=0.5),
                       reads=[("pd",)], writes=[("ph",)], small=True)
                for e_i in range(E):
                    sch.op("vector", lambda e, e_i=e_i, i=i: e.scalar_tensor_tensor(
                           out=gs[i][:, e_i:e_i + 1], in0=evec[:], scalar=float(e_i),
                           in1=phalf[:], op0=ALU.is_equal, op1=ALU.mult),
                           reads=[("evec",), ("ph",)], writes=[("gs", i, e_i)],
                           small=True)
            for e_i in range(E):
                wb = e_i % 2
                sch.dma("sync", f"weld{wb}h", weh[wb][:], weh_d[e_i],
                        reads=[("weh_d", e_i)], writes=[("weh", wb)])
                sch.dma("sync", f"weld{wb}l", wel[wb][:], wel_d[e_i],
                        reads=[("wel_d", e_i)], writes=[("wel", wb)])
                sch.dma("sync", f"beld{wb}", be2[wb][:],
                        be_b[:, e_i * D:(e_i + 1) * D], writes=[("be2", wb)])
                for i, t in enumerate(idxs):
                    hb = ps_h[i % 2]
                    hkey = ("psh", i % 2)
                    lhs_pairs = [(xfm_h[:, i * 1024 + c * 128: i * 1024 + (c + 1) * 128],
                                  xfm_l[:, i * 1024 + c * 128: i * 1024 + (c + 1) * 128])
                                 for c in range(8)]
                    emit_mm3(hb, hkey, lhs_pairs,
                             lambda c, n, which, wb=wb: (weh[wb] if which == "h"
                                                         else wel[wb])[
                                 :, c * 1024 + n * 512: c * 1024 + n * 512 + 512],
                             [("xfm", i, "h"), ("xfm", i, "l"),
                              ("weh", wb), ("wel", wb)])
                    sch.op("vector", lambda e, hb=hb, wb=wb: e.tensor_add(
                           out=hb[:], in0=hb[:], in1=be2[wb][:]),
                           reads=[hkey, ("be2", wb)], writes=[hkey])
                    tr = tmp_relu[i % 2]
                    sch.op("scalar", lambda e, hb=hb, i=i, e_i=e_i, tr=tr: e.activation(
                           out=tr[:], in_=hb[:], func=AF.Relu,
                           scale=gs[i][:, e_i:e_i + 1]),
                           reads=[hkey, ("gs", i, e_i)], writes=[("tmpr", i % 2)])
                    sch.op("vector", lambda e, i=i: e.tensor_add(
                           out=xacc[i][:], in0=xacc[i][:], in1=tmp_relu[i % 2][:]),
                           reads=[("tmpr", i % 2), ("xacc", i)], writes=[("xacc", i)])
            for i, t in enumerate(idxs):
                kind, o = tile_dsts[t]
                if kind == "x1":
                    sch.dma("gpsimd", f"x1sp{i}", x1_d[o * 128:(o + 1) * 128, :],
                            xacc[i][:], reads=[("xacc", i)])
                else:
                    sch.dma("gpsimd", f"osp{i}", out_d[o * 128:(o + 1) * 128, :],
                            xacc[i][:], reads=[("xacc", i)])
        fs.close()
        sch.phase()

    def emit_ffn_sparse(tile_srcs, tile_dsts, tag, cap, base_ap, base_key):
        ntiles = len(tile_srcs)
        ntok = ntiles * 128
        nsort = 8 * cap // 128
        tpe = cap // 128
        fs = contextlib.ExitStack()
        T = nc.sbuf_tensor
        slotcf = [fs.enter_context(T(f"{tag}slc{i}", [128, 1], F32)) for i in range(4)]
        slotu = fs.enter_context(T(f"{tag}slu", [128, ntiles], mybir.dt.uint32))
        phbuf = fs.enter_context(T(f"{tag}phb", [128, ntiles], F32))
        xfh = [fs.enter_context(T(f"{tag}xfh{i}", [128, D], BF16)) for i in range(2)]
        xfm = [fs.enter_context(T(f"{tag}xfm{i}", [128, D], BF16)) for i in range(2)]
        sx = [fs.enter_context(T(f"{tag}sx{i}", [128, D], F32)) for i in range(4)]
        fsa = contextlib.ExitStack()
        ohrows = fsa.enter_context(T(f"{tag}ohr", [8, ntok], BF16))
        rnk = fsa.enter_context(T(f"{tag}rnk", [8, ntok], F32))
        slotrow = fsa.enter_context(T(f"{tag}slr", [1, ntok], F32))
        xt = [fsa.enter_context(T(f"{tag}xt{i}", [128, D], F32)) for i in range(2)]
        xln = [fsa.enter_context(T(f"{tag}xln{i}", [128, D], F32)) for i in range(2)]
        r1b = [fsa.enter_context(T(f"{tag}r1{i}", [128, D], F32)) for i in range(2)]
        xfq = [fsa.enter_context(T(f"{tag}xfq{i}", [128, D], BF16)) for i in range(2)]
        sq_scr = fsa.enter_context(T(f"{tag}sq", [128, D], BF16))
        logits = fsa.enter_context(T(f"{tag}lg", [128, D], F32))
        descr = fsa.enter_context(T(f"{tag}dsc", [128, D], F32))
        ohb = [fsa.enter_context(T(f"{tag}oh{i}", [128, 8], F32)) for i in range(2)]

        def emit_oht(tt_):
            ob_ = tt_ % 2
            sch.op("tensor", lambda e, ob_=ob_: e.transpose(
                   out=PS[3][0:8, 0:128], in_=ohb[ob_][:], identity=id32[:]),
                   reads=[("ohb", ob_), ("id32",)], writes=[("ps3",)])
            sch.op("vector", lambda e, tt_=tt_: e.tensor_copy(
                   out=ohrows[:, tt_ * 128:(tt_ + 1) * 128], in_=PS[3][0:8, 0:128]),
                   reads=[("ps3",)], writes=[("ohr", tt_)], small=True)
        ps_t, ps_g = PS[0], PS[1]

        def prep_a(t):
            bs = t % 2
            pt_b = PS[0]
            ptkey = ("pst",)
            sch.dma("sync", f"{tag}xld{bs}", xt[bs][:], tile_srcs[t],
                    writes=[("xt", bs)])
            emit_ln(xt[bs][:], [("xt", bs)], xln[bs][:], ("xln", bs),
                    sq_scr, ("sq",))
            sch.dma("gpsimd", f"{tag}xlsp{bs}", xln_d[t * 128:(t + 1) * 128, :],
                    xln[bs][:], reads=[("xln", bs)], writes=[("xlnd", t)])
            for c in range(8):
                sch.op("tensor",
                       lambda e, c=c, bs=bs, pt_b=pt_b: e.transpose(
                           out=pt_b[:, c * 128:(c + 1) * 128],
                           in_=xln[bs][:, c * 128:(c + 1) * 128], identity=id32[:]),
                       reads=[("xln", bs), ("id32",)],
                       writes=([ptkey] if c in (0, 7) else []))
            sch.op("scalar", lambda e, bs=bs, pt_b=pt_b: e.activation(
                   out=xfh[bs][:], in_=pt_b[:], func=AF.Copy),
                   reads=[ptkey], writes=[("xfh", bs)])
            sch.op("vector", lambda e, bs=bs, pt_b=pt_b: e.tensor_sub(
                   out=r1b[bs][:], in0=pt_b[:], in1=xfh[bs][:]),
                   reads=[ptkey, ("xfh", bs)], writes=[("r1", bs)])
            sch.op("scalar", lambda e, bs=bs: e.activation(out=xfm[bs][:], in_=r1b[bs][:],
                                                           func=AF.Copy),
                   reads=[("r1", bs)], writes=[("xfm", bs)])
            sch.op("vector", lambda e, bs=bs: e.tensor_sub(out=xfq[bs][:], in0=r1b[bs][:],
                                                           in1=xfm[bs][:]),
                   reads=[("r1", bs), ("xfm", bs)], writes=[("xfq", bs)])

        prep_a(0)
        for t in range(ntiles):
            bs = t % 2
            if t + 1 < ntiles:
                prep_a(t + 1)
            ps_g = PS[1] if bs == 0 else PS[2]
            gkey = ("psg", bs)
            for n in range(2):
                plan = []
                for c in range(8):
                    xh = xfh[bs][:, c * 128:(c + 1) * 128]
                    xm = xfm[bs][:, c * 128:(c + 1) * 128]
                    xq = xfq[bs][:, c * 128:(c + 1) * 128]
                    wh = wgh[:, c * 1024 + n * 512: c * 1024 + n * 512 + 512]
                    wl = wgl[:, c * 1024 + n * 512: c * 1024 + n * 512 + 512]
                    wq = wgq[:, c * 1024 + n * 512: c * 1024 + n * 512 + 512]
                    plan += [(xh, wh), (xh, wl), (xm, wh), (xm, wl), (xh, wq), (xq, wh)]
                for i, (xs, ws) in enumerate(plan):
                    st, sp = (i == 0), (i == len(plan) - 1)
                    sch.op("tensor",
                           lambda e, xs=xs, ws=ws, st=st, sp=sp, n=n, ps_g=ps_g:
                           e.matmul(ps_g[:, n * 512:(n + 1) * 512], xs, ws,
                                    start=st, stop=sp),
                           reads=[("xfh", bs), ("xfm", bs), ("xfq", bs),
                                  ("wg", "h"), ("wg", "l"), ("wg", "q")],
                           writes=([gkey] if (st or sp) else []))
            sch.op("vector", lambda e, ps_g=ps_g: e.tensor_add(out=logits[:],
                                                               in0=ps_g[:],
                                                               in1=bg_sb[:]),
                   reads=[gkey, ("bg",)], writes=[("lgt",)])
            sch.op("vector", lambda e: e.reduce_max(out=mxv[:], in_=logits[:], axis=AX),
                   reads=[("lgt",)], writes=[("mxv",)], small=True)
            sch.op("vector", lambda e: e.tensor_scalar_mul(out=negmax[:], in0=mxv[:],
                                                           scalar1=-1.0),
                   reads=[("mxv",)], writes=[("negmax",)], small=True)
            sch.op("vector", lambda e: e.scalar_tensor_tensor(
                   out=descr[:], in0=logits[:], scalar=mxv[:], in1=desc_sb[:],
                   op0=ALU.is_equal, op1=ALU.mult),
                   reads=[("lgt",), ("mxv",), ("desc",)], writes=[("dscr",)])
            sch.op("vector", lambda e: e.reduce_max(out=rmax[:], in_=descr[:], axis=AX),
                   reads=[("dscr",)], writes=[("rmax",)], small=True)
            sch.op("vector", lambda e: e.scalar_tensor_tensor(
                   out=descr[:], in0=descr[:], scalar=rmax[:], in1=mod8_sb[:],
                   op0=ALU.is_equal, op1=ALU.mult),
                   reads=[("dscr",), ("rmax",), ("mod8",)], writes=[("dscr",)])
            sch.op("vector", lambda e: e.reduce_sum(out=evec[:], in_=descr[:], axis=AX),
                   reads=[("dscr",)], writes=[("evec",)], small=True)
            sch.op("scalar", lambda e: e.activation(out=sq_scr[:], in_=logits[:],
                                                    func=AF.Exp, bias=negmax[:],
                                                    accum_out=sumexp[:]),
                   reads=[("lgt",), ("negmax",)], writes=[("sq",), ("sume",)],
                   small=True)
            sch.op("vector", lambda e: e.reciprocal(out=pd[:], in_=sumexp[:]),
                   reads=[("sume",)], writes=[("pd",)], small=True)
            sch.op("vector", lambda e, t=t: e.tensor_scalar_mul(
                   out=phbuf[:, t:t + 1], in0=pd[:], scalar1=0.5),
                   reads=[("pd",)], writes=[("phb", t)], small=True)
            ob = t % 2
            sch.op("vector", lambda e, ob=ob: e.tensor_single_scalar(
                   out=ohb[ob][:], in_=iota8[:], scalar=evec[:], op=ALU.is_equal),
                   reads=[("evec",), ("iota8",)], writes=[("ohb", ob)], small=True)
            if t > 0:
                emit_oht(t - 1)
            if t == ntiles - 1:
                emit_oht(t)

        ohr_keys = [("ohr", t) for t in range(ntiles)]
        sch.op("vector", lambda e: e.tensor_tensor_scan(
               out=rnk[:, 0:ntok], data0=ohrows[:, 0:ntok], data1=ohrows[:, 0:ntok],
               initial=0.0, op0=ALU.add, op1=ALU.bypass),
               reads=ohr_keys, writes=[("rnk",)])
        sch.op("vector", lambda e: e.tensor_sub(out=rnk[:, 0:ntok], in0=rnk[:, 0:ntok],
                                                in1=ohrows[:, 0:ntok]),
               reads=[("rnk",)] + ohr_keys, writes=[("rnk",)], small=True)
        sch.op("vector", lambda e: e.tensor_scalar_min(out=rnk[:, 0:ntok],
                                                       in0=rnk[:, 0:ntok],
                                                       scalar1=float(cap - 1)),
               reads=[("rnk",)], writes=[("rnk",)], small=True)
        sch.op("vector", lambda e: e.tensor_scalar_add(out=rnk[:, 0:ntok],
                                                       in0=rnk[:, 0:ntok],
                                                       scalar1=base_ap),
               reads=[("rnk",), base_key], writes=[("rnk",)], small=True)
        sch.op("vector", lambda e: e.tensor_mul(out=rnk[:, 0:ntok], in0=rnk[:, 0:ntok],
                                                in1=ohrows[:, 0:ntok]),
               reads=[("rnk",)] + ohr_keys, writes=[("rnk",)], small=True)
        for c in range(ntok // 512):
            sch.op("tensor", lambda e, c=c: e.matmul(
                   PS[3][0:1, 0:512], ones8[:], rnk[:, c * 512:(c + 1) * 512],
                   start=True, stop=True),
                   reads=[("rnk",), ("ones8",)], writes=[("ps3",)])
            sch.op("vector", lambda e, c=c: e.tensor_copy(
                   out=slotrow[:, c * 512:(c + 1) * 512], in_=PS[3][0:1, 0:512]),
                   reads=[("ps3",)], writes=[("slr",)], small=True)
        sch.dma("gpsimd", f"{tag}slsp", slots_d[:, 0:ntok], slotrow[:, 0:ntok],
                reads=[("slr",)], writes=[("slots_d",)])
        for t in range(ntiles):
            bs = t % 4
            sch.dma("sync", f"{tag}slld{bs}", slotcf[bs][:],
                    slots_d[:, t * 128:(t + 1) * 128].rearrange("o p -> p o"),
                    reads=[("slots_d",)], writes=[("slcf", bs)])
            sch.op("vector", lambda e, t=t, bs=bs: e.tensor_copy(
                   out=slotu[:, t:t + 1], in_=slotcf[bs][:]),
                   reads=[("slcf", bs)], writes=[("slu", t)], small=True)
        for t in range(ntiles):
            bs = t % 4
            sch.dma("sync", f"{tag}xrl{bs}", sx[bs][:], xln_d[t * 128:(t + 1) * 128, :],
                    reads=[("xlnd", t)], writes=[("sx", bs)])
            tok = sch._mkop("gpsimd",
                            lambda e, bs=bs, t=t: e.indirect_dma_start(
                                out=sorted_d,
                                out_offset=IndirectOffsetOnAxis(ap=slotu[:, t:t + 1],
                                                                axis=0),
                                in_=sx[bs][:], in_offset=None),
                            [("sx", bs), ("slu", t)], [("sorted", t)], (), False,
                            True, f"{tag}sct{bs}")
        fsa.close()
        sch.phase()
        fsb = contextlib.ExitStack()
        weh = [fsb.enter_context(T(f"{tag}weh{i}", [128, 8 * D], BF16)) for i in range(2)]
        wel = [fsb.enter_context(T(f"{tag}wel{i}", [128, 8 * D], BF16)) for i in range(2)]
        be2 = [fsb.enter_context(T(f"{tag}be{i}", [128, D], F32)) for i in range(2)]
        hout = [fsb.enter_context(T(f"{tag}ho{i}", [128, D], F32)) for i in range(2)]
        sorted_keys = [("sorted", t) for t in range(ntiles)]
        def prep_b(s_i):
            e_i = s_i // tpe
            wb = e_i % 2
            sbs = s_i % 2
            pt_b = PS[0] if sbs == 0 else PS[1]
            ptkey = ("pst", sbs)
            if s_i % tpe == 0:
                sch.dma("sync", f"weld{wb}h", weh[wb][:], weh_d[e_i],
                        reads=[("weh_d", e_i)], writes=[("weh", wb)])
                sch.dma("sync", f"weld{wb}l", wel[wb][:], wel_d[e_i],
                        reads=[("wel_d", e_i)], writes=[("wel", wb)])
                sch.dma("sync", f"beld{wb}", be2[wb][:],
                        be_b[:, e_i * D:(e_i + 1) * D], writes=[("be2", wb)])
            sch.dma("sync", f"{tag}sxl{sbs}", sx[sbs][:],
                    sorted_d[s_i * 128:(s_i + 1) * 128, :],
                    reads=sorted_keys, writes=[("sx", sbs)])
            for c in range(8):
                sch.op("tensor",
                       lambda e, c=c, sbs=sbs, pt_b=pt_b: e.transpose(
                           out=pt_b[:, c * 128:(c + 1) * 128],
                           in_=sx[sbs][:, c * 128:(c + 1) * 128], identity=id32[:]),
                       reads=[("sx", sbs), ("id32",)],
                       writes=([ptkey] if c in (0, 7) else []))
            sch.op("scalar", lambda e, sbs=sbs, pt_b=pt_b: e.activation(
                   out=xfh[sbs][:], in_=pt_b[:], func=AF.Copy),
                   reads=[ptkey], writes=[("xfh", sbs)])
            sch.op("vector", lambda e, sbs=sbs, pt_b=pt_b: e.tensor_sub(
                   out=xfm[sbs][:], in0=pt_b[:], in1=xfh[sbs][:]),
                   reads=[ptkey, ("xfh", sbs)], writes=[("xfm", sbs)])

        prep_b(0)
        for s_i in range(nsort):
            e_i = s_i // tpe
            wb = e_i % 2
            sbs = s_i % 2
            if s_i + 1 < nsort:
                prep_b(s_i + 1)
            hb = PS[2] if sbs == 0 else PS[3]
            hkey = ("psh0",) if sbs == 0 else ("ps3",)
            for n in range(2):
                plan = []
                for c in range(8):
                    xh = xfh[sbs][:, c * 128:(c + 1) * 128]
                    xm = xfm[sbs][:, c * 128:(c + 1) * 128]
                    wh = weh[wb][:, c * 1024 + n * 512: c * 1024 + n * 512 + 512]
                    wl = wel[wb][:, c * 1024 + n * 512: c * 1024 + n * 512 + 512]
                    plan += [(xh, wh), (xh, wl), (xm, wh)]
                for i, (xs, ws) in enumerate(plan):
                    st, sp = (i == 0), (i == len(plan) - 1)
                    sch.op("tensor",
                           lambda e, xs=xs, ws=ws, st=st, sp=sp, n=n, hb=hb:
                           e.matmul(hb[:, n * 512:(n + 1) * 512], xs, ws,
                                    start=st, stop=sp),
                           reads=[("xfh", sbs), ("xfm", sbs), ("weh", wb),
                                  ("wel", wb)],
                           writes=([hkey] if (st or sp) else []))
            sch.op("vector", lambda e, hb=hb, wb=wb: e.tensor_add(
                   out=hb[:], in0=hb[:], in1=be2[wb][:]),
                   reads=[hkey, ("be2", wb)], writes=[hkey])
            sch.op("scalar", lambda e, hb=hb, sbs=sbs: e.activation(
                   out=hout[sbs][:], in_=hb[:], func=AF.Relu),
                   reads=[hkey], writes=[("ho", sbs)])
            sch.dma("gpsimd", f"{tag}hsp{sbs}", hsort_d[s_i * 128:(s_i + 1) * 128, :],
                    hout[sbs][:], reads=[("ho", sbs)], writes=[("hsd", s_i)])
        fsb.close()
        sch.phase()
        fsc = contextlib.ExitStack()
        ghat = [fsc.enter_context(T(f"{tag}gh{i}", [128, D], F32)) for i in range(2)]
        xres = [fsc.enter_context(T(f"{tag}xr{i}", [128, D], F32)) for i in range(2)]
        xacc = [fsc.enter_context(T(f"{tag}xa{i}", [128, D], F32)) for i in range(2)]
        hsd_keys = [("hsd", s_i) for s_i in range(nsort)]
        for t in range(ntiles):
            bs = t % 2
            tok = sch._mkop("gpsimd",
                            lambda e, bs=bs, t=t: e.indirect_dma_start(
                                out=ghat[bs][:], out_offset=None,
                                in_=hsort_d,
                                in_offset=IndirectOffsetOnAxis(ap=slotu[:, t:t + 1],
                                                               axis=0)),
                            hsd_keys + [("slu", t)], [("gh", bs)], (), False,
                            True, f"{tag}gth{bs}")
            sch.dma("sync", f"{tag}xrs{bs}", xres[bs][:], tile_srcs[t],
                    writes=[("xres", bs)])
            sch.op("vector", lambda e, bs=bs, t=t: e.scalar_tensor_tensor(
                   out=xacc[bs][:], in0=ghat[bs][:], scalar=phbuf[:, t:t + 1],
                   in1=xres[bs][:], op0=ALU.mult, op1=ALU.add),
                   reads=[("gh", bs), ("phb", t), ("xres", bs)],
                   writes=[("xacc", bs)])
            kind, o = tile_dsts[t]
            if kind == "x1":
                sch.dma("gpsimd", f"{tag}x1o{bs}", x1_d[o * 128:(o + 1) * 128, :],
                        xacc[bs][:], reads=[("xacc", bs)])
            else:
                sch.dma("gpsimd", f"{tag}oo{bs}", out_d[o * 128:(o + 1) * 128, :],
                        xacc[bs][:], reads=[("xacc", bs)])
        fsc.close()
        fs.close()
        sch.phase()

    if "p1a" in stages:
        emit_ffn_sparse([x_in[t * 128:(t + 1) * 128, :] for t in range(NT)],
                        [("x1", t) for t in range(NT)], "fa", CAP1, base1[:],
                        ("base1",))

    if "p1b" in stages:
        fs = contextlib.ExitStack()
        wk_h = fs.enter_context(nc.sbuf_tensor("wk_h", [128, 8 * D], BF16))
        wk_l = fs.enter_context(nc.sbuf_tensor("wk_l", [128, 8 * D], BF16))
        wv_h = fs.enter_context(nc.sbuf_tensor("wv_h", [128, 8 * D], BF16))
        wv_l = fs.enter_context(nc.sbuf_tensor("wv_l", [128, 8 * D], BF16))
        wq_h = fs.enter_context(nc.sbuf_tensor("wq_h", [128, 8 * D], BF16))
        wq_l = fs.enter_context(nc.sbuf_tensor("wq_l", [128, 8 * D], BF16))
        xt = [fs.enter_context(nc.sbuf_tensor(f"bxt{i}", [128, D], F32)) for i in range(2)]
        xln = [fs.enter_context(nc.sbuf_tensor(f"bxln{i}", [128, D], F32)) for i in range(2)]
        xfm_h = fs.enter_context(nc.sbuf_tensor("bxfh", [128, 4 * D], BF16))
        xfm_l = fs.enter_context(nc.sbuf_tensor("bxfl", [128, 4 * D], BF16))
        sq_scr = fs.enter_context(nc.sbuf_tensor("bsq", [128, D], BF16))
        vbuf = fs.enter_context(nc.sbuf_tensor("vbuf", [128, 16 * 65], F32))
        ksb = [fs.enter_context(nc.sbuf_tensor(f"ksb{i}", [128, 512], F32)) for i in range(2)]
        qsb = [fs.enter_context(nc.sbuf_tensor(f"qsb{i}", [128, 256], F32)) for i in range(2)]
        for w, hb, lb in (("wk", wk_h, wk_l), ("wv", wv_h, wv_l), ("wq", wq_h, wq_l)):
            sch.dma("sync", "pbw", hb[:], wsp_d[w + "h"], reads=[("wspd", w, "h")],
                    writes=[("pw", w, "h")])
            sch.dma("sync", "pbw", lb[:], wsp_d[w + "l"], reads=[("wspd", w, "l")],
                    writes=[("pw", w, "l")])
        vones = vbuf[:].rearrange("p (j c) -> p j c", c=65)[:, :, 64:65]
        sch.op("vector", lambda e: e.memset(vones, 1.0), writes=[("vones",)])
        ps_t, ps_v = PS[0], PS[1]
        ps_k, ps_q = [PS[2][:, 0:512], PS[2][:, 512:1024]], PS[3]

        for g in range(NT // 4):
            xf_keys = []
            for i in range(4):
                t = 4 * g + i
                bs = i % 2
                ptb = ps_t
                ptk = ("bpst",)
                sch.dma("sync", f"bxld{bs}", xt[bs][:], x1_d[t * 128:(t + 1) * 128, :],
                        writes=[("bxt", bs)])
                emit_ln(xt[bs][:], [("bxt", bs)], xln[bs][:], ("bxln", bs),
                        sq_scr, ("bsq",))
                hi_ap = xfm_h[:, i * 1024:(i + 1) * 1024]
                lo_ap = xfm_l[:, i * 1024:(i + 1) * 1024]
                hk, lk = ("bxfm", i, "h"), ("bxfm", i, "l")
                emit_tsplit(xln[bs][:], ("bxln", bs), ptb, ptk, hi_ap, lo_ap,
                            hk, lk)
                xf_keys.append([hk, lk])
            for i in range(4):
                t = 4 * g + i
                hk, lk = xf_keys[i]
                lhs_pairs = [(xfm_h[:, i * 1024 + c * 128: i * 1024 + (c + 1) * 128],
                              xfm_l[:, i * 1024 + c * 128: i * 1024 + (c + 1) * 128])
                             for c in range(8)]
                emit_mm3(ps_v, ("psv",), lhs_pairs,
                         lambda c, n, which: (wv_h if which == "h" else wv_l)[
                             :, c * 1024 + n * 512: c * 1024 + n * 512 + 512],
                         [hk, lk, ("pw", "wv", "h"), ("pw", "wv", "l")])
                vdst = vbuf[:].rearrange("p (j c) -> p j c", c=65)[:, :, 0:64]
                vsrc = ps_v[:].rearrange("p (j c) -> p j c", c=64)
                sch.op("vector", lambda e, vdst=vdst, vsrc=vsrc: e.tensor_copy(
                       out=vdst, in_=vsrc),
                       reads=[("psv",), ("vones",)], writes=[("vbuf",)])
                for p_i in range(NPAIR):
                    sch.dma("gpsimd", f"vsp{p_i}",
                            v_d[p_i, t * 128:(t + 1) * 128, :],
                            vbuf[:, p_i * 130:(p_i + 1) * 130],
                            reads=[("vbuf",)])
            xv_h = xfm_h[:].rearrange("p (i c k) -> p c i k", i=4, c=8)
            xv_l = xfm_l[:].rearrange("p (i c k) -> p c i k", i=4, c=8)
            for c8 in range(8):
                kb = c8 % 2
                lhs_pairs = [(wk_h[:, c * 1024 + c8 * 128: c * 1024 + c8 * 128 + 128],
                              wk_l[:, c * 1024 + c8 * 128: c * 1024 + c8 * 128 + 128])
                             for c in range(8)]
                emit_mm3(ps_k[kb], ("psk", kb), lhs_pairs,
                         lambda c, n, which: (xv_h if which == "h" else xv_l)[:, c],
                         [("pw", "wk", "h"), ("pw", "wk", "l")] +
                         [k for ks in xf_keys for k in ks], nhalves=1)
                sch.op("scalar", lambda e, kb=kb: e.activation(out=ksb[kb][:],
                                                               in_=ps_k[kb],
                                                               func=AF.Copy),
                       reads=[("psk", kb)], writes=[("ksb", kb)])
                sch.dma("gpsimd", f"ksp{kb}",
                        k_d[c8 * 128:(c8 + 1) * 128, g * 512:(g + 1) * 512],
                        ksb[kb][:], reads=[("ksb", kb)])
            for c8 in range(8):
                qb = c8 % 2
                lhs_pairs = [(wq_h[:, c * 1024 + c8 * 128: c * 1024 + c8 * 128 + 128],
                              wq_l[:, c * 1024 + c8 * 128: c * 1024 + c8 * 128 + 128])
                             for c in range(8)]
                emit_mm3(ps_q[:, 0:256], ("psq",), lhs_pairs,
                         lambda c, n, which: (xv_h if which == "h" else xv_l)[:, c, 0::2, :],
                         [("pw", "wq", "h"), ("pw", "wq", "l"),
                          xf_keys[0][0], xf_keys[0][1], xf_keys[2][0], xf_keys[2][1]],
                         nhalves=1, hw=256)
                sch.op("scalar", lambda e, qb=qb: e.activation(out=qsb[qb][:],
                                                               in_=ps_q[:, 0:256],
                                                               func=AF.Copy),
                       reads=[("psq",)], writes=[("qsb", qb)])
                sch.dma("gpsimd", f"qsp{qb}",
                        q_d[c8 * 128:(c8 + 1) * 128, g * 256:(g + 1) * 256],
                        qsb[qb][:], reads=[("qsb", qb)])
        fs.close()
        sch.phase()


    if "p2" in stages:
        fs = contextlib.ExitStack()
        fs2 = contextlib.ExitStack()
        kpair2 = [fs2.enter_context(nc.sbuf_tensor(f"kpair{i}", [128, S], F32))
                  for i in range(2)]
        _qp = fs2.enter_context(nc.sbuf_tensor("qpair0", [128, NOWN * 128], F32))
        qpair2 = [_qp, _qp]
        vpair2 = [fs2.enter_context(nc.sbuf_tensor("vpair0", [128, 32 * 130], F32))] * 2
        kh2 = [fs2.enter_context(nc.sbuf_tensor(f"kh{i}", [128, S], BF16))
               for i in range(2)]
        kl2 = [fs2.enter_context(nc.sbuf_tensor(f"kl{i}", [128, S], BF16))
               for i in range(2)]
        qh2 = [fs2.enter_context(nc.sbuf_tensor(f"qh{i}", [128, NOWN * 128], BF16))
               for i in range(2)]
        ql2 = [fs2.enter_context(nc.sbuf_tensor(f"ql{i}", [128, NOWN * 128], BF16))
               for i in range(2)]
        att_st = [fs2.enter_context(nc.sbuf_tensor(f"attst{i}", [128, 256], F32))
                  for i in range(2)]
        vh_sb = fs2.enter_context(nc.sbuf_tensor("vh_sb", [128, 32 * 130], BF16))
        vl_sb = fs2.enter_context(nc.sbuf_tensor("vl_sb", [128, 32 * 130], BF16))
        pth = [[fs2.enter_context(nc.sbuf_tensor(f"pth{h}_{b}", [128, 512], BF16))
                for b in range(2)] for h in range(2)]
        ptl = [[fs2.enter_context(nc.sbuf_tensor(f"ptl{h}_{b}", [128, 512], BF16))
                for b in range(2)] for h in range(2)]
        _pt32 = fs2.enter_context(nc.sbuf_tensor("pt32_0", [128, 512], F32))
        pt32 = [_pt32, _pt32]

        rsum = [fs2.enter_context(nc.sbuf_tensor(f"rsum{i}", [1, 256], F32))
                for i in range(2)]
        recs = [fs2.enter_context(nc.sbuf_tensor(f"recs{i}", [1, 256], F32))
                for i in range(2)]
        tmpo = fs2.enter_context(nc.sbuf_tensor("tmpo", [64, 256], F32))
        def pair_load(p_i):
            pb = p_i % 2
            sch.dma("sync", f"kpld{pb}", kpair2[pb][:],
                    k_d[p_i * 128:(p_i + 1) * 128, :], writes=[("kpair", pb)])
            sch.dma("sync", "qpld", qpair2[pb][:],
                    q_d[p_i * 128:(p_i + 1) * 128, :],
                    writes=[("qpair", 0), ("qpair", 1)])

        pair_load(0)
        final_norm = [None]
        for p_i in range(NPAIR):
            pb = p_i % 2
            kpair, qpair, vpair = kpair2[pb], qpair2[pb], vpair2[pb]
            kh, kl, qh, ql = kh2[pb], kl2[pb], qh2[pb], ql2[pb]
            sch.op("vector", lambda e, kh=kh, kpair=kpair: e.tensor_copy(
                   out=kh[:], in_=kpair[:]),
                   reads=[("kpair", pb)], writes=[("kh", pb)])
            sch.op("vector", lambda e, kl=kl, kh=kh, kpair=kpair: e.tensor_sub(
                   out=kl[:], in0=kpair[:], in1=kh[:]),
                   reads=[("kpair", pb), ("kh", pb)], writes=[("kl", pb)])
            sch.op("vector", lambda e, qh=qh, qpair=qpair: e.tensor_copy(
                   out=qh[:], in_=qpair[:]),
                   reads=[("qpair", pb)], writes=[("qh", pb)])
            sch.op("vector", lambda e, ql=ql, qh=qh, qpair=qpair: e.tensor_sub(
                   out=ql[:], in0=qpair[:], in1=qh[:]),
                   reads=[("qpair", pb), ("qh", pb)], writes=[("ql", pb)])
            sch.dma("sync", "vpld",
                    vpair[:].rearrange("pt (m c) -> pt m c", c=130),
                    v_d[p_i].rearrange("(m pt) c -> pt m c", pt=128),
                    writes=[("vpair", 0), ("vpair", 1)])
            sch.op("vector", lambda e, vpair=vpair: e.tensor_copy(out=vh_sb[:],
                                                                  in_=vpair[:]),
                   reads=[("vpair", 0)], writes=[("vh",)])
            sch.op("vector", lambda e, vpair=vpair: e.tensor_sub(out=vl_sb[:],
                                                                 in0=vpair[:],
                                                                 in1=vh_sb[:]),
                   reads=[("vpair", 0), ("vh",)], writes=[("vl",)])
            if p_i + 1 < NPAIR:
                pair_load(p_i + 1)
            if final_norm[0] is not None:
                final_norm[0]()
            norm_pending = []

            def emit_norm():
                for (hp_, ab_, pw_, pi_, t_rs_) in norm_pending:
                    pokey_ = ("pso", hp_)
                    sl_ = hp_
                    sch.op("tensor", lambda e, sl_=sl_: e.matmul(
                           PS[3][0:64, 0:256], ones64[:], recs[sl_][:],
                           start=True, stop=True),
                           reads=[("recs", sl_), ("ones64",)], writes=[("psbc",)])
                    sch.op("scalar", lambda e, hp=hp_: e.activation(
                           out=tmpo[:], in_=PS[2][0:64, hp * 512:hp * 512 + 256],
                           func=AF.Copy),
                           reads=[pokey_], writes=[("tmpo",)], after=[t_rs_])
                    dst = att_st[ab_][hp_ * 64:(hp_ + 1) * 64, :]
                    sch.op("vector", lambda e, dst=dst: e.tensor_mul(
                           out=dst, in0=tmpo[:], in1=PS[3][0:64, 0:256]),
                           reads=[("tmpo",), ("psbc",)],
                           writes=[("attst", ab_, hp_)])
                if norm_pending:
                    _, ab_, pw_, pi_, _t = norm_pending[0]
                    sch.dma("gpsimd", f"atsp{ab_}",
                            attn_d[:, pi_ * 2048 + pw_ * 256:
                                   pi_ * 2048 + (pw_ + 1) * 256],
                            att_st[ab_][:],
                            reads=[("attst", ab_, 0), ("attst", ab_, 1)],
                            writes=[("afm", pi_, pw_)])
                norm_pending.clear()

            final_norm[0] = emit_norm
            for w in range(8):
                n_m = 4 * w + 4
                for hp in range(2):
                    tp = (0, 0) if hp == 0 else (64, 0)
                    pbase = hp * 64
                    po = PS[2][0:65, hp * 512:hp * 512 + 256]
                    pokey = ("pso", hp)

                    def sco(mp, hp=hp, w=w, tp=tp, pbase=pbase, pb=pb, kh=kh,
                            kl=kl, qh=qh, ql=ql):
                        buf = mp % 2
                        skey = ("pss", hp, buf)
                        for m in (2 * mp, 2 * mp + 1):
                            sps = PS[hp][:, buf * 512 + (m % 2) * 256:
                                         buf * 512 + (m % 2) * 256 + 256]
                            plan3 = [(kh, qh), (kh, ql), (kl, qh)]
                            for pi3, (ka, qa) in enumerate(plan3):
                                sch.op("tensor", lambda e, sps=sps, m=m, w=w, tp=tp,
                                       pbase=pbase, ka=ka, qa=qa, pi3=pi3:
                                       e.matmul(sps,
                                                ka[pbase:pbase + 64,
                                                   m * 128:(m + 1) * 128],
                                                qa[pbase:pbase + 64,
                                                   w * 256:(w + 1) * 256],
                                                start=(pi3 == 0), stop=(pi3 == 2),
                                                tile_position=tp),
                                       reads=[("kh", pb), ("kl", pb), ("qh", pb),
                                              ("ql", pb)],
                                       writes=([skey] if pi3 in (0, 2) else []))
                        for m in (2 * mp, 2 * mp + 1):
                            km = m - (4 * w)
                            if 0 <= km <= 3:
                                sps = PS[hp][:, buf * 512 + (m % 2) * 256:
                                             buf * 512 + (m % 2) * 256 + 256]
                                sch.op("vector", lambda e, sps=sps, km=km: e.tensor_add(
                                       out=sps, in0=sps,
                                       in1=maskc[:, km * 256:(km + 1) * 256]),
                                       reads=[skey, ("maskc",)], writes=[skey])
                        sch.op("scalar", lambda e, hp=hp, buf=buf:
                               e.activation(out=pth[hp][buf][:],
                                            in_=PS[hp][:, buf * 512: buf * 512 + 512],
                                            func=AF.Exp, scale=0.125, bias=negC_ap),
                               reads=[skey, ("sconst",)], writes=[("pth", hp, buf)])
                        sch.op("scalar", lambda e, hp=hp, buf=buf:
                               e.activation(out=pt32[buf][:],
                                            in_=PS[hp][:, buf * 512: buf * 512 + 512],
                                            func=AF.Exp, scale=0.125, bias=negC_ap),
                               reads=[skey, ("sconst",)], writes=[("pt32", 0),
                                                                  ("pt32", 1)])
                        sch.op("vector", lambda e, hp=hp, buf=buf: e.tensor_sub(
                               out=ptl[hp][buf][:], in0=pt32[buf][:],
                               in1=pth[hp][buf][:]),
                               reads=[("pt32", buf), ("pth", hp, buf)],
                               writes=[("ptl", hp, buf)])

                    sco(0)
                    if hp == 0:
                        emit_norm()
                    for mp in range(n_m // 2):
                        buf = mp % 2
                        if mp + 1 < n_m // 2:
                            sco(mp + 1)
                        for m in (2 * mp, 2 * mp + 1):
                            vsh = vh_sb[:, m * 130 + hp * 65: m * 130 + hp * 65 + 65]
                            vsl = vl_sb[:, m * 130 + hp * 65: m * 130 + hp * 65 + 65]
                            mh = pth[hp][buf][:, (m % 2) * 256:(m % 2) * 256 + 256]
                            ml = ptl[hp][buf][:, (m % 2) * 256:(m % 2) * 256 + 256]
                            plan_pv = [(vsh, mh), (vsh, ml), (vsl, mh)]
                            for pj, (va, pa) in enumerate(plan_pv):
                                st = (m == 0 and pj == 0)
                                sp = (m == n_m - 1 and pj == 2)
                                sch.op("tensor", lambda e, po=po, va=va, pa=pa, st=st,
                                       sp=sp:
                                       e.matmul(po, va, pa, start=st, stop=sp),
                                       reads=[("vh",), ("vl",), ("pth", hp, buf),
                                              ("ptl", hp, buf)] +
                                             ([pokey] if st else []),
                                       writes=([pokey] if st or sp else []))
                    ab = w % 2
                    sl = hp
                    t_rs = sch.op("vector", lambda e, hp=hp, sl=sl: e.tensor_copy(
                           out=rsum[sl][:],
                           in_=PS[2][64:65, hp * 512:hp * 512 + 256]),
                           reads=[pokey], writes=[("rsum", sl)], small=True)
                    sch.op("vector", lambda e, sl=sl: e.reciprocal(
                           out=recs[sl][:], in_=rsum[sl][:]),
                           reads=[("rsum", sl)], writes=[("recs", sl)], small=True)
                    norm_pending.append((hp, ab, w, p_i, t_rs))

        if final_norm[0] is not None:
            final_norm[0]()
        fs2.close()
        sch.phase()
        if "p3a" in stages:
            wo_sb = fs.enter_context(nc.sbuf_tensor("wo_sb", [128, 8 * D], F32))
            x1b = [fs.enter_context(nc.sbuf_tensor(f"x1b{i}", [128, D], F32))
                   for i in range(2)]
            x2b = [fs.enter_context(nc.sbuf_tensor(f"x2b{i}", [128, D], F32))
                   for i in range(2)]
            afm_t = [fs.enter_context(nc.sbuf_tensor(f"afmt{i}", [128, D], F32))
                     for i in range(2)]
            sch.dma("sync", "wold", wo_sb[:].rearrange("p (c d) -> p c d", c=8),
                    wo_in.rearrange("(c p) d -> p c d", p=128), writes=[("wo",)])
            afm_all = [("afm", c_, w_) for c_ in range(NPAIR) for w_ in range(8)]
            for j in range(NOWN):
                bs = j % 2
                l = 2 * j
                sch.dma("sync", f"x1ld{bs}", x1b[bs][:],
                        x1_d[l * 128:(l + 1) * 128, :], writes=[("x1b", bs)])
                sch.dma("sync", f"afld{bs}",
                        afm_t[bs][:].rearrange("p (c q) -> p c q", c=8),
                        attn_d.rearrange("p (c q) -> p c q", c=8)[
                            :, :, j * 128:(j + 1) * 128],
                        reads=[("afm", c_, j // 2) for c_ in range(NPAIR)],
                        writes=[("afmt", bs)])
                psw = PS[0] if bs == 0 else PS[1]
                wkey = ("psw", bs)
                for n in range(2):
                    for c in range(8):
                        st, sp = (c == 0), (c == 7)
                        lhs = afm_t[bs][:, c * 128:(c + 1) * 128]
                        rhs = wo_sb[:, c * 1024 + n * 512: c * 1024 + n * 512 + 512]
                        sch.op("tensor", lambda e, psw=psw, lhs=lhs, rhs=rhs, st=st,
                               sp=sp, n=n:
                               e.matmul(psw[:, n * 512:(n + 1) * 512], lhs, rhs,
                                        start=st, stop=sp),
                               reads=[("wo",), ("afmt", bs)],
                               writes=([wkey] if (st or sp) else []))
                sch.op("vector", lambda e, bs=bs: e.tensor_add(
                       out=x2b[bs][:], in0=x1b[bs][:], in1=(PS[0] if bs == 0
                                                            else PS[1])[:]),
                       reads=[wkey, ("x1b", bs)], writes=[("x2b", bs)])
                sch.dma("gpsimd", f"x2sp{bs}", x2_d[j * 128:(j + 1) * 128, :],
                        x2b[bs][:], reads=[("x2b", bs)])
        fs.close()
        sch.phase()

    if "p3b" in stages:
        emit_ffn_sparse([x2_d[j * 128:(j + 1) * 128, :] for j in range(NOWN)],
                        [("out", j) for j in range(NOWN)], "fb", CAP2, base2[:],
                        ("base2",))

    finals = [o.seq for o in sch.all if o.is_dma and o.stream and
              (o.stream.startswith("osp") or o.stream.startswith("x2sp") or
               o.stream.startswith("x1sp") or o.stream.startswith("adump"))]
    sch.finish("sync", finals)
    sch.emit()
    return nc


def stage_inputs(inputs):
    x = np.asarray(inputs["L0_attn_x"], dtype=np.float32)
    wq = np.asarray(inputs["L0_attn_W_Q"], np.float32)
    wk = np.asarray(inputs["L0_attn_W_K"], np.float32)
    wv = np.asarray(inputs["L0_attn_W_V"], np.float32)
    wo = np.asarray(inputs["L0_attn_W_o"], np.float32)
    wg = np.asarray(inputs["L0_ffn_Wg"], np.float32)
    bgv = np.asarray(inputs["L0_ffn_bg"], np.float32)
    we = np.asarray(inputs["L0_ffn_We"], np.float32)
    bev = np.asarray(inputs["L0_ffn_be"], np.float32)

    perm = head_col_perm()
    wq_p = np.ascontiguousarray(wq[:, perm])
    wk_p = np.ascontiguousarray(wk[:, perm])
    wv_p = np.ascontiguousarray(wv[:, perm])
    wo_p = np.ascontiguousarray(wo[perm, :])
    bg_bc = np.ascontiguousarray(np.broadcast_to(bgv[None, :], (128, D)))
    be_bc = np.ascontiguousarray(np.broadcast_to(bev.reshape(1, E * D), (128, E * D)))

    in_maps = []
    for c in range(8):
        b, r = c // 2, c % 2
        xb = x[b].reshape(NT, 128, D)
        if r == 0:
            xl = xb
        else:
            order = [l + 1 - 2 * (l % 2) for l in range(NT)]
            xl = xb[order]
        in_maps.append({
            "x_in": np.ascontiguousarray(xl.reshape(S, D)),
            "wg_in": wg, "bg_b": bg_bc, "we_in": we, "be_b": be_bc,
            "wq_in": wq_p, "wk_in": wk_p, "wv_in": wv_p, "wo_in": wo_p,
            "maskc_in": build_masks(r),
        })
    return in_maps


def gather_output(results):
    out = np.empty((B, S, D), np.float32)
    for c in range(8):
        b, r = c // 2, c % 2
        sh = np.asarray(results[c]["out_d"]).reshape(NOWN, 128, D)
        for j in range(NOWN):
            g = 2 * j + r
            out[b, g * 128:(g + 1) * 128, :] = sh[j]
    return out


def kernel(**inputs):
    nc = build_program()
    in_maps = stage_inputs(inputs)
    res = run_bass_kernel_spmd(nc, in_maps, core_ids=list(range(8)))
    return gather_output(res.results)


# version 10
# speedup vs baseline: 1.0679x; 1.0679x over previous
import contextlib
import numpy as np
import ml_dtypes
import concourse.bass as bass
import concourse.mybir as mybir
from concourse.bass_utils import run_bass_kernel_spmd
from concourse.bass_core_rust import IndirectOffsetOnAxis

F32 = mybir.dt.float32
BF16 = mybir.dt.bfloat16
AX = mybir.AxisListType.X
AF = mybir.ActivationFunctionType
ALU = mybir.AluOpType

B, S, D, H, DH, E = 4, 4096, 1024, 16, 64, 8
NT = S // 128
NOWN = NT // 2
NPAIR = H // 2
EXP_SHIFT = 20.0
LN_EPS = 1e-5
NEG_BIG = -1e30
G1 = 4
CAP1 = 768
CAP2 = 512
NSORT1 = 8 * CAP1 // 128
NSORT2 = 8 * CAP2 // 128

ENGINES = ("sync", "vector", "scalar", "tensor", "gpsimd")


class Op:
    __slots__ = ("engine", "fn", "seq", "deps", "small", "is_dma", "stream",
                 "inc_val")

    def __init__(self, engine, fn, seq, deps, small, is_dma, stream):
        self.engine, self.fn, self.seq = engine, fn, seq
        self.deps, self.small, self.is_dma = deps, small, is_dma
        self.stream = stream
        self.inc_val = None


class Sch:

    def __init__(self, nc):
        self.nc = nc
        self.all = []
        self.last_writer = {}
        self.readers = {}
        self.stream_last = {}
        self.last_real = {e: None for e in ENGINES}

    def _reduce(self, deps):
        best = {}
        for t in deps:
            p = self.all[t]
            key = ("s", p.stream) if p.is_dma else ("e", p.engine)
            if key not in best or best[key] < t:
                best[key] = t
        return sorted(best.values())

    def _mkop(self, engine, fn, reads, writes, extra, small, is_dma, stream):
        deps = set()
        for k in reads:
            w = self.last_writer.get(k)
            if w is not None:
                deps.add(w)
        for k in writes:
            w = self.last_writer.get(k)
            if w is not None:
                deps.add(w)
            for rd in self.readers.get(k, ()):
                deps.add(rd)
        for t in extra:
            if t is not None:
                deps.add(t)
        if is_dma:
            prev = self.stream_last.get(stream)
            if prev is not None:
                deps.add(prev)
        op = Op(engine, fn, len(self.all), self._reduce(deps), small, is_dma, stream)
        self.all.append(op)
        tok = op.seq
        if is_dma:
            self.stream_last[stream] = tok
        if fn is not None:
            self.last_real[engine] = tok
        for k in writes:
            self.last_writer[k] = tok
            self.readers[k] = []
        for k in reads:
            self.readers.setdefault(k, []).append(tok)
        return tok

    def op(self, engine, fn, reads=(), writes=(), after=(), small=False):
        return self._mkop(engine, fn, reads, writes, after, small, False, None)

    def dma(self, engine, stream, out, in_, reads=(), writes=(), after=()):
        fn = lambda e, o=out, i=in_: e.dma_start(out=o, in_=i)
        return self._mkop(engine, fn, reads, writes, after, False, True, stream)

    def phase(self):
        lasts = [t for t in self.last_real.values() if t is not None]
        lasts += list(self.stream_last.values())
        for e in ENGINES:
            self._mkop(e, None, (), (), lasts, False, False, None)

    def finish(self, engine, tokens):
        self._mkop(engine, None, (), (), tokens, False, False, None)

    def emit(self):
        nc = self.nc
        ops = self.all
        def needs_sync(p, op):
            if p.is_dma:
                return False
            if p.engine == "tensor" and op.engine == "tensor":
                return False
            return True

        for op in ops:
            for d in op.deps:
                p = ops[d]
                if needs_sync(p, op) and p.inc_val is None:
                    p.inc_val = ("prog", p.engine, 0)
        prog_sem = {e: nc.alloc_semaphore(f"prog_{e}") for e in ENGINES}
        prog_count = {e: 0 for e in ENGINES}
        dma_sems, dma_count = {}, {}
        for op in ops:
            if op.is_dma:
                s = op.stream
                if s not in dma_sems:
                    dma_sems[s] = nc.alloc_semaphore(f"dm_{s}")
                    dma_count[s] = 0
                dma_count[s] += 16
                op.inc_val = ("dma", s, dma_count[s])
            elif op.inc_val is not None:
                prog_count[op.engine] += 1
                op.inc_val = ("prog", op.engine, prog_count[op.engine])

        with nc.Block() as block:
            for ename in ENGINES:
                eops = [o for o in ops if o.engine == ename]

                def make(eops=eops, ename=ename):
                    def run(eng):
                        hi = {}

                        def wait(sem, key, val):
                            if hi.get(key, 0) < val:
                                eng.wait_ge(sem, val)
                                hi[key] = val

                        for op in eops:
                            for d in op.deps:
                                p = ops[d]
                                if p.inc_val is None:
                                    continue
                                kind, key, val = p.inc_val
                                if kind == "dma":
                                    wait(dma_sems[key], ("d", key), val)
                                elif needs_sync(p, op):
                                    wait(prog_sem[key], ("p", key), val)
                            ins = op.fn(eng) if op.fn is not None else None
                            if op.inc_val is not None and ins is not None:
                                kind, key, _ = op.inc_val
                                if kind == "dma":
                                    ins.then_inc(dma_sems[key], 16)
                                else:
                                    ins.then_inc(prog_sem[key], 1)
                            elif op.inc_val is not None and not op.is_dma:
                                raise RuntimeError("inc on fn-less op")
                    return run

                getattr(block, ename)(make())
        n_sems = len(prog_sem) + len(dma_sems)
        return n_sems


def head_col_perm():
    perm = []
    for p in range(NPAIR):
        for hp in range(2):
            h = p + NPAIR * hp
            perm.extend(range(h * DH, (h + 1) * DH))
    return np.array(perm)


def build_masks(role):
    k = np.arange(128)[:, None]
    q = np.arange(128)[None, :]
    tri = np.where(k > q, NEG_BIG, 0.0).astype(np.float32)
    zero = np.zeros((128, 128), np.float32)
    fm = np.full((128, 128), NEG_BIG, np.float32)
    blk = fm if role == 0 else zero
    m0 = np.concatenate([tri, zero], axis=1)
    m1 = np.concatenate([blk, zero], axis=1)
    m2 = np.concatenate([fm, tri], axis=1)
    m3 = np.concatenate([fm, blk], axis=1)
    return np.ascontiguousarray(np.concatenate([m0, m1, m2, m3], axis=1))


def build_program(stages=("p1a", "p1b", "p2", "p3a", "p3b"), dev=False, p2_input=False):
    nc = bass.Bass("TRN2", target_bir_lowering=False, debug=False)
    scratch_kind = "ExternalOutput" if dev else "Internal"

    x_in = nc.dram_tensor("x_in", [S, D], F32, kind="ExternalInput").ap()
    wg_in = nc.dram_tensor("wg_in", [D, D], F32, kind="ExternalInput").ap()
    bg_b = nc.dram_tensor("bg_b", [128, D], F32, kind="ExternalInput").ap()
    we_in = nc.dram_tensor("we_in", [E, D, D], F32, kind="ExternalInput").ap()
    be_b = nc.dram_tensor("be_b", [128, E * D], F32, kind="ExternalInput").ap()
    wq_in = nc.dram_tensor("wq_in", [D, D], F32, kind="ExternalInput").ap()
    wk_in = nc.dram_tensor("wk_in", [D, D], F32, kind="ExternalInput").ap()
    wv_in = nc.dram_tensor("wv_in", [D, D], F32, kind="ExternalInput").ap()
    wo_in = nc.dram_tensor("wo_in", [D, D], F32, kind="ExternalInput").ap()
    maskc_in = nc.dram_tensor("maskc_in", [128, 4 * 256], F32, kind="ExternalInput").ap()
    out_d = nc.dram_tensor("out_d", [NOWN * 128, D], F32, kind="ExternalOutput").ap()

    x1_d = nc.dram_tensor("x1_d", [S, D], F32, kind=scratch_kind).ap()
    xln_d = nc.dram_tensor("xln_d", [S, D], F32, kind="Internal").ap()
    sorted_d = nc.dram_tensor("sorted_d", [NSORT1 * 128, D], F32, kind="Internal").ap()
    hsort_d = nc.dram_tensor("hsort_d", [NSORT1 * 128, D], F32, kind="Internal").ap()
    slots_d = nc.dram_tensor("slots_d", [1, S], F32, kind="Internal").ap()
    x2_d = nc.dram_tensor("x2_d", [NOWN * 128, D], F32, kind=scratch_kind).ap()
    weh_d = nc.dram_tensor("weh_d", [E, 128, 8 * D], BF16, kind="Internal").ap()
    wel_d = nc.dram_tensor("wel_d", [E, 128, 8 * D], BF16, kind="Internal").ap()
    wsp_d = {}
    for w in ("wq", "wk", "wv"):
        wsp_d[w + "h"] = nc.dram_tensor(w + "h_d", [128, 8 * D], BF16, kind="Internal").ap()
        wsp_d[w + "l"] = nc.dram_tensor(w + "l_d", [128, 8 * D], BF16, kind="Internal").ap()
    kvq_kind = "ExternalInput" if p2_input else scratch_kind
    k_d = nc.dram_tensor("k_d", [D, S], F32, kind=kvq_kind).ap()
    q_d = nc.dram_tensor("q_d", [D, NOWN * 128], F32, kind=kvq_kind).ap()
    v_d = nc.dram_tensor("v_d", [NPAIR, S, 130], F32, kind=kvq_kind).ap()
    attn_d = nc.dram_tensor("attn_d", [128, NPAIR * 2048], F32,
                            kind=("ExternalOutput" if dev else "Internal")).ap()

    ident32 = nc.inline_tensor(np.eye(128, dtype=np.float32), name="ident32").ap()
    desc_np = np.broadcast_to((D - np.arange(D)).astype(np.float32), (128, D)).copy()
    desc_c = nc.inline_tensor(desc_np, name="desc_c").ap()
    mod8_np = np.broadcast_to((np.arange(D) % 8).astype(np.float32), (128, D)).copy()
    mod8_c = nc.inline_tensor(mod8_np, name="mod8_c").ap()
    sconst_np = np.stack([np.full(128, LN_EPS, np.float32),
                          np.full(128, -EXP_SHIFT, np.float32)], axis=1).copy()
    sconst_c = nc.inline_tensor(sconst_np, name="sconst_c").ap()
    ones64_c = nc.inline_tensor(np.ones((1, 64), np.float32), name="ones64_c").ap()
    iota8_c = nc.inline_tensor(
        np.broadcast_to(np.arange(8, dtype=np.float32), (128, 8)).copy(), name="iota8_c").ap()
    ones8_c = nc.inline_tensor(np.ones((8, 1), np.float32), name="ones8_c").ap()
    base1_c = nc.inline_tensor((np.arange(8, dtype=np.float32) * CAP1).reshape(8, 1).copy(),
                               name="base1_c").ap()
    base2_c = nc.inline_tensor((np.arange(8, dtype=np.float32) * CAP2).reshape(8, 1).copy(),
                               name="base2_c").ap()

    A = nc.alloc_sbuf_tensor
    sconst = A("sconst", [128, 2], F32)
    desc_sb = A("desc_sb", [128, D], F32)
    mod8_sb = A("mod8_sb", [128, D], F32)
    id32 = A("id32", [128, 128], F32)
    maskc = A("maskc", [128, 4 * 256], F32)
    ones64 = A("ones64", [1, 64], F32)
    bg_sb = A("bg_sb", [128, D], F32)
    wgh = A("wgh", [128, 8 * D], BF16)
    wgl = A("wgl", [128, 8 * D], BF16)
    wgq = A("wgq", [128, 8 * D], BF16)
    iota8 = A("iota8", [128, 8], F32)
    ones8 = A("ones8", [8, 1], F32)
    base1 = A("base1", [8, 1], F32)
    base2 = A("base2", [8, 1], F32)
    NSLOT = 2
    msum = A("msum", [128, NSLOT], F32)
    negm = A("negm", [128, NSLOT], F32)
    vsum = A("vsum", [128, NSLOT], F32)
    lnv = A("lnv", [128, NSLOT], F32)
    rstd = A("rstd", [128, NSLOT], F32)
    mxv = A("mxv", [128, 1], F32)
    negmax = A("negmax", [128, 1], F32)
    rmax = A("rmax", [128, 1], F32)
    evec = A("evec", [128, 1], F32)
    sumexp = A("sumexp", [128, 1], F32)
    pd = A("pd", [128, 1], F32)
    phalf = A("phalf", [128, 1], F32)
    gs = [A(f"gs{i}", [128, E], F32) for i in range(G1)]

    PS = [nc.alloc_psum_tensor(f"ps{i}", [128, 1024], F32) for i in range(4)]

    sch = Sch(nc)
    eps_ap = sconst[:, 0:1]
    negC_ap = sconst[:, 1:2]

    sch.dma("sync", "cst", id32[:], ident32, writes=[("id32",)])
    sch.dma("sync", "cst", sconst[:], sconst_c, writes=[("sconst",)])
    sch.dma("sync", "cst", desc_sb[:], desc_c, writes=[("desc",)])
    sch.dma("sync", "cst", mod8_sb[:], mod8_c, writes=[("mod8",)])
    sch.dma("sync", "cst", bg_sb[:], bg_b, writes=[("bg",)])
    sch.dma("sync", "cst", maskc[:], maskc_in, writes=[("maskc",)])
    sch.dma("sync", "cst", ones64[:], ones64_c, writes=[("ones64",)])
    sch.dma("sync", "cst", iota8[:], iota8_c, writes=[("iota8",)])
    sch.dma("sync", "cst", ones8[:], ones8_c, writes=[("ones8",)])
    sch.dma("sync", "cst", base1[:], base1_c, writes=[("base1",)])
    sch.dma("sync", "cst", base2[:], base2_c, writes=[("base2",)])

    p0 = contextlib.ExitStack()
    stage32 = p0.enter_context(nc.sbuf_tensor("stage32", [128, 8 * D], F32))
    sph = p0.enter_context(nc.sbuf_tensor("sph", [128, 8 * D], BF16))
    spl = p0.enter_context(nc.sbuf_tensor("spl", [128, 8 * D], BF16))

    def split_to(dram_src_view, hi_buf, lo_buf):
        sch.dma("sync", "wld", stage32[:].rearrange("p (c d) -> p c d", c=8),
                dram_src_view, writes=[("stage32",)])
        sch.op("vector", lambda e, h=hi_buf: e.tensor_copy(out=h[:], in_=stage32[:]),
               reads=[("stage32",)], writes=[("sph",)])
        sch.op("vector", lambda e, h=hi_buf, l=lo_buf: e.tensor_sub(
               out=l[:], in0=stage32[:], in1=h[:]),
               reads=[("stage32",), ("sph",)], writes=[("spl",)])

    wg_view = wg_in.rearrange("(c p) d -> p c d", p=128)
    sch.dma("sync", "wld", stage32[:], wg_view, writes=[("stage32",)])
    sch.op("vector", lambda e: e.tensor_copy(out=wgh[:], in_=stage32[:]),
           reads=[("stage32",)], writes=[("wg", "h")])
    sch.op("vector", lambda e: e.tensor_sub(out=wgl[:], in0=stage32[:], in1=wgh[:]),
           reads=[("stage32",), ("wg", "h")], writes=[("wg", "l")])
    sch.op("vector", lambda e: e.tensor_sub(out=sph[:].bitcast(F32),
               in0=stage32[:, 0:4096], in1=wgh[:, 0:4096]),
           reads=[("stage32",), ("wg", "h")], writes=[("sph",)])
    sch.op("vector", lambda e: e.tensor_sub(out=spl[:].bitcast(F32),
               in0=stage32[:, 4096:8192], in1=wgh[:, 4096:8192]),
           reads=[("stage32",), ("wg", "h")], writes=[("spl",)])
    sch.op("vector", lambda e: e.tensor_sub(out=wgq[:, 0:4096],
               in0=sph[:].bitcast(F32), in1=wgl[:, 0:4096]),
           reads=[("sph",), ("wg", "l")], writes=[("wg", "q")])
    sch.op("vector", lambda e: e.tensor_sub(out=wgq[:, 4096:8192],
               in0=spl[:].bitcast(F32), in1=wgl[:, 4096:8192]),
           reads=[("spl",), ("wg", "l")], writes=[("wg", "q")])
    for e_i in range(E):
        split_to(we_in[e_i].rearrange("(c p) d -> p c d", p=128), sph, spl)
        sch.dma("gpsimd", "wesph", weh_d[e_i], sph[:], reads=[("sph",)],
                writes=[("weh_d", e_i)])
        sch.dma("gpsimd", "wespl", wel_d[e_i], spl[:], reads=[("spl",)],
                writes=[("wel_d", e_i)])
    for w, w_inp in (("wq", wq_in), ("wk", wk_in), ("wv", wv_in)):
        split_to(w_inp.rearrange("(c p) d -> p c d", p=128), sph, spl)
        sch.dma("gpsimd", "wesph", wsp_d[w + "h"], sph[:], reads=[("sph",)],
                writes=[("wspd", w, "h")])
        sch.dma("gpsimd", "wespl", wsp_d[w + "l"], spl[:], reads=[("spl",)],
                writes=[("wspd", w, "l")])
    zz0 = p0.enter_context(nc.sbuf_tensor("zz0", [128, D], F32))
    sch.op("vector", lambda e: e.memset(zz0[:], 0.0), writes=[("zz0",)])
    for s_i in range(NSORT1):
        sch.dma("gpsimd", "zinit", sorted_d[s_i * 128:(s_i + 1) * 128, :], zz0[:],
                reads=[("zz0",)])
    p0.close()
    sch.phase()

    ln_ctr = [0]

    def emit_ln(src_ap, src_keys, dst_ap, dst_key, sq_buf, sq_key):
        sl = ln_ctr[0] % NSLOT
        ln_ctr[0] += 1
        ms, nm = msum[:, sl:sl + 1], negm[:, sl:sl + 1]
        vv, lv, rs = vsum[:, sl:sl + 1], lnv[:, sl:sl + 1], rstd[:, sl:sl + 1]
        K = ("lnst", sl)
        sch.op("vector", lambda e: e.reduce_sum(out=ms, in_=src_ap, axis=AX),
               reads=list(src_keys), writes=[K + ("ms",)], small=True)
        sch.op("vector", lambda e: e.tensor_scalar_mul(out=nm, in0=ms, scalar1=-1.0 / D),
               reads=[K + ("ms",)], writes=[K + ("nm",)], small=True)
        sch.op("scalar", lambda e: e.activation(out=sq_buf[:], in_=src_ap,
                                                func=AF.Square, bias=nm, accum_out=vv),
               reads=list(src_keys) + [K + ("nm",), ("sconst",)],
               writes=[sq_key, K + ("vv",)], small=True)
        sch.op("scalar", lambda e: e.activation(out=lv, in_=vv, func=AF.Ln,
                                                bias=eps_ap, scale=1.0 / D),
               reads=[K + ("vv",), ("sconst",)], writes=[K + ("lv",)], small=True)
        sch.op("scalar", lambda e: e.activation(out=rs, in_=lv, func=AF.Exp, scale=-0.5),
               reads=[K + ("lv",)], writes=[K + ("rs",)], small=True)
        sch.op("vector", lambda e: e.tensor_scalar(out=dst_ap, in0=src_ap, scalar1=nm,
                                                   scalar2=rs, op0=ALU.add, op1=ALU.mult),
               reads=list(src_keys) + [K + ("nm",), K + ("rs",)], writes=[dst_key])

    def emit_tsplit(xln_ap, xln_key, ps, ps_key, hi_ap, lo_ap, hi_key, lo_key):
        for c in range(8):
            sch.op("tensor",
                   lambda e, c=c: e.transpose(out=ps[:, c * 128:(c + 1) * 128],
                                              in_=xln_ap[:, c * 128:(c + 1) * 128],
                                              identity=id32[:]),
                   reads=[xln_key, ("id32",)],
                   writes=([ps_key] if c in (0, 7) else []))
        sch.op("scalar", lambda e: e.activation(out=hi_ap, in_=ps[:], func=AF.Copy),
               reads=[ps_key], writes=[hi_key])
        sch.op("vector", lambda e: e.tensor_sub(out=lo_ap, in0=ps[:], in1=hi_ap),
               reads=[ps_key, hi_key], writes=[lo_key])

    def emit_mm3(ps, ps_key, lhs_pairs, rhs_of, read_keys, nhalves=2, hw=512):
        for n in range(nhalves):
            plan = []
            for c, (xh, xl) in enumerate(lhs_pairs):
                rh = rhs_of(c, n, "h")
                rl = rhs_of(c, n, "l")
                plan += [(xh, rh), (xh, rl), (xl, rh)]
            for i, (xs, ws) in enumerate(plan):
                st, sp = (i == 0), (i == len(plan) - 1)
                sch.op("tensor",
                       lambda e, xs=xs, ws=ws, st=st, sp=sp, n=n:
                       e.matmul(ps[:, n * hw:(n + 1) * hw], xs, ws, start=st, stop=sp),
                       reads=list(read_keys),
                       writes=([ps_key] if (st or sp) else []))

    def emit_ffn(tile_srcs, tile_dsts, phase_tag):
        fs = contextlib.ExitStack()
        weh = [fs.enter_context(nc.sbuf_tensor(f"{phase_tag}weh{i}", [128, 8 * D], BF16))
               for i in range(2)]
        wel = [fs.enter_context(nc.sbuf_tensor(f"{phase_tag}wel{i}", [128, 8 * D], BF16))
               for i in range(2)]
        xt = [fs.enter_context(nc.sbuf_tensor(f"{phase_tag}xt{i}", [128, D], F32))
              for i in range(2)]
        xln = [fs.enter_context(nc.sbuf_tensor(f"{phase_tag}xln{i}", [128, D], F32))
               for i in range(2)]
        xfm_h = fs.enter_context(nc.sbuf_tensor(f"{phase_tag}xfh", [128, G1 * D], BF16))
        xfm_l = fs.enter_context(nc.sbuf_tensor(f"{phase_tag}xfl", [128, G1 * D], BF16))
        sq_scr = fs.enter_context(nc.sbuf_tensor(f"{phase_tag}sq", [128, D], BF16))
        logits = fs.enter_context(nc.sbuf_tensor(f"{phase_tag}lg", [128, D], F32))
        descr = fs.enter_context(nc.sbuf_tensor(f"{phase_tag}dsc", [128, D], F32))
        tmp_relu = [fs.enter_context(nc.sbuf_tensor(f"{phase_tag}tr{i}", [128, D], F32))
                    for i in range(2)]
        xacc = [fs.enter_context(nc.sbuf_tensor(f"{phase_tag}xa{i}", [128, D], F32))
                for i in range(G1)]
        be2 = [fs.enter_context(nc.sbuf_tensor(f"{phase_tag}be{i}", [128, D], F32))
               for i in range(2)]
        ps_t = PS[0]
        ps_h = [PS[2], PS[3]]

        nt = len(tile_srcs)
        ngroups = (nt + G1 - 1) // G1
        for g in range(ngroups):
            idxs = list(range(g * G1, min((g + 1) * G1, nt)))
            for i, t in enumerate(idxs):
                bs = i % 2
                sch.dma("sync", f"xld{bs}", xt[bs][:], tile_srcs[t],
                        writes=[("xt", bs)])
                emit_ln(xt[bs][:], [("xt", bs)], xln[bs][:], ("xln", bs),
                        sq_scr, ("sq",))
                hi_ap = xfm_h[:, i * 1024:(i + 1) * 1024]
                lo_ap = xfm_l[:, i * 1024:(i + 1) * 1024]
                hk, lk = ("xfm", i, "h"), ("xfm", i, "l")
                emit_tsplit(xln[bs][:], ("xln", bs), ps_t, ("pst",), hi_ap, lo_ap, hk, lk)
                sch.op("vector", lambda e, i=i, bs=bs: e.tensor_copy(out=xacc[i][:],
                                                                     in_=xt[bs][:]),
                       reads=[("xt", bs)], writes=[("xacc", i)])
                lhs_pairs = [(xfm_h[:, i * 1024 + c * 128: i * 1024 + (c + 1) * 128],
                              xfm_l[:, i * 1024 + c * 128: i * 1024 + (c + 1) * 128])
                             for c in range(8)]
                emit_mm3(ps_g, ("psg",), lhs_pairs,
                         lambda c, n, which: (wgh if which == "h" else wgl)[
                             :, c * 1024 + n * 512: c * 1024 + n * 512 + 512],
                         [hk, lk, ("wg", "h"), ("wg", "l")])
                sch.op("vector", lambda e: e.tensor_add(out=logits[:], in0=ps_g[:],
                                                        in1=bg_sb[:]),
                       reads=[("psg",), ("bg",)], writes=[("lgt",)])
                sch.op("vector", lambda e: e.reduce_max(out=mxv[:], in_=logits[:],
                                                        axis=AX),
                       reads=[("lgt",)], writes=[("mxv",)], small=True)
                sch.op("vector", lambda e: e.tensor_scalar_mul(out=negmax[:], in0=mxv[:],
                                                               scalar1=-1.0),
                       reads=[("mxv",)], writes=[("negmax",)], small=True)
                sch.op("vector", lambda e: e.scalar_tensor_tensor(
                       out=descr[:], in0=logits[:], scalar=mxv[:], in1=desc_sb[:],
                       op0=ALU.is_equal, op1=ALU.mult),
                       reads=[("lgt",), ("mxv",), ("desc",)], writes=[("dscr",)])
                sch.op("vector", lambda e: e.reduce_max(out=rmax[:], in_=descr[:],
                                                        axis=AX),
                       reads=[("dscr",)], writes=[("rmax",)], small=True)
                sch.op("vector", lambda e: e.scalar_tensor_tensor(
                       out=descr[:], in0=descr[:], scalar=rmax[:], in1=mod8_sb[:],
                       op0=ALU.is_equal, op1=ALU.mult),
                       reads=[("dscr",), ("rmax",), ("mod8",)], writes=[("dscr",)])
                sch.op("vector", lambda e: e.reduce_sum(out=evec[:], in_=descr[:],
                                                        axis=AX),
                       reads=[("dscr",)], writes=[("evec",)], small=True)
                sch.op("scalar", lambda e: e.activation(out=sq_scr[:], in_=logits[:],
                                                        func=AF.Exp, bias=negmax[:],
                                                        accum_out=sumexp[:]),
                       reads=[("lgt",), ("negmax",)], writes=[("sq",), ("sume",)],
                       small=True)
                sch.op("vector", lambda e: e.reciprocal(out=pd[:], in_=sumexp[:]),
                       reads=[("sume",)], writes=[("pd",)], small=True)
                sch.op("vector", lambda e: e.tensor_scalar_mul(out=phalf[:], in0=pd[:],
                                                               scalar1=0.5),
                       reads=[("pd",)], writes=[("ph",)], small=True)
                for e_i in range(E):
                    sch.op("vector", lambda e, e_i=e_i, i=i: e.scalar_tensor_tensor(
                           out=gs[i][:, e_i:e_i + 1], in0=evec[:], scalar=float(e_i),
                           in1=phalf[:], op0=ALU.is_equal, op1=ALU.mult),
                           reads=[("evec",), ("ph",)], writes=[("gs", i, e_i)],
                           small=True)
            for e_i in range(E):
                wb = e_i % 2
                sch.dma("sync", f"weld{wb}h", weh[wb][:], weh_d[e_i],
                        reads=[("weh_d", e_i)], writes=[("weh", wb)])
                sch.dma("sync", f"weld{wb}l", wel[wb][:], wel_d[e_i],
                        reads=[("wel_d", e_i)], writes=[("wel", wb)])
                sch.dma("sync", f"beld{wb}", be2[wb][:],
                        be_b[:, e_i * D:(e_i + 1) * D], writes=[("be2", wb)])
                for i, t in enumerate(idxs):
                    hb = ps_h[i % 2]
                    hkey = ("psh", i % 2)
                    lhs_pairs = [(xfm_h[:, i * 1024 + c * 128: i * 1024 + (c + 1) * 128],
                                  xfm_l[:, i * 1024 + c * 128: i * 1024 + (c + 1) * 128])
                                 for c in range(8)]
                    emit_mm3(hb, hkey, lhs_pairs,
                             lambda c, n, which, wb=wb: (weh[wb] if which == "h"
                                                         else wel[wb])[
                                 :, c * 1024 + n * 512: c * 1024 + n * 512 + 512],
                             [("xfm", i, "h"), ("xfm", i, "l"),
                              ("weh", wb), ("wel", wb)])
                    sch.op("vector", lambda e, hb=hb, wb=wb: e.tensor_add(
                           out=hb[:], in0=hb[:], in1=be2[wb][:]),
                           reads=[hkey, ("be2", wb)], writes=[hkey])
                    tr = tmp_relu[i % 2]
                    sch.op("scalar", lambda e, hb=hb, i=i, e_i=e_i, tr=tr: e.activation(
                           out=tr[:], in_=hb[:], func=AF.Relu,
                           scale=gs[i][:, e_i:e_i + 1]),
                           reads=[hkey, ("gs", i, e_i)], writes=[("tmpr", i % 2)])
                    sch.op("vector", lambda e, i=i: e.tensor_add(
                           out=xacc[i][:], in0=xacc[i][:], in1=tmp_relu[i % 2][:]),
                           reads=[("tmpr", i % 2), ("xacc", i)], writes=[("xacc", i)])
            for i, t in enumerate(idxs):
                kind, o = tile_dsts[t]
                if kind == "x1":
                    sch.dma("gpsimd", f"x1sp{i}", x1_d[o * 128:(o + 1) * 128, :],
                            xacc[i][:], reads=[("xacc", i)])
                else:
                    sch.dma("gpsimd", f"osp{i}", out_d[o * 128:(o + 1) * 128, :],
                            xacc[i][:], reads=[("xacc", i)])
        fs.close()
        sch.phase()

    def emit_ffn_sparse(tile_srcs, tile_dsts, tag, cap, base_ap, base_key):
        ntiles = len(tile_srcs)
        ntok = ntiles * 128
        nsort = 8 * cap // 128
        tpe = cap // 128
        fs = contextlib.ExitStack()
        T = nc.sbuf_tensor
        slotcf = [fs.enter_context(T(f"{tag}slc{i}", [128, 1], F32)) for i in range(4)]
        slotu = fs.enter_context(T(f"{tag}slu", [128, ntiles], mybir.dt.uint32))
        phbuf = fs.enter_context(T(f"{tag}phb", [128, ntiles], F32))
        xfh = [fs.enter_context(T(f"{tag}xfh{i}", [128, D], BF16)) for i in range(2)]
        xfm = [fs.enter_context(T(f"{tag}xfm{i}", [128, D], BF16)) for i in range(2)]
        sx = [fs.enter_context(T(f"{tag}sx{i}", [128, D], F32)) for i in range(4)]
        fsa = contextlib.ExitStack()
        ohrows = fsa.enter_context(T(f"{tag}ohr", [8, ntok], BF16))
        rnk = fsa.enter_context(T(f"{tag}rnk", [8, ntok], F32))
        rnk2 = fsa.enter_context(T(f"{tag}rnk2", [8, ntok], F32))
        slotrow = fsa.enter_context(T(f"{tag}slr", [1, ntok], F32))
        xt = [fsa.enter_context(T(f"{tag}xt{i}", [128, D], F32)) for i in range(2)]
        xln = [fsa.enter_context(T(f"{tag}xln{i}", [128, D], F32)) for i in range(2)]
        r1b = [fsa.enter_context(T(f"{tag}r1{i}", [128, D], F32)) for i in range(2)]
        xfq = [fsa.enter_context(T(f"{tag}xfq{i}", [128, D], BF16)) for i in range(2)]
        sq_scr = fsa.enter_context(T(f"{tag}sq", [128, D], BF16))
        logits = fsa.enter_context(T(f"{tag}lg", [128, D], F32))
        descr = fsa.enter_context(T(f"{tag}dsc", [128, D], F32))
        ohb = [fsa.enter_context(T(f"{tag}oh{i}", [128, 8], F32)) for i in range(2)]

        def emit_oht(tt_):
            ob_ = tt_ % 2
            sch.op("tensor", lambda e, ob_=ob_: e.transpose(
                   out=PS[3][0:8, 0:128], in_=ohb[ob_][:], identity=id32[:]),
                   reads=[("ohb", ob_), ("id32",)], writes=[("ps3",)])
            sch.op("vector", lambda e, tt_=tt_: e.tensor_copy(
                   out=ohrows[:, tt_ * 128:(tt_ + 1) * 128], in_=PS[3][0:8, 0:128]),
                   reads=[("ps3",)], writes=[("ohr", tt_)], small=True)
        ps_t, ps_g = PS[0], PS[1]

        def emit_slots(t0, t1):
            o0, o1 = t0 * 128, t1 * 128
            ohr_keys = [("ohr", t) for t in range(t0, t1)]
            sch.op("vector", lambda e: e.tensor_tensor_scan(
                   out=rnk[:, o0:o1], data0=ohrows[:, o0:o1],
                   data1=ohrows[:, o0:o1], initial=0.0,
                   op0=ALU.add, op1=ALU.bypass),
                   reads=ohr_keys + [("rnk",)], writes=[("rnk",)])
            if t0 > 0:
                sch.op("vector", lambda e: e.tensor_scalar_add(
                       out=rnk[:, o0:o1], in0=rnk[:, o0:o1],
                       scalar1=rnk[:, o0 - 1:o0]),
                       reads=[("rnk",)], writes=[("rnk",)], small=True)
            sch.op("vector", lambda e: e.tensor_sub(out=rnk2[:, o0:o1],
                                                    in0=rnk[:, o0:o1],
                                                    in1=ohrows[:, o0:o1]),
                   reads=[("rnk",)] + ohr_keys, writes=[("rnk2", t0)],
                   small=True)
            sch.op("vector", lambda e: e.tensor_scalar_min(out=rnk2[:, o0:o1],
                                                           in0=rnk2[:, o0:o1],
                                                           scalar1=float(cap - 1)),
                   reads=[("rnk2", t0)], writes=[("rnk2", t0)], small=True)
            sch.op("vector", lambda e: e.tensor_scalar_add(out=rnk2[:, o0:o1],
                                                           in0=rnk2[:, o0:o1],
                                                           scalar1=base_ap),
                   reads=[("rnk2", t0), base_key], writes=[("rnk2", t0)],
                   small=True)
            sch.op("vector", lambda e: e.tensor_mul(out=rnk2[:, o0:o1],
                                                    in0=rnk2[:, o0:o1],
                                                    in1=ohrows[:, o0:o1]),
                   reads=[("rnk2", t0)] + ohr_keys, writes=[("rnk2", t0)],
                   small=True)
            for c in range(o0 // 512, o1 // 512):
                sch.op("tensor", lambda e, c=c: e.matmul(
                       PS[3][0:1, 0:512], ones8[:], rnk2[:, c * 512:(c + 1) * 512],
                       start=True, stop=True),
                       reads=[("rnk2", t0), ("ones8",)], writes=[("ps3",)])
                sch.op("vector", lambda e, c=c: e.tensor_copy(
                       out=slotrow[:, c * 512:(c + 1) * 512], in_=PS[3][0:1, 0:512]),
                       reads=[("ps3",)], writes=[("slr", t0)], small=True)
            sch.dma("gpsimd", f"{tag}slsp", slots_d[:, o0:o1], slotrow[:, o0:o1],
                    reads=[("slr", t0)], writes=[("slots_d", t0)])
            for t in range(t0, t1):
                bs = t % 4
                sch.dma("sync", f"{tag}slld{bs}", slotcf[bs][:],
                        slots_d[:, t * 128:(t + 1) * 128].rearrange("o p -> p o"),
                        reads=[("slots_d", t0)], writes=[("slcf", bs)])
                sch.op("vector", lambda e, t=t, bs=bs: e.tensor_copy(
                       out=slotu[:, t:t + 1], in_=slotcf[bs][:]),
                       reads=[("slcf", bs)], writes=[("slu", t)], small=True)
            for t in range(t0, t1):
                bs = t % 4
                sch.dma("sync", f"{tag}xrl{bs}", sx[bs][:],
                        xln_d[t * 128:(t + 1) * 128, :],
                        reads=[("xlnd", t)], writes=[("sx", bs)])
                sch._mkop("gpsimd",
                          lambda e, bs=bs, t=t: e.indirect_dma_start(
                              out=sorted_d,
                              out_offset=IndirectOffsetOnAxis(ap=slotu[:, t:t + 1],
                                                              axis=0),
                              in_=sx[bs][:], in_offset=None),
                          [("sx", bs), ("slu", t)], [("sorted", t)], (), False,
                          True, f"{tag}sct{bs}")

        def prep_a(t):
            bs = t % 2
            pt_b = PS[0]
            ptkey = ("pst",)
            sch.dma("sync", f"{tag}xld{bs}", xt[bs][:], tile_srcs[t],
                    writes=[("xt", bs)])
            emit_ln(xt[bs][:], [("xt", bs)], xln[bs][:], ("xln", bs),
                    sq_scr, ("sq",))
            sch.dma("gpsimd", f"{tag}xlsp{bs}", xln_d[t * 128:(t + 1) * 128, :],
                    xln[bs][:], reads=[("xln", bs)], writes=[("xlnd", t)])
            for c in range(8):
                sch.op("tensor",
                       lambda e, c=c, bs=bs, pt_b=pt_b: e.transpose(
                           out=pt_b[:, c * 128:(c + 1) * 128],
                           in_=xln[bs][:, c * 128:(c + 1) * 128], identity=id32[:]),
                       reads=[("xln", bs), ("id32",)],
                       writes=([ptkey] if c in (0, 7) else []))
            sch.op("scalar", lambda e, bs=bs, pt_b=pt_b: e.activation(
                   out=xfh[bs][:], in_=pt_b[:], func=AF.Copy),
                   reads=[ptkey], writes=[("xfh", bs)])
            sch.op("vector", lambda e, bs=bs, pt_b=pt_b: e.tensor_sub(
                   out=r1b[bs][:], in0=pt_b[:], in1=xfh[bs][:]),
                   reads=[ptkey, ("xfh", bs)], writes=[("r1", bs)])
            sch.op("scalar", lambda e, bs=bs: e.activation(out=xfm[bs][:], in_=r1b[bs][:],
                                                           func=AF.Copy),
                   reads=[("r1", bs)], writes=[("xfm", bs)])
            sch.op("vector", lambda e, bs=bs: e.tensor_sub(out=xfq[bs][:], in0=r1b[bs][:],
                                                           in1=xfm[bs][:]),
                   reads=[("r1", bs), ("xfm", bs)], writes=[("xfq", bs)])

        prep_a(0)
        for t in range(ntiles):
            bs = t % 2
            if t + 1 < ntiles:
                prep_a(t + 1)
            ps_g = PS[1] if bs == 0 else PS[2]
            gkey = ("psg", bs)
            for n in range(2):
                plan = []
                for c in range(8):
                    xh = xfh[bs][:, c * 128:(c + 1) * 128]
                    xm = xfm[bs][:, c * 128:(c + 1) * 128]
                    xq = xfq[bs][:, c * 128:(c + 1) * 128]
                    wh = wgh[:, c * 1024 + n * 512: c * 1024 + n * 512 + 512]
                    wl = wgl[:, c * 1024 + n * 512: c * 1024 + n * 512 + 512]
                    wq = wgq[:, c * 1024 + n * 512: c * 1024 + n * 512 + 512]
                    plan += [(xh, wh), (xh, wl), (xm, wh), (xm, wl), (xh, wq), (xq, wh)]
                for i, (xs, ws) in enumerate(plan):
                    st, sp = (i == 0), (i == len(plan) - 1)
                    sch.op("tensor",
                           lambda e, xs=xs, ws=ws, st=st, sp=sp, n=n, ps_g=ps_g:
                           e.matmul(ps_g[:, n * 512:(n + 1) * 512], xs, ws,
                                    start=st, stop=sp),
                           reads=[("xfh", bs), ("xfm", bs), ("xfq", bs),
                                  ("wg", "h"), ("wg", "l"), ("wg", "q")],
                           writes=([gkey] if (st or sp) else []))
            sch.op("vector", lambda e, ps_g=ps_g: e.tensor_add(out=logits[:],
                                                               in0=ps_g[:],
                                                               in1=bg_sb[:]),
                   reads=[gkey, ("bg",)], writes=[("lgt",)])
            sch.op("vector", lambda e: e.reduce_max(out=mxv[:], in_=logits[:], axis=AX),
                   reads=[("lgt",)], writes=[("mxv",)], small=True)
            sch.op("vector", lambda e: e.tensor_scalar_mul(out=negmax[:], in0=mxv[:],
                                                           scalar1=-1.0),
                   reads=[("mxv",)], writes=[("negmax",)], small=True)
            sch.op("vector", lambda e: e.scalar_tensor_tensor(
                   out=descr[:], in0=logits[:], scalar=mxv[:], in1=desc_sb[:],
                   op0=ALU.is_equal, op1=ALU.mult),
                   reads=[("lgt",), ("mxv",), ("desc",)], writes=[("dscr",)])
            sch.op("vector", lambda e: e.reduce_max(out=rmax[:], in_=descr[:], axis=AX),
                   reads=[("dscr",)], writes=[("rmax",)], small=True)
            sch.op("vector", lambda e: e.scalar_tensor_tensor(
                   out=descr[:], in0=descr[:], scalar=rmax[:], in1=mod8_sb[:],
                   op0=ALU.is_equal, op1=ALU.mult),
                   reads=[("dscr",), ("rmax",), ("mod8",)], writes=[("dscr",)])
            sch.op("vector", lambda e: e.reduce_sum(out=evec[:], in_=descr[:], axis=AX),
                   reads=[("dscr",)], writes=[("evec",)], small=True)
            sch.op("scalar", lambda e: e.activation(out=sq_scr[:], in_=logits[:],
                                                    func=AF.Exp, bias=negmax[:],
                                                    accum_out=sumexp[:]),
                   reads=[("lgt",), ("negmax",)], writes=[("sq",), ("sume",)],
                   small=True)
            sch.op("vector", lambda e: e.reciprocal(out=pd[:], in_=sumexp[:]),
                   reads=[("sume",)], writes=[("pd",)], small=True)
            sch.op("vector", lambda e, t=t: e.tensor_scalar_mul(
                   out=phbuf[:, t:t + 1], in0=pd[:], scalar1=0.5),
                   reads=[("pd",)], writes=[("phb", t)], small=True)
            ob = t % 2
            sch.op("vector", lambda e, ob=ob: e.tensor_single_scalar(
                   out=ohb[ob][:], in_=iota8[:], scalar=evec[:], op=ALU.is_equal),
                   reads=[("evec",), ("iota8",)], writes=[("ohb", ob)], small=True)
            if t > 0:
                emit_oht(t - 1)
            if t == ntiles - 1:
                emit_oht(t)
            qsz = ntiles // 4
            if t in (qsz, 2 * qsz, 3 * qsz):
                emit_slots(t - qsz, t)

        emit_slots(3 * (ntiles // 4), ntiles)
        fsa.close()
        sch.phase()
        fsb = contextlib.ExitStack()
        weh = [fsb.enter_context(T(f"{tag}weh{i}", [128, 8 * D], BF16)) for i in range(2)]
        wel = [fsb.enter_context(T(f"{tag}wel{i}", [128, 8 * D], BF16)) for i in range(2)]
        be2 = [fsb.enter_context(T(f"{tag}be{i}", [128, D], F32)) for i in range(2)]
        hout = [fsb.enter_context(T(f"{tag}ho{i}", [128, D], F32)) for i in range(2)]
        sorted_keys = [("sorted", t) for t in range(ntiles)]
        def prep_b(s_i):
            e_i = s_i // tpe
            wb = e_i % 2
            sbs = s_i % 2
            pt_b = PS[0] if sbs == 0 else PS[1]
            ptkey = ("pst", sbs)
            if s_i % tpe == 0:
                sch.dma("sync", f"weld{wb}h", weh[wb][:], weh_d[e_i],
                        reads=[("weh_d", e_i)], writes=[("weh", wb)])
                sch.dma("sync", f"weld{wb}l", wel[wb][:], wel_d[e_i],
                        reads=[("wel_d", e_i)], writes=[("wel", wb)])
                sch.dma("sync", f"beld{wb}", be2[wb][:],
                        be_b[:, e_i * D:(e_i + 1) * D], writes=[("be2", wb)])
            sch.dma("sync", f"{tag}sxl{sbs}", sx[sbs][:],
                    sorted_d[s_i * 128:(s_i + 1) * 128, :],
                    reads=sorted_keys, writes=[("sx", sbs)])
            for c in range(8):
                sch.op("tensor",
                       lambda e, c=c, sbs=sbs, pt_b=pt_b: e.transpose(
                           out=pt_b[:, c * 128:(c + 1) * 128],
                           in_=sx[sbs][:, c * 128:(c + 1) * 128], identity=id32[:]),
                       reads=[("sx", sbs), ("id32",)],
                       writes=([ptkey] if c in (0, 7) else []))
            sch.op("scalar", lambda e, sbs=sbs, pt_b=pt_b: e.activation(
                   out=xfh[sbs][:], in_=pt_b[:], func=AF.Copy),
                   reads=[ptkey], writes=[("xfh", sbs)])
            sch.op("vector", lambda e, sbs=sbs, pt_b=pt_b: e.tensor_sub(
                   out=xfm[sbs][:], in0=pt_b[:], in1=xfh[sbs][:]),
                   reads=[ptkey, ("xfh", sbs)], writes=[("xfm", sbs)])

        prep_b(0)
        for s_i in range(nsort):
            e_i = s_i // tpe
            wb = e_i % 2
            sbs = s_i % 2
            if s_i + 1 < nsort:
                prep_b(s_i + 1)
            hb = PS[2] if sbs == 0 else PS[3]
            hkey = ("psh0",) if sbs == 0 else ("ps3",)
            for n in range(2):
                plan = []
                for c in range(8):
                    xh = xfh[sbs][:, c * 128:(c + 1) * 128]
                    xm = xfm[sbs][:, c * 128:(c + 1) * 128]
                    wh = weh[wb][:, c * 1024 + n * 512: c * 1024 + n * 512 + 512]
                    wl = wel[wb][:, c * 1024 + n * 512: c * 1024 + n * 512 + 512]
                    plan += [(xh, wh), (xh, wl), (xm, wh)]
                for i, (xs, ws) in enumerate(plan):
                    st, sp = (i == 0), (i == len(plan) - 1)
                    sch.op("tensor",
                           lambda e, xs=xs, ws=ws, st=st, sp=sp, n=n, hb=hb:
                           e.matmul(hb[:, n * 512:(n + 1) * 512], xs, ws,
                                    start=st, stop=sp),
                           reads=[("xfh", sbs), ("xfm", sbs), ("weh", wb),
                                  ("wel", wb)],
                           writes=([hkey] if (st or sp) else []))
            sch.op("vector", lambda e, hb=hb, wb=wb: e.tensor_add(
                   out=hb[:], in0=hb[:], in1=be2[wb][:]),
                   reads=[hkey, ("be2", wb)], writes=[hkey])
            sch.op("scalar", lambda e, hb=hb, sbs=sbs: e.activation(
                   out=hout[sbs][:], in_=hb[:], func=AF.Relu),
                   reads=[hkey], writes=[("ho", sbs)])
            sch.dma("gpsimd", f"{tag}hsp{sbs}", hsort_d[s_i * 128:(s_i + 1) * 128, :],
                    hout[sbs][:], reads=[("ho", sbs)], writes=[("hsd", s_i)])
        fsb.close()
        sch.phase()
        fsc = contextlib.ExitStack()
        ghat = [fsc.enter_context(T(f"{tag}gh{i}", [128, D], F32)) for i in range(2)]
        xres = [fsc.enter_context(T(f"{tag}xr{i}", [128, D], F32)) for i in range(2)]
        xacc = [fsc.enter_context(T(f"{tag}xa{i}", [128, D], F32)) for i in range(2)]
        hsd_keys = [("hsd", s_i) for s_i in range(nsort)]
        for t in range(ntiles):
            bs = t % 2
            tok = sch._mkop("gpsimd",
                            lambda e, bs=bs, t=t: e.indirect_dma_start(
                                out=ghat[bs][:], out_offset=None,
                                in_=hsort_d,
                                in_offset=IndirectOffsetOnAxis(ap=slotu[:, t:t + 1],
                                                               axis=0)),
                            hsd_keys + [("slu", t)], [("gh", bs)], (), False,
                            True, f"{tag}gth{bs}")
            sch.dma("sync", f"{tag}xrs{bs}", xres[bs][:], tile_srcs[t],
                    writes=[("xres", bs)])
            sch.op("vector", lambda e, bs=bs, t=t: e.scalar_tensor_tensor(
                   out=xacc[bs][:], in0=ghat[bs][:], scalar=phbuf[:, t:t + 1],
                   in1=xres[bs][:], op0=ALU.mult, op1=ALU.add),
                   reads=[("gh", bs), ("phb", t), ("xres", bs)],
                   writes=[("xacc", bs)])
            kind, o = tile_dsts[t]
            if kind == "x1":
                sch.dma("gpsimd", f"{tag}x1o{bs}", x1_d[o * 128:(o + 1) * 128, :],
                        xacc[bs][:], reads=[("xacc", bs)])
            else:
                sch.dma("gpsimd", f"{tag}oo{bs}", out_d[o * 128:(o + 1) * 128, :],
                        xacc[bs][:], reads=[("xacc", bs)])
        fsc.close()
        fs.close()
        sch.phase()

    if "p1a" in stages:
        emit_ffn_sparse([x_in[t * 128:(t + 1) * 128, :] for t in range(NT)],
                        [("x1", t) for t in range(NT)], "fa", CAP1, base1[:],
                        ("base1",))

    if "p1b" in stages:
        fs = contextlib.ExitStack()
        wk_h = fs.enter_context(nc.sbuf_tensor("wk_h", [128, 8 * D], BF16))
        wk_l = fs.enter_context(nc.sbuf_tensor("wk_l", [128, 8 * D], BF16))
        wv_h = fs.enter_context(nc.sbuf_tensor("wv_h", [128, 8 * D], BF16))
        wv_l = fs.enter_context(nc.sbuf_tensor("wv_l", [128, 8 * D], BF16))
        wq_h = fs.enter_context(nc.sbuf_tensor("wq_h", [128, 8 * D], BF16))
        wq_l = fs.enter_context(nc.sbuf_tensor("wq_l", [128, 8 * D], BF16))
        xt = [fs.enter_context(nc.sbuf_tensor(f"bxt{i}", [128, D], F32)) for i in range(2)]
        xln = [fs.enter_context(nc.sbuf_tensor(f"bxln{i}", [128, D], F32)) for i in range(2)]
        xfm_h = fs.enter_context(nc.sbuf_tensor("bxfh", [128, 4 * D], BF16))
        xfm_l = fs.enter_context(nc.sbuf_tensor("bxfl", [128, 4 * D], BF16))
        sq_scr = fs.enter_context(nc.sbuf_tensor("bsq", [128, D], BF16))
        vbuf = fs.enter_context(nc.sbuf_tensor("vbuf", [128, 16 * 65], F32))
        ksb = [fs.enter_context(nc.sbuf_tensor(f"ksb{i}", [128, 512], F32)) for i in range(2)]
        qsb = [fs.enter_context(nc.sbuf_tensor(f"qsb{i}", [128, 256], F32)) for i in range(2)]
        for w, hb, lb in (("wk", wk_h, wk_l), ("wv", wv_h, wv_l), ("wq", wq_h, wq_l)):
            sch.dma("sync", "pbw", hb[:], wsp_d[w + "h"], reads=[("wspd", w, "h")],
                    writes=[("pw", w, "h")])
            sch.dma("sync", "pbw", lb[:], wsp_d[w + "l"], reads=[("wspd", w, "l")],
                    writes=[("pw", w, "l")])
        vones = vbuf[:].rearrange("p (j c) -> p j c", c=65)[:, :, 64:65]
        sch.op("vector", lambda e: e.memset(vones, 1.0), writes=[("vones",)])
        ps_t, ps_v = PS[0], PS[1]
        ps_k, ps_q = [PS[2][:, 0:512], PS[2][:, 512:1024]], PS[3]

        for g in range(NT // 4):
            xf_keys = []
            for i in range(4):
                t = 4 * g + i
                bs = i % 2
                ptb = ps_t
                ptk = ("bpst",)
                sch.dma("sync", f"bxld{bs}", xt[bs][:], x1_d[t * 128:(t + 1) * 128, :],
                        writes=[("bxt", bs)])
                emit_ln(xt[bs][:], [("bxt", bs)], xln[bs][:], ("bxln", bs),
                        sq_scr, ("bsq",))
                hi_ap = xfm_h[:, i * 1024:(i + 1) * 1024]
                lo_ap = xfm_l[:, i * 1024:(i + 1) * 1024]
                hk, lk = ("bxfm", i, "h"), ("bxfm", i, "l")
                emit_tsplit(xln[bs][:], ("bxln", bs), ptb, ptk, hi_ap, lo_ap,
                            hk, lk)
                xf_keys.append([hk, lk])
            for i in range(4):
                t = 4 * g + i
                hk, lk = xf_keys[i]
                lhs_pairs = [(xfm_h[:, i * 1024 + c * 128: i * 1024 + (c + 1) * 128],
                              xfm_l[:, i * 1024 + c * 128: i * 1024 + (c + 1) * 128])
                             for c in range(8)]
                emit_mm3(ps_v, ("psv",), lhs_pairs,
                         lambda c, n, which: (wv_h if which == "h" else wv_l)[
                             :, c * 1024 + n * 512: c * 1024 + n * 512 + 512],
                         [hk, lk, ("pw", "wv", "h"), ("pw", "wv", "l")])
                vdst = vbuf[:].rearrange("p (j c) -> p j c", c=65)[:, :, 0:64]
                vsrc = ps_v[:].rearrange("p (j c) -> p j c", c=64)
                sch.op("vector", lambda e, vdst=vdst, vsrc=vsrc: e.tensor_copy(
                       out=vdst, in_=vsrc),
                       reads=[("psv",), ("vones",)], writes=[("vbuf",)])
                for p_i in range(NPAIR):
                    sch.dma("gpsimd", f"vsp{p_i}",
                            v_d[p_i, t * 128:(t + 1) * 128, :],
                            vbuf[:, p_i * 130:(p_i + 1) * 130],
                            reads=[("vbuf",)])
            xv_h = xfm_h[:].rearrange("p (i c k) -> p c i k", i=4, c=8)
            xv_l = xfm_l[:].rearrange("p (i c k) -> p c i k", i=4, c=8)
            for c8 in range(8):
                kb = c8 % 2
                lhs_pairs = [(wk_h[:, c * 1024 + c8 * 128: c * 1024 + c8 * 128 + 128],
                              wk_l[:, c * 1024 + c8 * 128: c * 1024 + c8 * 128 + 128])
                             for c in range(8)]
                emit_mm3(ps_k[kb], ("psk", kb), lhs_pairs,
                         lambda c, n, which: (xv_h if which == "h" else xv_l)[:, c],
                         [("pw", "wk", "h"), ("pw", "wk", "l")] +
                         [k for ks in xf_keys for k in ks], nhalves=1)
                sch.op("scalar", lambda e, kb=kb: e.activation(out=ksb[kb][:],
                                                               in_=ps_k[kb],
                                                               func=AF.Copy),
                       reads=[("psk", kb)], writes=[("ksb", kb)])
                sch.dma("gpsimd", f"ksp{kb}",
                        k_d[c8 * 128:(c8 + 1) * 128, g * 512:(g + 1) * 512],
                        ksb[kb][:], reads=[("ksb", kb)])
            for c8 in range(8):
                qb = c8 % 2
                lhs_pairs = [(wq_h[:, c * 1024 + c8 * 128: c * 1024 + c8 * 128 + 128],
                              wq_l[:, c * 1024 + c8 * 128: c * 1024 + c8 * 128 + 128])
                             for c in range(8)]
                emit_mm3(ps_q[:, 0:256], ("psq",), lhs_pairs,
                         lambda c, n, which: (xv_h if which == "h" else xv_l)[:, c, 0::2, :],
                         [("pw", "wq", "h"), ("pw", "wq", "l"),
                          xf_keys[0][0], xf_keys[0][1], xf_keys[2][0], xf_keys[2][1]],
                         nhalves=1, hw=256)
                sch.op("scalar", lambda e, qb=qb: e.activation(out=qsb[qb][:],
                                                               in_=ps_q[:, 0:256],
                                                               func=AF.Copy),
                       reads=[("psq",)], writes=[("qsb", qb)])
                sch.dma("gpsimd", f"qsp{qb}",
                        q_d[c8 * 128:(c8 + 1) * 128, g * 256:(g + 1) * 256],
                        qsb[qb][:], reads=[("qsb", qb)])
        fs.close()
        sch.phase()


    if "p2" in stages:
        fs = contextlib.ExitStack()
        fs2 = contextlib.ExitStack()
        kpair2 = [fs2.enter_context(nc.sbuf_tensor(f"kpair{i}", [128, S], F32))
                  for i in range(2)]
        qpair2 = [fs2.enter_context(nc.sbuf_tensor(f"qpair{i}", [128, NOWN * 128], F32))
                  for i in range(2)]
        vpair2 = [fs2.enter_context(nc.sbuf_tensor("vpair0", [128, 32 * 130], F32))] * 2
        kh2 = [fs2.enter_context(nc.sbuf_tensor(f"kh{i}", [128, S], BF16))
               for i in range(2)]
        kl2 = [fs2.enter_context(nc.sbuf_tensor(f"kl{i}", [128, S], BF16))
               for i in range(2)]
        qh2 = [fs2.enter_context(nc.sbuf_tensor(f"qh{i}", [128, NOWN * 128], BF16))
               for i in range(2)]
        ql2 = [fs2.enter_context(nc.sbuf_tensor(f"ql{i}", [128, NOWN * 128], BF16))
               for i in range(2)]
        att_st = [fs2.enter_context(nc.sbuf_tensor(f"attst{i}", [128, 256], F32))
                  for i in range(2)]
        pt = [[fs2.enter_context(nc.sbuf_tensor(f"pt{h}_{b}", [128, 512], F32))
               for b in range(2)] for h in range(2)]
        rsum = [fs2.enter_context(nc.sbuf_tensor(f"rsum{i}", [1, 256], F32))
                for i in range(4)]
        recs = [fs2.enter_context(nc.sbuf_tensor(f"recs{i}", [1, 256], F32))
                for i in range(4)]
        tmpo = fs2.enter_context(nc.sbuf_tensor("tmpo", [64, 256], F32))
        def pair_load(p_i):
            pb = p_i % 2
            sch.dma("sync", f"kpld{pb}", kpair2[pb][:],
                    k_d[p_i * 128:(p_i + 1) * 128, :], writes=[("kpair", pb)])
            sch.dma("sync", f"qpld{pb}", qpair2[pb][:],
                    q_d[p_i * 128:(p_i + 1) * 128, :], writes=[("qpair", pb)])

        pair_load(0)
        final_norm = [None]
        for p_i in range(NPAIR):
            pb = p_i % 2
            kpair, qpair, vpair = kpair2[pb], qpair2[pb], vpair2[pb]
            kh, kl, qh, ql = kh2[pb], kl2[pb], qh2[pb], ql2[pb]
            sch.op("vector", lambda e, kh=kh, kpair=kpair: e.tensor_copy(
                   out=kh[:], in_=kpair[:]),
                   reads=[("kpair", pb)], writes=[("kh", pb)])
            sch.op("vector", lambda e, kl=kl, kh=kh, kpair=kpair: e.tensor_sub(
                   out=kl[:], in0=kpair[:], in1=kh[:]),
                   reads=[("kpair", pb), ("kh", pb)], writes=[("kl", pb)])
            sch.op("vector", lambda e, qh=qh, qpair=qpair: e.tensor_copy(
                   out=qh[:], in_=qpair[:]),
                   reads=[("qpair", pb)], writes=[("qh", pb)])
            sch.op("vector", lambda e, ql=ql, qh=qh, qpair=qpair: e.tensor_sub(
                   out=ql[:], in0=qpair[:], in1=qh[:]),
                   reads=[("qpair", pb), ("qh", pb)], writes=[("ql", pb)])
            sch.dma("sync", "vpld",
                    vpair[:].rearrange("pt (m c) -> pt m c", c=130),
                    v_d[p_i].rearrange("(m pt) c -> pt m c", pt=128),
                    writes=[("vpair", 0), ("vpair", 1)])
            if p_i + 1 < NPAIR:
                pair_load(p_i + 1)
            if final_norm[0] is not None:
                final_norm[0]()
            norm_pending = []

            def emit_norm():
                for (hp_, ab_, pw_, pi_, t_rs_) in norm_pending:
                    pokey_ = ("pso", hp_)
                    sl_ = ab_ * 2 + hp_
                    sch.op("tensor", lambda e, sl_=sl_: e.matmul(
                           PS[3][0:64, 0:256], ones64[:], recs[sl_][:],
                           start=True, stop=True),
                           reads=[("recs", sl_), ("ones64",)], writes=[("psbc",)])
                    sch.op("scalar", lambda e, hp=hp_: e.activation(
                           out=tmpo[:], in_=PS[2][0:64, hp * 512:hp * 512 + 256],
                           func=AF.Copy),
                           reads=[pokey_], writes=[("tmpo",)], after=[t_rs_])
                    dst = att_st[ab_][hp_ * 64:(hp_ + 1) * 64, :]
                    sch.op("vector", lambda e, dst=dst: e.tensor_mul(
                           out=dst, in0=tmpo[:], in1=PS[3][0:64, 0:256]),
                           reads=[("tmpo",), ("psbc",)],
                           writes=[("attst", ab_, hp_)])
                if norm_pending:
                    _, ab_, pw_, pi_, _t = norm_pending[0]
                    sch.dma("gpsimd", f"atsp{ab_}",
                            attn_d[:, pi_ * 2048 + pw_ * 256:
                                   pi_ * 2048 + (pw_ + 1) * 256],
                            att_st[ab_][:],
                            reads=[("attst", ab_, 0), ("attst", ab_, 1)],
                            writes=[("afm", pi_, pw_)])
                norm_pending.clear()

            final_norm[0] = emit_norm
            for w in range(8):
                n_m = 4 * w + 4
                for hp in range(2):
                    tp = (0, 0) if hp == 0 else (64, 0)
                    pbase = hp * 64
                    po = PS[2][0:65, hp * 512:hp * 512 + 256]
                    pokey = ("pso", hp)

                    def sco(mp, hp=hp, w=w, tp=tp, pbase=pbase, pb=pb, kh=kh,
                            kl=kl, qh=qh, ql=ql):
                        buf = mp % 2
                        skey = ("pss", hp, buf)
                        for m in (2 * mp, 2 * mp + 1):
                            sps = PS[hp][:, buf * 512 + (m % 2) * 256:
                                         buf * 512 + (m % 2) * 256 + 256]
                            plan3 = [(kh, qh), (kh, ql), (kl, qh)]
                            for pi3, (ka, qa) in enumerate(plan3):
                                sch.op("tensor", lambda e, sps=sps, m=m, w=w, tp=tp,
                                       pbase=pbase, ka=ka, qa=qa, pi3=pi3:
                                       e.matmul(sps,
                                                ka[pbase:pbase + 64,
                                                   m * 128:(m + 1) * 128],
                                                qa[pbase:pbase + 64,
                                                   w * 256:(w + 1) * 256],
                                                start=(pi3 == 0), stop=(pi3 == 2),
                                                tile_position=tp),
                                       reads=[("kh", pb), ("kl", pb), ("qh", pb),
                                              ("ql", pb)],
                                       writes=([skey] if pi3 in (0, 2) else []))
                        for m in (2 * mp, 2 * mp + 1):
                            km = m - (4 * w)
                            if 0 <= km <= 3:
                                sps = PS[hp][:, buf * 512 + (m % 2) * 256:
                                             buf * 512 + (m % 2) * 256 + 256]
                                sch.op("vector", lambda e, sps=sps, km=km: e.tensor_add(
                                       out=sps, in0=sps,
                                       in1=maskc[:, km * 256:(km + 1) * 256]),
                                       reads=[skey, ("maskc",)], writes=[skey])
                        ptb = pt[hp][buf]
                        sch.op("scalar", lambda e, ptb=ptb, hp=hp, buf=buf:
                               e.activation(out=ptb[:], in_=PS[hp][:, buf * 512:
                                                                   buf * 512 + 512],
                                            func=AF.Exp, scale=0.125, bias=negC_ap),
                               reads=[skey, ("sconst",)], writes=[("pt", hp, buf)])

                    sco(0)
                    if hp == 0:
                        emit_norm()
                    for mp in range(n_m // 2):
                        buf = mp % 2
                        if mp + 1 < n_m // 2:
                            sco(mp + 1)
                        for m in (2 * mp, 2 * mp + 1):
                            vst = vpair[:, m * 130 + hp * 65: m * 130 + hp * 65 + 65]
                            mv = pt[hp][buf][:, (m % 2) * 256:(m % 2) * 256 + 256]
                            st, sp = (m == 0), (m == n_m - 1)
                            sch.op("tensor", lambda e, po=po, vst=vst, mv=mv, st=st,
                                   sp=sp:
                                   e.matmul(po, vst, mv, start=st, stop=sp),
                                   reads=[("vpair", pb), ("pt", hp, buf)] +
                                         ([pokey] if st else []),
                                   writes=([pokey] if st or sp else []))
                    ab = w % 2
                    sl = ab * 2 + hp
                    t_rs = sch.op("vector", lambda e, hp=hp, sl=sl: e.tensor_copy(
                           out=rsum[sl][:],
                           in_=PS[2][64:65, hp * 512:hp * 512 + 256]),
                           reads=[pokey], writes=[("rsum", sl)], small=True)
                    sch.op("vector", lambda e, sl=sl: e.reciprocal(
                           out=recs[sl][:], in_=rsum[sl][:]),
                           reads=[("rsum", sl)], writes=[("recs", sl)], small=True)
                    norm_pending.append((hp, ab, w, p_i, t_rs))

        if final_norm[0] is not None:
            final_norm[0]()
        fs2.close()
        sch.phase()
        if "p3a" in stages:
            wo_sb = fs.enter_context(nc.sbuf_tensor("wo_sb", [128, 8 * D], F32))
            x1b = [fs.enter_context(nc.sbuf_tensor(f"x1b{i}", [128, D], F32))
                   for i in range(2)]
            x2b = [fs.enter_context(nc.sbuf_tensor(f"x2b{i}", [128, D], F32))
                   for i in range(2)]
            afm_t = [fs.enter_context(nc.sbuf_tensor(f"afmt{i}", [128, D], F32))
                     for i in range(2)]
            sch.dma("sync", "wold", wo_sb[:].rearrange("p (c d) -> p c d", c=8),
                    wo_in.rearrange("(c p) d -> p c d", p=128), writes=[("wo",)])
            afm_all = [("afm", c_, w_) for c_ in range(NPAIR) for w_ in range(8)]
            for j in range(NOWN):
                bs = j % 2
                l = 2 * j
                sch.dma("sync", f"x1ld{bs}", x1b[bs][:],
                        x1_d[l * 128:(l + 1) * 128, :], writes=[("x1b", bs)])
                sch.dma("sync", f"afld{bs}",
                        afm_t[bs][:].rearrange("p (c q) -> p c q", c=8),
                        attn_d.rearrange("p (c q) -> p c q", c=8)[
                            :, :, j * 128:(j + 1) * 128],
                        reads=[("afm", c_, j // 2) for c_ in range(NPAIR)],
                        writes=[("afmt", bs)])
                psw = PS[0] if bs == 0 else PS[1]
                wkey = ("psw", bs)
                for n in range(2):
                    for c in range(8):
                        st, sp = (c == 0), (c == 7)
                        lhs = afm_t[bs][:, c * 128:(c + 1) * 128]
                        rhs = wo_sb[:, c * 1024 + n * 512: c * 1024 + n * 512 + 512]
                        sch.op("tensor", lambda e, psw=psw, lhs=lhs, rhs=rhs, st=st,
                               sp=sp, n=n:
                               e.matmul(psw[:, n * 512:(n + 1) * 512], lhs, rhs,
                                        start=st, stop=sp),
                               reads=[("wo",), ("afmt", bs)],
                               writes=([wkey] if (st or sp) else []))
                sch.op("vector", lambda e, bs=bs: e.tensor_add(
                       out=x2b[bs][:], in0=x1b[bs][:], in1=(PS[0] if bs == 0
                                                            else PS[1])[:]),
                       reads=[wkey, ("x1b", bs)], writes=[("x2b", bs)])
                sch.dma("gpsimd", f"x2sp{bs}", x2_d[j * 128:(j + 1) * 128, :],
                        x2b[bs][:], reads=[("x2b", bs)])
        fs.close()
        sch.phase()

    if "p3b" in stages:
        emit_ffn_sparse([x2_d[j * 128:(j + 1) * 128, :] for j in range(NOWN)],
                        [("out", j) for j in range(NOWN)], "fb", CAP2, base2[:],
                        ("base2",))

    finals = [o.seq for o in sch.all if o.is_dma and o.stream and
              (o.stream.startswith("osp") or o.stream.startswith("x2sp") or
               o.stream.startswith("x1sp") or o.stream.startswith("adump"))]
    sch.finish("sync", finals)
    sch.emit()
    return nc


def stage_inputs(inputs):
    x = np.asarray(inputs["L0_attn_x"], dtype=np.float32)
    wq = np.asarray(inputs["L0_attn_W_Q"], np.float32)
    wk = np.asarray(inputs["L0_attn_W_K"], np.float32)
    wv = np.asarray(inputs["L0_attn_W_V"], np.float32)
    wo = np.asarray(inputs["L0_attn_W_o"], np.float32)
    wg = np.asarray(inputs["L0_ffn_Wg"], np.float32)
    bgv = np.asarray(inputs["L0_ffn_bg"], np.float32)
    we = np.asarray(inputs["L0_ffn_We"], np.float32)
    bev = np.asarray(inputs["L0_ffn_be"], np.float32)

    perm = head_col_perm()
    wq_p = np.ascontiguousarray(wq[:, perm])
    wk_p = np.ascontiguousarray(wk[:, perm])
    wv_p = np.ascontiguousarray(wv[:, perm])
    wo_p = np.ascontiguousarray(wo[perm, :])
    bg_bc = np.ascontiguousarray(np.broadcast_to(bgv[None, :], (128, D)))
    be_bc = np.ascontiguousarray(np.broadcast_to(bev.reshape(1, E * D), (128, E * D)))

    in_maps = []
    for c in range(8):
        b, r = c // 2, c % 2
        xb = x[b].reshape(NT, 128, D)
        if r == 0:
            xl = xb
        else:
            order = [l + 1 - 2 * (l % 2) for l in range(NT)]
            xl = xb[order]
        in_maps.append({
            "x_in": np.ascontiguousarray(xl.reshape(S, D)),
            "wg_in": wg, "bg_b": bg_bc, "we_in": we, "be_b": be_bc,
            "wq_in": wq_p, "wk_in": wk_p, "wv_in": wv_p, "wo_in": wo_p,
            "maskc_in": build_masks(r),
        })
    return in_maps


def gather_output(results):
    out = np.empty((B, S, D), np.float32)
    for c in range(8):
        b, r = c // 2, c % 2
        sh = np.asarray(results[c]["out_d"]).reshape(NOWN, 128, D)
        for j in range(NOWN):
            g = 2 * j + r
            out[b, g * 128:(g + 1) * 128, :] = sh[j]
    return out


def kernel(**inputs):
    nc = build_program()
    in_maps = stage_inputs(inputs)
    res = run_bass_kernel_spmd(nc, in_maps, core_ids=list(range(8)))
    return gather_output(res.results)


# version 11 (final)
# speedup vs baseline: 1.1379x; 1.0656x over previous
import contextlib
import numpy as np
import ml_dtypes
import concourse.bass as bass
import concourse.mybir as mybir
from concourse.bass_utils import run_bass_kernel_spmd
from concourse.bass_core_rust import IndirectOffsetOnAxis

F32 = mybir.dt.float32
BF16 = mybir.dt.bfloat16
AX = mybir.AxisListType.X
AF = mybir.ActivationFunctionType
ALU = mybir.AluOpType

B, S, D, H, DH, E = 4, 4096, 1024, 16, 64, 8
NT = S // 128
NOWN = NT // 2
NPAIR = H // 2
EXP_SHIFT = 20.0
LN_EPS = 1e-5
NEG_BIG = -1e30
G1 = 4
CAP1 = 768
CAP2 = 512
NSORT1 = 8 * CAP1 // 128
NSORT2 = 8 * CAP2 // 128

ENGINES = ("sync", "vector", "scalar", "tensor", "gpsimd")


class Op:
    __slots__ = ("engine", "fn", "seq", "deps", "small", "is_dma", "stream",
                 "inc_val")

    def __init__(self, engine, fn, seq, deps, small, is_dma, stream):
        self.engine, self.fn, self.seq = engine, fn, seq
        self.deps, self.small, self.is_dma = deps, small, is_dma
        self.stream = stream
        self.inc_val = None


class Sch:

    def __init__(self, nc):
        self.nc = nc
        self.all = []
        self.last_writer = {}
        self.readers = {}
        self.stream_last = {}
        self.last_real = {e: None for e in ENGINES}

    def _reduce(self, deps):
        best = {}
        for t in deps:
            p = self.all[t]
            key = ("s", p.stream) if p.is_dma else ("e", p.engine)
            if key not in best or best[key] < t:
                best[key] = t
        return sorted(best.values())

    def _mkop(self, engine, fn, reads, writes, extra, small, is_dma, stream):
        deps = set()
        for k in reads:
            w = self.last_writer.get(k)
            if w is not None:
                deps.add(w)
        for k in writes:
            w = self.last_writer.get(k)
            if w is not None:
                deps.add(w)
            for rd in self.readers.get(k, ()):
                deps.add(rd)
        for t in extra:
            if t is not None:
                deps.add(t)
        if is_dma:
            prev = self.stream_last.get(stream)
            if prev is not None:
                deps.add(prev)
        op = Op(engine, fn, len(self.all), self._reduce(deps), small, is_dma, stream)
        self.all.append(op)
        tok = op.seq
        if is_dma:
            self.stream_last[stream] = tok
        if fn is not None:
            self.last_real[engine] = tok
        for k in writes:
            self.last_writer[k] = tok
            self.readers[k] = []
        for k in reads:
            self.readers.setdefault(k, []).append(tok)
        return tok

    def op(self, engine, fn, reads=(), writes=(), after=(), small=False):
        return self._mkop(engine, fn, reads, writes, after, small, False, None)

    def dma(self, engine, stream, out, in_, reads=(), writes=(), after=()):
        fn = lambda e, o=out, i=in_: e.dma_start(out=o, in_=i)
        return self._mkop(engine, fn, reads, writes, after, False, True, stream)

    def phase(self):
        lasts = [t for t in self.last_real.values() if t is not None]
        lasts += list(self.stream_last.values())
        for e in ENGINES:
            self._mkop(e, None, (), (), lasts, False, False, None)

    def finish(self, engine, tokens):
        self._mkop(engine, None, (), (), tokens, False, False, None)

    def emit(self):
        nc = self.nc
        ops = self.all
        def needs_sync(p, op):
            if p.is_dma:
                return False
            if p.engine == "tensor" and op.engine == "tensor":
                return False
            return True

        for op in ops:
            for d in op.deps:
                p = ops[d]
                if needs_sync(p, op) and p.inc_val is None:
                    p.inc_val = ("prog", p.engine, 0)
        prog_sem = {e: nc.alloc_semaphore(f"prog_{e}") for e in ENGINES}
        prog_count = {e: 0 for e in ENGINES}
        dma_sems, dma_count = {}, {}
        for op in ops:
            if op.is_dma:
                s = op.stream
                if s not in dma_sems:
                    dma_sems[s] = nc.alloc_semaphore(f"dm_{s}")
                    dma_count[s] = 0
                dma_count[s] += 16
                op.inc_val = ("dma", s, dma_count[s])
            elif op.inc_val is not None:
                prog_count[op.engine] += 1
                op.inc_val = ("prog", op.engine, prog_count[op.engine])

        with nc.Block() as block:
            for ename in ENGINES:
                eops = [o for o in ops if o.engine == ename]

                def make(eops=eops, ename=ename):
                    def run(eng):
                        hi = {}

                        def wait(sem, key, val):
                            if hi.get(key, 0) < val:
                                eng.wait_ge(sem, val)
                                hi[key] = val

                        for op in eops:
                            for d in op.deps:
                                p = ops[d]
                                if p.inc_val is None:
                                    continue
                                kind, key, val = p.inc_val
                                if kind == "dma":
                                    wait(dma_sems[key], ("d", key), val)
                                elif needs_sync(p, op):
                                    wait(prog_sem[key], ("p", key), val)
                            ins = op.fn(eng) if op.fn is not None else None
                            if op.inc_val is not None and ins is not None:
                                kind, key, _ = op.inc_val
                                if kind == "dma":
                                    ins.then_inc(dma_sems[key], 16)
                                else:
                                    ins.then_inc(prog_sem[key], 1)
                            elif op.inc_val is not None and not op.is_dma:
                                raise RuntimeError("inc on fn-less op")
                    return run

                getattr(block, ename)(make())
        n_sems = len(prog_sem) + len(dma_sems)
        return n_sems


def head_col_perm():
    perm = []
    for p in range(NPAIR):
        for hp in range(2):
            h = p + NPAIR * hp
            perm.extend(range(h * DH, (h + 1) * DH))
    return np.array(perm)


def build_masks(role):
    k = np.arange(128)[:, None]
    q = np.arange(128)[None, :]
    tri = np.where(k > q, NEG_BIG, 0.0).astype(np.float32)
    zero = np.zeros((128, 128), np.float32)
    fm = np.full((128, 128), NEG_BIG, np.float32)
    blk = fm if role == 0 else zero
    m0 = np.concatenate([tri, zero], axis=1)
    m1 = np.concatenate([blk, zero], axis=1)
    m2 = np.concatenate([fm, tri], axis=1)
    m3 = np.concatenate([fm, blk], axis=1)
    return np.ascontiguousarray(np.concatenate([m0, m1, m2, m3], axis=1))


def build_program(stages=("p1a", "p1b", "p2", "p3a", "p3b"), dev=False, p2_input=False):
    nc = bass.Bass("TRN2", target_bir_lowering=False, debug=False)
    scratch_kind = "ExternalOutput" if dev else "Internal"

    x_in = nc.dram_tensor("x_in", [S, D], F32, kind="ExternalInput").ap()
    wg_in = nc.dram_tensor("wg_in", [D, D], F32, kind="ExternalInput").ap()
    bg_b = nc.dram_tensor("bg_b", [128, D], F32, kind="ExternalInput").ap()
    we_in = nc.dram_tensor("we_in", [E, D, D], F32, kind="ExternalInput").ap()
    be_b = nc.dram_tensor("be_b", [128, E * D], F32, kind="ExternalInput").ap()
    wq_in = nc.dram_tensor("wq_in", [D, D], F32, kind="ExternalInput").ap()
    wk_in = nc.dram_tensor("wk_in", [D, D], F32, kind="ExternalInput").ap()
    wv_in = nc.dram_tensor("wv_in", [D, D], F32, kind="ExternalInput").ap()
    wo_in = nc.dram_tensor("wo_in", [D, D], F32, kind="ExternalInput").ap()
    maskc_in = nc.dram_tensor("maskc_in", [128, 4 * 256], F32, kind="ExternalInput").ap()
    out_d = nc.dram_tensor("out_d", [NOWN * 128, D], F32, kind="ExternalOutput").ap()

    x1_d = nc.dram_tensor("x1_d", [S, D], F32, kind=scratch_kind).ap()
    xln_d = nc.dram_tensor("xln_d", [S, D], F32, kind="Internal").ap()
    sorted_d = nc.dram_tensor("sorted_d", [NSORT1 * 128, D], F32, kind="Internal").ap()
    hsort_d = nc.dram_tensor("hsort_d", [NSORT1 * 128, D], F32, kind="Internal").ap()
    slots_d = nc.dram_tensor("slots_d", [1, S], F32, kind="Internal").ap()
    x2_d = nc.dram_tensor("x2_d", [NOWN * 128, D], F32, kind=scratch_kind).ap()
    weh_d = nc.dram_tensor("weh_d", [E, 128, 8 * D], BF16, kind="Internal").ap()
    wel_d = nc.dram_tensor("wel_d", [E, 128, 8 * D], BF16, kind="Internal").ap()
    wsp_d = {}
    for w in ("wq", "wk", "wv"):
        wsp_d[w + "h"] = nc.dram_tensor(w + "h_d", [128, 8 * D], BF16, kind="Internal").ap()
        wsp_d[w + "l"] = nc.dram_tensor(w + "l_d", [128, 8 * D], BF16, kind="Internal").ap()
    kvq_kind = "ExternalInput" if p2_input else scratch_kind
    k_d = nc.dram_tensor("k_d", [D, S], F32, kind=kvq_kind).ap()
    q_d = nc.dram_tensor("q_d", [D, NOWN * 128], F32, kind=kvq_kind).ap()
    v_d = nc.dram_tensor("v_d", [NPAIR, S, 130], F32, kind=kvq_kind).ap()
    attn_d = nc.dram_tensor("attn_d", [128, NPAIR * 2048], F32,
                            kind=("ExternalOutput" if dev else "Internal")).ap()

    ident32 = nc.inline_tensor(np.eye(128, dtype=np.float32), name="ident32").ap()
    desc_np = np.broadcast_to((D - np.arange(D)).astype(np.float32), (128, D)).copy()
    desc_c = nc.inline_tensor(desc_np, name="desc_c").ap()
    mod8_np = np.broadcast_to((np.arange(D) % 8).astype(np.float32), (128, D)).copy()
    mod8_c = nc.inline_tensor(mod8_np, name="mod8_c").ap()
    sconst_np = np.stack([np.full(128, LN_EPS, np.float32),
                          np.full(128, -EXP_SHIFT, np.float32)], axis=1).copy()
    sconst_c = nc.inline_tensor(sconst_np, name="sconst_c").ap()
    ones64_c = nc.inline_tensor(np.ones((1, 64), np.float32), name="ones64_c").ap()
    iota8_c = nc.inline_tensor(
        np.broadcast_to(np.arange(8, dtype=np.float32), (128, 8)).copy(), name="iota8_c").ap()
    ones8_c = nc.inline_tensor(np.ones((8, 1), np.float32), name="ones8_c").ap()
    base1_c = nc.inline_tensor((np.arange(8, dtype=np.float32) * CAP1).reshape(8, 1).copy(),
                               name="base1_c").ap()
    base2_c = nc.inline_tensor((np.arange(8, dtype=np.float32) * CAP2).reshape(8, 1).copy(),
                               name="base2_c").ap()

    A = nc.alloc_sbuf_tensor
    sconst = A("sconst", [128, 2], F32)
    desc_sb = A("desc_sb", [128, D], F32)
    mod8_sb = A("mod8_sb", [128, D], F32)
    id32 = A("id32", [128, 128], F32)
    maskc = A("maskc", [128, 4 * 256], F32)
    ones64 = A("ones64", [1, 64], F32)
    bg_sb = A("bg_sb", [128, D], F32)
    wgh = A("wgh", [128, 8 * D], BF16)
    wgl = A("wgl", [128, 8 * D], BF16)
    wgq = A("wgq", [128, 8 * D], BF16)
    iota8 = A("iota8", [128, 8], F32)
    ones8 = A("ones8", [8, 1], F32)
    base1 = A("base1", [8, 1], F32)
    base2 = A("base2", [8, 1], F32)
    NSLOT = 2
    msum = A("msum", [128, NSLOT], F32)
    negm = A("negm", [128, NSLOT], F32)
    vsum = A("vsum", [128, NSLOT], F32)
    lnv = A("lnv", [128, NSLOT], F32)
    rstd = A("rstd", [128, NSLOT], F32)
    mxv = A("mxv", [128, 1], F32)
    negmax = A("negmax", [128, 1], F32)
    rmax = A("rmax", [128, 1], F32)
    evec = A("evec", [128, 1], F32)
    sumexp = A("sumexp", [128, 1], F32)
    pd = A("pd", [128, 1], F32)
    phalf = A("phalf", [128, 1], F32)
    gs = [A(f"gs{i}", [128, E], F32) for i in range(G1)]

    PS = [nc.alloc_psum_tensor(f"ps{i}", [128, 1024], F32) for i in range(4)]

    sch = Sch(nc)
    eps_ap = sconst[:, 0:1]
    negC_ap = sconst[:, 1:2]

    sch.dma("sync", "cst", id32[:], ident32, writes=[("id32",)])
    sch.dma("sync", "cst", sconst[:], sconst_c, writes=[("sconst",)])
    sch.dma("sync", "cst", desc_sb[:], desc_c, writes=[("desc",)])
    sch.dma("sync", "cst", mod8_sb[:], mod8_c, writes=[("mod8",)])
    sch.dma("sync", "cst", bg_sb[:], bg_b, writes=[("bg",)])
    sch.dma("sync", "cst", maskc[:], maskc_in, writes=[("maskc",)])
    sch.dma("sync", "cst", ones64[:], ones64_c, writes=[("ones64",)])
    sch.dma("sync", "cst", iota8[:], iota8_c, writes=[("iota8",)])
    sch.dma("sync", "cst", ones8[:], ones8_c, writes=[("ones8",)])
    sch.dma("sync", "cst", base1[:], base1_c, writes=[("base1",)])
    sch.dma("sync", "cst", base2[:], base2_c, writes=[("base2",)])

    p0 = contextlib.ExitStack()
    stage32 = p0.enter_context(nc.sbuf_tensor("stage32", [128, 8 * D], F32))
    sph = p0.enter_context(nc.sbuf_tensor("sph", [128, 8 * D], BF16))
    spl = p0.enter_context(nc.sbuf_tensor("spl", [128, 8 * D], BF16))

    def split_to(dram_src_view, hi_buf, lo_buf):
        sch.dma("sync", "wld", stage32[:].rearrange("p (c d) -> p c d", c=8),
                dram_src_view, writes=[("stage32",)])
        sch.op("vector", lambda e, h=hi_buf: e.tensor_copy(out=h[:], in_=stage32[:]),
               reads=[("stage32",)], writes=[("sph",)])
        sch.op("vector", lambda e, h=hi_buf, l=lo_buf: e.tensor_sub(
               out=l[:], in0=stage32[:], in1=h[:]),
               reads=[("stage32",), ("sph",)], writes=[("spl",)])

    wg_view = wg_in.rearrange("(c p) d -> p c d", p=128)
    sch.dma("sync", "wld", stage32[:], wg_view, writes=[("stage32",)])
    sch.op("vector", lambda e: e.tensor_copy(out=wgh[:], in_=stage32[:]),
           reads=[("stage32",)], writes=[("wg", "h")])
    sch.op("vector", lambda e: e.tensor_sub(out=wgl[:], in0=stage32[:], in1=wgh[:]),
           reads=[("stage32",), ("wg", "h")], writes=[("wg", "l")])
    sch.op("vector", lambda e: e.tensor_sub(out=sph[:].bitcast(F32),
               in0=stage32[:, 0:4096], in1=wgh[:, 0:4096]),
           reads=[("stage32",), ("wg", "h")], writes=[("sph",)])
    sch.op("vector", lambda e: e.tensor_sub(out=spl[:].bitcast(F32),
               in0=stage32[:, 4096:8192], in1=wgh[:, 4096:8192]),
           reads=[("stage32",), ("wg", "h")], writes=[("spl",)])
    sch.op("vector", lambda e: e.tensor_sub(out=wgq[:, 0:4096],
               in0=sph[:].bitcast(F32), in1=wgl[:, 0:4096]),
           reads=[("sph",), ("wg", "l")], writes=[("wg", "q")])
    sch.op("vector", lambda e: e.tensor_sub(out=wgq[:, 4096:8192],
               in0=spl[:].bitcast(F32), in1=wgl[:, 4096:8192]),
           reads=[("spl",), ("wg", "l")], writes=[("wg", "q")])
    for e_i in range(E):
        split_to(we_in[e_i].rearrange("(c p) d -> p c d", p=128), sph, spl)
        sch.dma("gpsimd", "wesph", weh_d[e_i], sph[:], reads=[("sph",)],
                writes=[("weh_d", e_i)])
        sch.dma("gpsimd", "wespl", wel_d[e_i], spl[:], reads=[("spl",)],
                writes=[("wel_d", e_i)])
    for w, w_inp in (("wq", wq_in), ("wk", wk_in), ("wv", wv_in)):
        split_to(w_inp.rearrange("(c p) d -> p c d", p=128), sph, spl)
        sch.dma("gpsimd", "wesph", wsp_d[w + "h"], sph[:], reads=[("sph",)],
                writes=[("wspd", w, "h")])
        sch.dma("gpsimd", "wespl", wsp_d[w + "l"], spl[:], reads=[("spl",)],
                writes=[("wspd", w, "l")])
    zz0 = p0.enter_context(nc.sbuf_tensor("zz0", [128, D], F32))
    sch.op("vector", lambda e: e.memset(zz0[:], 0.0), writes=[("zz0",)])
    for s_i in range(NSORT1):
        sch.dma("gpsimd", "zinit", sorted_d[s_i * 128:(s_i + 1) * 128, :], zz0[:],
                reads=[("zz0",)])
    p0.close()
    sch.phase()

    ln_ctr = [0]

    def emit_ln(src_ap, src_keys, dst_ap, dst_key, sq_buf, sq_key):
        sl = ln_ctr[0] % NSLOT
        ln_ctr[0] += 1
        ms, nm = msum[:, sl:sl + 1], negm[:, sl:sl + 1]
        vv, lv, rs = vsum[:, sl:sl + 1], lnv[:, sl:sl + 1], rstd[:, sl:sl + 1]
        K = ("lnst", sl)
        sch.op("vector", lambda e: e.reduce_sum(out=ms, in_=src_ap, axis=AX),
               reads=list(src_keys), writes=[K + ("ms",)], small=True)
        sch.op("vector", lambda e: e.tensor_scalar_mul(out=nm, in0=ms, scalar1=-1.0 / D),
               reads=[K + ("ms",)], writes=[K + ("nm",)], small=True)
        sch.op("scalar", lambda e: e.activation(out=sq_buf[:], in_=src_ap,
                                                func=AF.Square, bias=nm, accum_out=vv),
               reads=list(src_keys) + [K + ("nm",), ("sconst",)],
               writes=[sq_key, K + ("vv",)], small=True)
        sch.op("scalar", lambda e: e.activation(out=lv, in_=vv, func=AF.Ln,
                                                bias=eps_ap, scale=1.0 / D),
               reads=[K + ("vv",), ("sconst",)], writes=[K + ("lv",)], small=True)
        sch.op("scalar", lambda e: e.activation(out=rs, in_=lv, func=AF.Exp, scale=-0.5),
               reads=[K + ("lv",)], writes=[K + ("rs",)], small=True)
        sch.op("vector", lambda e: e.tensor_scalar(out=dst_ap, in0=src_ap, scalar1=nm,
                                                   scalar2=rs, op0=ALU.add, op1=ALU.mult),
               reads=list(src_keys) + [K + ("nm",), K + ("rs",)], writes=[dst_key])

    def emit_tsplit(xln_ap, xln_key, ps, ps_key, hi_ap, lo_ap, hi_key, lo_key):
        for c in range(8):
            sch.op("tensor",
                   lambda e, c=c: e.transpose(out=ps[:, c * 128:(c + 1) * 128],
                                              in_=xln_ap[:, c * 128:(c + 1) * 128],
                                              identity=id32[:]),
                   reads=[xln_key, ("id32",)],
                   writes=([ps_key] if c in (0, 7) else []))
        sch.op("scalar", lambda e: e.activation(out=hi_ap, in_=ps[:], func=AF.Copy),
               reads=[ps_key], writes=[hi_key])
        sch.op("vector", lambda e: e.tensor_sub(out=lo_ap, in0=ps[:], in1=hi_ap),
               reads=[ps_key, hi_key], writes=[lo_key])

    def emit_mm3(ps, ps_key, lhs_pairs, rhs_of, read_keys, nhalves=2, hw=512):
        for n in range(nhalves):
            plan = []
            for c, (xh, xl) in enumerate(lhs_pairs):
                rh = rhs_of(c, n, "h")
                rl = rhs_of(c, n, "l")
                plan += [(xh, rh), (xh, rl), (xl, rh)]
            for i, (xs, ws) in enumerate(plan):
                st, sp = (i == 0), (i == len(plan) - 1)
                sch.op("tensor",
                       lambda e, xs=xs, ws=ws, st=st, sp=sp, n=n:
                       e.matmul(ps[:, n * hw:(n + 1) * hw], xs, ws, start=st, stop=sp),
                       reads=list(read_keys),
                       writes=([ps_key] if (st or sp) else []))

    def emit_ffn(tile_srcs, tile_dsts, phase_tag):
        fs = contextlib.ExitStack()
        weh = [fs.enter_context(nc.sbuf_tensor(f"{phase_tag}weh{i}", [128, 8 * D], BF16))
               for i in range(2)]
        wel = [fs.enter_context(nc.sbuf_tensor(f"{phase_tag}wel{i}", [128, 8 * D], BF16))
               for i in range(2)]
        xt = [fs.enter_context(nc.sbuf_tensor(f"{phase_tag}xt{i}", [128, D], F32))
              for i in range(2)]
        xln = [fs.enter_context(nc.sbuf_tensor(f"{phase_tag}xln{i}", [128, D], F32))
               for i in range(2)]
        xfm_h = fs.enter_context(nc.sbuf_tensor(f"{phase_tag}xfh", [128, G1 * D], BF16))
        xfm_l = fs.enter_context(nc.sbuf_tensor(f"{phase_tag}xfl", [128, G1 * D], BF16))
        sq_scr = fs.enter_context(nc.sbuf_tensor(f"{phase_tag}sq", [128, D], BF16))
        logits = fs.enter_context(nc.sbuf_tensor(f"{phase_tag}lg", [128, D], F32))
        descr = fs.enter_context(nc.sbuf_tensor(f"{phase_tag}dsc", [128, D], F32))
        tmp_relu = [fs.enter_context(nc.sbuf_tensor(f"{phase_tag}tr{i}", [128, D], F32))
                    for i in range(2)]
        xacc = [fs.enter_context(nc.sbuf_tensor(f"{phase_tag}xa{i}", [128, D], F32))
                for i in range(G1)]
        be2 = [fs.enter_context(nc.sbuf_tensor(f"{phase_tag}be{i}", [128, D], F32))
               for i in range(2)]
        ps_t = PS[0]
        ps_h = [PS[2], PS[3]]

        nt = len(tile_srcs)
        ngroups = (nt + G1 - 1) // G1
        for g in range(ngroups):
            idxs = list(range(g * G1, min((g + 1) * G1, nt)))
            for i, t in enumerate(idxs):
                bs = i % 2
                sch.dma("sync", f"xld{bs}", xt[bs][:], tile_srcs[t],
                        writes=[("xt", bs)])
                emit_ln(xt[bs][:], [("xt", bs)], xln[bs][:], ("xln", bs),
                        sq_scr, ("sq",))
                hi_ap = xfm_h[:, i * 1024:(i + 1) * 1024]
                lo_ap = xfm_l[:, i * 1024:(i + 1) * 1024]
                hk, lk = ("xfm", i, "h"), ("xfm", i, "l")
                emit_tsplit(xln[bs][:], ("xln", bs), ps_t, ("pst",), hi_ap, lo_ap, hk, lk)
                sch.op("vector", lambda e, i=i, bs=bs: e.tensor_copy(out=xacc[i][:],
                                                                     in_=xt[bs][:]),
                       reads=[("xt", bs)], writes=[("xacc", i)])
                lhs_pairs = [(xfm_h[:, i * 1024 + c * 128: i * 1024 + (c + 1) * 128],
                              xfm_l[:, i * 1024 + c * 128: i * 1024 + (c + 1) * 128])
                             for c in range(8)]
                emit_mm3(ps_g, ("psg",), lhs_pairs,
                         lambda c, n, which: (wgh if which == "h" else wgl)[
                             :, c * 1024 + n * 512: c * 1024 + n * 512 + 512],
                         [hk, lk, ("wg", "h"), ("wg", "l")])
                sch.op("vector", lambda e: e.tensor_add(out=logits[:], in0=ps_g[:],
                                                        in1=bg_sb[:]),
                       reads=[("psg",), ("bg",)], writes=[("lgt",)])
                sch.op("vector", lambda e: e.reduce_max(out=mxv[:], in_=logits[:],
                                                        axis=AX),
                       reads=[("lgt",)], writes=[("mxv",)], small=True)
                sch.op("vector", lambda e: e.tensor_scalar_mul(out=negmax[:], in0=mxv[:],
                                                               scalar1=-1.0),
                       reads=[("mxv",)], writes=[("negmax",)], small=True)
                sch.op("vector", lambda e: e.scalar_tensor_tensor(
                       out=descr[:], in0=logits[:], scalar=mxv[:], in1=desc_sb[:],
                       op0=ALU.is_equal, op1=ALU.mult),
                       reads=[("lgt",), ("mxv",), ("desc",)], writes=[("dscr",)])
                sch.op("vector", lambda e: e.reduce_max(out=rmax[:], in_=descr[:],
                                                        axis=AX),
                       reads=[("dscr",)], writes=[("rmax",)], small=True)
                sch.op("vector", lambda e: e.scalar_tensor_tensor(
                       out=descr[:], in0=descr[:], scalar=rmax[:], in1=mod8_sb[:],
                       op0=ALU.is_equal, op1=ALU.mult),
                       reads=[("dscr",), ("rmax",), ("mod8",)], writes=[("dscr",)])
                sch.op("vector", lambda e: e.reduce_sum(out=evec[:], in_=descr[:],
                                                        axis=AX),
                       reads=[("dscr",)], writes=[("evec",)], small=True)
                sch.op("scalar", lambda e: e.activation(out=sq_scr[:], in_=logits[:],
                                                        func=AF.Exp, bias=negmax[:],
                                                        accum_out=sumexp[:]),
                       reads=[("lgt",), ("negmax",)], writes=[("sq",), ("sume",)],
                       small=True)
                sch.op("vector", lambda e: e.reciprocal(out=pd[:], in_=sumexp[:]),
                       reads=[("sume",)], writes=[("pd",)], small=True)
                sch.op("vector", lambda e: e.tensor_scalar_mul(out=phalf[:], in0=pd[:],
                                                               scalar1=0.5),
                       reads=[("pd",)], writes=[("ph",)], small=True)
                for e_i in range(E):
                    sch.op("vector", lambda e, e_i=e_i, i=i: e.scalar_tensor_tensor(
                           out=gs[i][:, e_i:e_i + 1], in0=evec[:], scalar=float(e_i),
                           in1=phalf[:], op0=ALU.is_equal, op1=ALU.mult),
                           reads=[("evec",), ("ph",)], writes=[("gs", i, e_i)],
                           small=True)
            for e_i in range(E):
                wb = e_i % 2
                sch.dma("sync", f"weld{wb}h", weh[wb][:], weh_d[e_i],
                        reads=[("weh_d", e_i)], writes=[("weh", wb)])
                sch.dma("sync", f"weld{wb}l", wel[wb][:], wel_d[e_i],
                        reads=[("wel_d", e_i)], writes=[("wel", wb)])
                sch.dma("sync", f"beld{wb}", be2[wb][:],
                        be_b[:, e_i * D:(e_i + 1) * D], writes=[("be2", wb)])
                for i, t in enumerate(idxs):
                    hb = ps_h[i % 2]
                    hkey = ("psh", i % 2)
                    lhs_pairs = [(xfm_h[:, i * 1024 + c * 128: i * 1024 + (c + 1) * 128],
                                  xfm_l[:, i * 1024 + c * 128: i * 1024 + (c + 1) * 128])
                                 for c in range(8)]
                    emit_mm3(hb, hkey, lhs_pairs,
                             lambda c, n, which, wb=wb: (weh[wb] if which == "h"
                                                         else wel[wb])[
                                 :, c * 1024 + n * 512: c * 1024 + n * 512 + 512],
                             [("xfm", i, "h"), ("xfm", i, "l"),
                              ("weh", wb), ("wel", wb)])
                    sch.op("vector", lambda e, hb=hb, wb=wb: e.tensor_add(
                           out=hb[:], in0=hb[:], in1=be2[wb][:]),
                           reads=[hkey, ("be2", wb)], writes=[hkey])
                    tr = tmp_relu[i % 2]
                    sch.op("scalar", lambda e, hb=hb, i=i, e_i=e_i, tr=tr: e.activation(
                           out=tr[:], in_=hb[:], func=AF.Relu,
                           scale=gs[i][:, e_i:e_i + 1]),
                           reads=[hkey, ("gs", i, e_i)], writes=[("tmpr", i % 2)])
                    sch.op("vector", lambda e, i=i: e.tensor_add(
                           out=xacc[i][:], in0=xacc[i][:], in1=tmp_relu[i % 2][:]),
                           reads=[("tmpr", i % 2), ("xacc", i)], writes=[("xacc", i)])
            for i, t in enumerate(idxs):
                kind, o = tile_dsts[t]
                if kind == "x1":
                    sch.dma("gpsimd", f"x1sp{i}", x1_d[o * 128:(o + 1) * 128, :],
                            xacc[i][:], reads=[("xacc", i)])
                else:
                    sch.dma("gpsimd", f"osp{i}", out_d[o * 128:(o + 1) * 128, :],
                            xacc[i][:], reads=[("xacc", i)])
        fs.close()
        sch.phase()

    def emit_ffn_sparse(tile_srcs, tile_dsts, tag, cap, base_ap, base_key):
        ntiles = len(tile_srcs)
        ntok = ntiles * 128
        nsort = 8 * cap // 128
        tpe = cap // 128
        fs = contextlib.ExitStack()
        T = nc.sbuf_tensor
        slotcf = [fs.enter_context(T(f"{tag}slc{i}", [128, 1], F32)) for i in range(8)]
        slotu = fs.enter_context(T(f"{tag}slu", [128, ntiles], mybir.dt.uint32))
        phbuf = fs.enter_context(T(f"{tag}phb", [128, ntiles], F32))
        xfh = [fs.enter_context(T(f"{tag}xfh{i}", [128, D], BF16)) for i in range(2)]
        xfm = [fs.enter_context(T(f"{tag}xfm{i}", [128, D], BF16)) for i in range(2)]
        sx = [fs.enter_context(T(f"{tag}sx{i}", [128, D], F32)) for i in range(4)]
        fsa = contextlib.ExitStack()
        ohrows = fsa.enter_context(T(f"{tag}ohr", [8, ntok], BF16))
        rnk = fsa.enter_context(T(f"{tag}rnk", [8, ntok], F32))
        rnk2 = fsa.enter_context(T(f"{tag}rnk2", [8, ntok], F32))
        slotrow = fsa.enter_context(T(f"{tag}slr", [1, ntok], F32))
        xt = [fsa.enter_context(T(f"{tag}xt{i}", [128, D], F32)) for i in range(2)]
        xln = [fsa.enter_context(T(f"{tag}xln{i}", [128, D], F32)) for i in range(2)]
        r1b = [fsa.enter_context(T(f"{tag}r1{i}", [128, D], F32)) for i in range(2)]
        xfq = [fsa.enter_context(T(f"{tag}xfq{i}", [128, D], BF16)) for i in range(2)]
        sq_scr = fsa.enter_context(T(f"{tag}sq", [128, D], BF16))
        logits = fsa.enter_context(T(f"{tag}lg", [128, D], F32))
        descr = fsa.enter_context(T(f"{tag}dsc", [128, D], F32))
        ohb = [fsa.enter_context(T(f"{tag}oh{i}", [128, 8], F32)) for i in range(2)]

        def emit_oht(tt_):
            ob_ = tt_ % 2
            sch.op("tensor", lambda e, ob_=ob_: e.transpose(
                   out=PS[3][0:8, 0:128], in_=ohb[ob_][:], identity=id32[:]),
                   reads=[("ohb", ob_), ("id32",)], writes=[("ps3",)])
            sch.op("vector", lambda e, tt_=tt_: e.tensor_copy(
                   out=ohrows[:, tt_ * 128:(tt_ + 1) * 128], in_=PS[3][0:8, 0:128]),
                   reads=[("ps3",)], writes=[("ohr", tt_)], small=True)
        ps_t, ps_g = PS[0], PS[1]

        def slots_rank(t0, t1):
            o0, o1 = t0 * 128, t1 * 128
            ohr_keys = [("ohr", t) for t in range(t0, t1)]
            sch.op("vector", lambda e: e.tensor_tensor_scan(
                   out=rnk[:, o0:o1], data0=ohrows[:, o0:o1],
                   data1=ohrows[:, o0:o1], initial=0.0,
                   op0=ALU.add, op1=ALU.bypass),
                   reads=ohr_keys + [("rnk",)], writes=[("rnk",)])
            if t0 > 0:
                sch.op("vector", lambda e: e.tensor_scalar_add(
                       out=rnk[:, o0:o1], in0=rnk[:, o0:o1],
                       scalar1=rnk[:, o0 - 1:o0]),
                       reads=[("rnk",)], writes=[("rnk",)], small=True)
            sch.op("vector", lambda e: e.tensor_sub(out=rnk2[:, o0:o1],
                                                    in0=rnk[:, o0:o1],
                                                    in1=ohrows[:, o0:o1]),
                   reads=[("rnk",)] + ohr_keys, writes=[("rnk2", t0)],
                   small=True)
            sch.op("vector", lambda e: e.tensor_scalar_min(out=rnk2[:, o0:o1],
                                                           in0=rnk2[:, o0:o1],
                                                           scalar1=float(cap - 1)),
                   reads=[("rnk2", t0)], writes=[("rnk2", t0)], small=True)
            sch.op("vector", lambda e: e.tensor_scalar_add(out=rnk2[:, o0:o1],
                                                           in0=rnk2[:, o0:o1],
                                                           scalar1=base_ap),
                   reads=[("rnk2", t0), base_key], writes=[("rnk2", t0)],
                   small=True)
            sch.op("vector", lambda e: e.tensor_mul(out=rnk2[:, o0:o1],
                                                    in0=rnk2[:, o0:o1],
                                                    in1=ohrows[:, o0:o1]),
                   reads=[("rnk2", t0)] + ohr_keys, writes=[("rnk2", t0)],
                   small=True)

        def slots_emit(t0, t1):
            o0, o1 = t0 * 128, t1 * 128
            for c in range(o0 // 512, o1 // 512):
                sch.op("tensor", lambda e, c=c: e.matmul(
                       PS[3][0:1, 0:512], ones8[:], rnk2[:, c * 512:(c + 1) * 512],
                       start=True, stop=True),
                       reads=[("rnk2", t0), ("ones8",)], writes=[("ps3",)])
                sch.op("vector", lambda e, c=c: e.tensor_copy(
                       out=slotrow[:, c * 512:(c + 1) * 512], in_=PS[3][0:1, 0:512]),
                       reads=[("ps3",)], writes=[("slr", t0)], small=True)
            sch.dma("gpsimd", f"{tag}slsp", slots_d[:, o0:o1], slotrow[:, o0:o1],
                    reads=[("slr", t0)], writes=[("slots_d", t0)])
            for t in range(t0, t1):
                bs = t % 8
                sch.dma("sync", f"{tag}slld{t % 4}", slotcf[bs][:],
                        slots_d[:, t * 128:(t + 1) * 128].rearrange("o p -> p o"),
                        reads=[("slots_d", t0)], writes=[("slcf", bs)])
            for t in range(t0, t1):
                bs = t % 8
                sch.op("vector", lambda e, t=t, bs=bs: e.tensor_copy(
                       out=slotu[:, t:t + 1], in_=slotcf[bs][:]),
                       reads=[("slcf", bs)], writes=[("slu", t)], small=True)
            for t in range(t0, t1):
                bs = t % 4
                sch.dma("sync", f"{tag}xrl{bs}", sx[bs][:],
                        xln_d[t * 128:(t + 1) * 128, :],
                        reads=[("xlnd", t)], writes=[("sx", bs)])
                sch._mkop("gpsimd",
                          lambda e, bs=bs, t=t: e.indirect_dma_start(
                              out=sorted_d,
                              out_offset=IndirectOffsetOnAxis(ap=slotu[:, t:t + 1],
                                                              axis=0),
                              in_=sx[bs][:], in_offset=None),
                          [("sx", bs), ("slu", t)], [("sorted", t)], (), False,
                          True, f"{tag}sct{bs}")

        def prep_ln(t):
            bs = t % 2
            sch.dma("sync", f"{tag}xld{bs}", xt[bs][:], tile_srcs[t],
                    writes=[("xt", bs)])
            emit_ln(xt[bs][:], [("xt", bs)], xln[bs][:], ("xln", bs),
                    sq_scr, ("sq",))
            sch.dma("gpsimd", f"{tag}xlsp{bs}", xln_d[t * 128:(t + 1) * 128, :],
                    xln[bs][:], reads=[("xln", bs)], writes=[("xlnd", t)])

        def prep_tp(t):
            bs = t % 2
            pt_b = PS[0]
            ptkey = ("pst",)
            for c in range(8):
                sch.op("tensor",
                       lambda e, c=c, bs=bs, pt_b=pt_b: e.transpose(
                           out=pt_b[:, c * 128:(c + 1) * 128],
                           in_=xln[bs][:, c * 128:(c + 1) * 128], identity=id32[:]),
                       reads=[("xln", bs), ("id32",)],
                       writes=([ptkey] if c in (0, 7) else []))
            sch.op("scalar", lambda e, bs=bs, pt_b=pt_b: e.activation(
                   out=xfh[bs][:], in_=pt_b[:], func=AF.Copy),
                   reads=[ptkey], writes=[("xfh", bs)])
            sch.op("vector", lambda e, bs=bs, pt_b=pt_b: e.tensor_sub(
                   out=r1b[bs][:], in0=pt_b[:], in1=xfh[bs][:]),
                   reads=[ptkey, ("xfh", bs)], writes=[("r1", bs)])
            sch.op("scalar", lambda e, bs=bs: e.activation(out=xfm[bs][:], in_=r1b[bs][:],
                                                           func=AF.Copy),
                   reads=[("r1", bs)], writes=[("xfm", bs)])
            sch.op("vector", lambda e, bs=bs: e.tensor_sub(out=xfq[bs][:], in0=r1b[bs][:],
                                                           in1=xfm[bs][:]),
                   reads=[("r1", bs), ("xfm", bs)], writes=[("xfq", bs)])

        prep_ln(0)
        prep_ln(1)
        prep_tp(0)
        for t in range(ntiles):
            bs = t % 2
            if t + 2 < ntiles:
                prep_ln(t + 2)
            if t + 1 < ntiles:
                prep_tp(t + 1)
            ps_g = PS[1] if bs == 0 else PS[2]
            gkey = ("psg", bs)
            for n in range(2):
                plan = []
                for c in range(8):
                    xh = xfh[bs][:, c * 128:(c + 1) * 128]
                    xm = xfm[bs][:, c * 128:(c + 1) * 128]
                    xq = xfq[bs][:, c * 128:(c + 1) * 128]
                    wh = wgh[:, c * 1024 + n * 512: c * 1024 + n * 512 + 512]
                    wl = wgl[:, c * 1024 + n * 512: c * 1024 + n * 512 + 512]
                    wq = wgq[:, c * 1024 + n * 512: c * 1024 + n * 512 + 512]
                    plan += [(xh, wh), (xh, wl), (xm, wh), (xm, wl), (xh, wq), (xq, wh)]
                for i, (xs, ws) in enumerate(plan):
                    st, sp = (i == 0), (i == len(plan) - 1)
                    sch.op("tensor",
                           lambda e, xs=xs, ws=ws, st=st, sp=sp, n=n, ps_g=ps_g:
                           e.matmul(ps_g[:, n * 512:(n + 1) * 512], xs, ws,
                                    start=st, stop=sp),
                           reads=[("xfh", bs), ("xfm", bs), ("xfq", bs),
                                  ("wg", "h"), ("wg", "l"), ("wg", "q")],
                           writes=([gkey] if (st or sp) else []))
            sch.op("vector", lambda e, ps_g=ps_g: e.tensor_add(out=logits[:],
                                                               in0=ps_g[:],
                                                               in1=bg_sb[:]),
                   reads=[gkey, ("bg",)], writes=[("lgt",)])
            sch.op("vector", lambda e: e.reduce_max(out=mxv[:], in_=logits[:], axis=AX),
                   reads=[("lgt",)], writes=[("mxv",)], small=True)
            sch.op("vector", lambda e: e.tensor_scalar_mul(out=negmax[:], in0=mxv[:],
                                                           scalar1=-1.0),
                   reads=[("mxv",)], writes=[("negmax",)], small=True)
            sch.op("vector", lambda e: e.scalar_tensor_tensor(
                   out=descr[:], in0=logits[:], scalar=mxv[:], in1=desc_sb[:],
                   op0=ALU.is_equal, op1=ALU.mult),
                   reads=[("lgt",), ("mxv",), ("desc",)], writes=[("dscr",)])
            sch.op("vector", lambda e: e.reduce_max(out=rmax[:], in_=descr[:], axis=AX),
                   reads=[("dscr",)], writes=[("rmax",)], small=True)
            sch.op("vector", lambda e: e.scalar_tensor_tensor(
                   out=descr[:], in0=descr[:], scalar=rmax[:], in1=mod8_sb[:],
                   op0=ALU.is_equal, op1=ALU.mult),
                   reads=[("dscr",), ("rmax",), ("mod8",)], writes=[("dscr",)])
            sch.op("vector", lambda e: e.reduce_sum(out=evec[:], in_=descr[:], axis=AX),
                   reads=[("dscr",)], writes=[("evec",)], small=True)
            sch.op("scalar", lambda e: e.activation(out=sq_scr[:], in_=logits[:],
                                                    func=AF.Exp, bias=negmax[:],
                                                    accum_out=sumexp[:]),
                   reads=[("lgt",), ("negmax",)], writes=[("sq",), ("sume",)],
                   small=True)
            sch.op("vector", lambda e: e.reciprocal(out=pd[:], in_=sumexp[:]),
                   reads=[("sume",)], writes=[("pd",)], small=True)
            sch.op("vector", lambda e, t=t: e.tensor_scalar_mul(
                   out=phbuf[:, t:t + 1], in0=pd[:], scalar1=0.5),
                   reads=[("pd",)], writes=[("phb", t)], small=True)
            ob = t % 2
            sch.op("vector", lambda e, ob=ob: e.tensor_single_scalar(
                   out=ohb[ob][:], in_=iota8[:], scalar=evec[:], op=ALU.is_equal),
                   reads=[("evec",), ("iota8",)], writes=[("ohb", ob)], small=True)
            if t > 0:
                emit_oht(t - 1)
            if t == ntiles - 1:
                emit_oht(t)
            qsz = ntiles // 4
            if t in (qsz, 2 * qsz, 3 * qsz):
                slots_rank(t - qsz, t)
                slots_emit(t - qsz, t)

        slots_rank(3 * (ntiles // 4), ntiles)
        slots_emit(3 * (ntiles // 4), ntiles)
        fsa.close()
        sch.phase()
        fsb = contextlib.ExitStack()
        weh = [fsb.enter_context(T(f"{tag}weh{i}", [128, 8 * D], BF16)) for i in range(2)]
        wel = [fsb.enter_context(T(f"{tag}wel{i}", [128, 8 * D], BF16)) for i in range(2)]
        be2 = [fsb.enter_context(T(f"{tag}be{i}", [128, D], F32)) for i in range(2)]
        hout = [fsb.enter_context(T(f"{tag}ho{i}", [128, D], F32)) for i in range(2)]
        sorted_keys = [("sorted", t) for t in range(ntiles)]
        def prep_b(s_i):
            e_i = s_i // tpe
            wb = e_i % 2
            sbs = s_i % 2
            pt_b = PS[0] if sbs == 0 else PS[1]
            ptkey = ("pst", sbs)
            if s_i % tpe == 0:
                sch.dma("sync", f"weld{wb}h", weh[wb][:], weh_d[e_i],
                        reads=[("weh_d", e_i)], writes=[("weh", wb)])
                sch.dma("sync", f"weld{wb}l", wel[wb][:], wel_d[e_i],
                        reads=[("wel_d", e_i)], writes=[("wel", wb)])
                sch.dma("sync", f"beld{wb}", be2[wb][:],
                        be_b[:, e_i * D:(e_i + 1) * D], writes=[("be2", wb)])
            sch.dma("sync", f"{tag}sxl{sbs}", sx[sbs][:],
                    sorted_d[s_i * 128:(s_i + 1) * 128, :],
                    reads=sorted_keys, writes=[("sx", sbs)])
            for c in range(8):
                sch.op("tensor",
                       lambda e, c=c, sbs=sbs, pt_b=pt_b: e.transpose(
                           out=pt_b[:, c * 128:(c + 1) * 128],
                           in_=sx[sbs][:, c * 128:(c + 1) * 128], identity=id32[:]),
                       reads=[("sx", sbs), ("id32",)],
                       writes=([ptkey] if c in (0, 7) else []))
            sch.op("scalar", lambda e, sbs=sbs, pt_b=pt_b: e.activation(
                   out=xfh[sbs][:], in_=pt_b[:], func=AF.Copy),
                   reads=[ptkey], writes=[("xfh", sbs)])
            sch.op("vector", lambda e, sbs=sbs, pt_b=pt_b: e.tensor_sub(
                   out=xfm[sbs][:], in0=pt_b[:], in1=xfh[sbs][:]),
                   reads=[ptkey, ("xfh", sbs)], writes=[("xfm", sbs)])

        prep_b(0)
        for s_i in range(nsort):
            e_i = s_i // tpe
            wb = e_i % 2
            sbs = s_i % 2
            if s_i + 1 < nsort:
                prep_b(s_i + 1)
            hb = PS[2] if sbs == 0 else PS[3]
            hkey = ("psh0",) if sbs == 0 else ("ps3",)
            for n in range(2):
                plan = []
                for c in range(8):
                    xh = xfh[sbs][:, c * 128:(c + 1) * 128]
                    xm = xfm[sbs][:, c * 128:(c + 1) * 128]
                    wh = weh[wb][:, c * 1024 + n * 512: c * 1024 + n * 512 + 512]
                    wl = wel[wb][:, c * 1024 + n * 512: c * 1024 + n * 512 + 512]
                    plan += [(xh, wh), (xh, wl), (xm, wh)]
                for i, (xs, ws) in enumerate(plan):
                    st, sp = (i == 0), (i == len(plan) - 1)
                    sch.op("tensor",
                           lambda e, xs=xs, ws=ws, st=st, sp=sp, n=n, hb=hb:
                           e.matmul(hb[:, n * 512:(n + 1) * 512], xs, ws,
                                    start=st, stop=sp),
                           reads=[("xfh", sbs), ("xfm", sbs), ("weh", wb),
                                  ("wel", wb)],
                           writes=([hkey] if (st or sp) else []))
            sch.op("vector", lambda e, hb=hb, wb=wb: e.tensor_add(
                   out=hb[:], in0=hb[:], in1=be2[wb][:]),
                   reads=[hkey, ("be2", wb)], writes=[hkey])
            sch.op("scalar", lambda e, hb=hb, sbs=sbs: e.activation(
                   out=hout[sbs][:], in_=hb[:], func=AF.Relu),
                   reads=[hkey], writes=[("ho", sbs)])
            sch.dma("gpsimd", f"{tag}hsp{sbs}", hsort_d[s_i * 128:(s_i + 1) * 128, :],
                    hout[sbs][:], reads=[("ho", sbs)], writes=[("hsd", s_i)])
        fsb.close()
        sch.phase()
        fsc = contextlib.ExitStack()
        ghat = [fsc.enter_context(T(f"{tag}gh{i}", [128, D], F32)) for i in range(2)]
        xres = [fsc.enter_context(T(f"{tag}xr{i}", [128, D], F32)) for i in range(2)]
        xacc = [fsc.enter_context(T(f"{tag}xa{i}", [128, D], F32)) for i in range(2)]
        hsd_keys = [("hsd", s_i) for s_i in range(nsort)]
        for t in range(ntiles):
            bs = t % 2
            tok = sch._mkop("gpsimd",
                            lambda e, bs=bs, t=t: e.indirect_dma_start(
                                out=ghat[bs][:], out_offset=None,
                                in_=hsort_d,
                                in_offset=IndirectOffsetOnAxis(ap=slotu[:, t:t + 1],
                                                               axis=0)),
                            hsd_keys + [("slu", t)], [("gh", bs)], (), False,
                            True, f"{tag}gth{bs}")
            sch.dma("sync", f"{tag}xrs{bs}", xres[bs][:], tile_srcs[t],
                    writes=[("xres", bs)])
            sch.op("vector", lambda e, bs=bs, t=t: e.scalar_tensor_tensor(
                   out=xacc[bs][:], in0=ghat[bs][:], scalar=phbuf[:, t:t + 1],
                   in1=xres[bs][:], op0=ALU.mult, op1=ALU.add),
                   reads=[("gh", bs), ("phb", t), ("xres", bs)],
                   writes=[("xacc", bs)])
            kind, o = tile_dsts[t]
            if kind == "x1":
                sch.dma("gpsimd", f"{tag}x1o{bs}", x1_d[o * 128:(o + 1) * 128, :],
                        xacc[bs][:], reads=[("xacc", bs)])
            else:
                sch.dma("gpsimd", f"{tag}oo{bs}", out_d[o * 128:(o + 1) * 128, :],
                        xacc[bs][:], reads=[("xacc", bs)])
        fsc.close()
        fs.close()
        sch.phase()

    if "p1a" in stages:
        emit_ffn_sparse([x_in[t * 128:(t + 1) * 128, :] for t in range(NT)],
                        [("x1", t) for t in range(NT)], "fa", CAP1, base1[:],
                        ("base1",))

    if "p1b" in stages:
        fs = contextlib.ExitStack()
        wk_h = fs.enter_context(nc.sbuf_tensor("wk_h", [128, 8 * D], BF16))
        wk_l = fs.enter_context(nc.sbuf_tensor("wk_l", [128, 8 * D], BF16))
        wv_h = fs.enter_context(nc.sbuf_tensor("wv_h", [128, 8 * D], BF16))
        wv_l = fs.enter_context(nc.sbuf_tensor("wv_l", [128, 8 * D], BF16))
        wq_h = fs.enter_context(nc.sbuf_tensor("wq_h", [128, 8 * D], BF16))
        wq_l = fs.enter_context(nc.sbuf_tensor("wq_l", [128, 8 * D], BF16))
        xt = [fs.enter_context(nc.sbuf_tensor(f"bxt{i}", [128, D], F32)) for i in range(2)]
        xln = [fs.enter_context(nc.sbuf_tensor(f"bxln{i}", [128, D], F32)) for i in range(2)]
        xfm_h = fs.enter_context(nc.sbuf_tensor("bxfh", [128, 4 * D], BF16))
        xfm_l = fs.enter_context(nc.sbuf_tensor("bxfl", [128, 4 * D], BF16))
        sq_scr = fs.enter_context(nc.sbuf_tensor("bsq", [128, D], BF16))
        vbuf = fs.enter_context(nc.sbuf_tensor("vbuf", [128, 16 * 65], F32))
        ksb = [fs.enter_context(nc.sbuf_tensor(f"ksb{i}", [128, 512], F32)) for i in range(2)]
        qsb = [fs.enter_context(nc.sbuf_tensor(f"qsb{i}", [128, 256], F32)) for i in range(2)]
        for w, hb, lb in (("wk", wk_h, wk_l), ("wv", wv_h, wv_l), ("wq", wq_h, wq_l)):
            sch.dma("sync", "pbw", hb[:], wsp_d[w + "h"], reads=[("wspd", w, "h")],
                    writes=[("pw", w, "h")])
            sch.dma("sync", "pbw", lb[:], wsp_d[w + "l"], reads=[("wspd", w, "l")],
                    writes=[("pw", w, "l")])
        vones = vbuf[:].rearrange("p (j c) -> p j c", c=65)[:, :, 64:65]
        sch.op("vector", lambda e: e.memset(vones, 1.0), writes=[("vones",)])
        ps_t, ps_v = PS[0], PS[1]
        ps_k, ps_q = [PS[2][:, 0:512], PS[2][:, 512:1024]], PS[3]

        for g in range(NT // 4):
            xf_keys = []
            for i in range(4):
                t = 4 * g + i
                bs = i % 2
                ptb = ps_t
                ptk = ("bpst",)
                sch.dma("sync", f"bxld{bs}", xt[bs][:], x1_d[t * 128:(t + 1) * 128, :],
                        writes=[("bxt", bs)])
                emit_ln(xt[bs][:], [("bxt", bs)], xln[bs][:], ("bxln", bs),
                        sq_scr, ("bsq",))
                hi_ap = xfm_h[:, i * 1024:(i + 1) * 1024]
                lo_ap = xfm_l[:, i * 1024:(i + 1) * 1024]
                hk, lk = ("bxfm", i, "h"), ("bxfm", i, "l")
                emit_tsplit(xln[bs][:], ("bxln", bs), ptb, ptk, hi_ap, lo_ap,
                            hk, lk)
                xf_keys.append([hk, lk])
            for i in range(4):
                t = 4 * g + i
                hk, lk = xf_keys[i]
                lhs_pairs = [(xfm_h[:, i * 1024 + c * 128: i * 1024 + (c + 1) * 128],
                              xfm_l[:, i * 1024 + c * 128: i * 1024 + (c + 1) * 128])
                             for c in range(8)]
                emit_mm3(ps_v, ("psv",), lhs_pairs,
                         lambda c, n, which: (wv_h if which == "h" else wv_l)[
                             :, c * 1024 + n * 512: c * 1024 + n * 512 + 512],
                         [hk, lk, ("pw", "wv", "h"), ("pw", "wv", "l")])
                vdst = vbuf[:].rearrange("p (j c) -> p j c", c=65)[:, :, 0:64]
                vsrc = ps_v[:].rearrange("p (j c) -> p j c", c=64)
                sch.op("vector", lambda e, vdst=vdst, vsrc=vsrc: e.tensor_copy(
                       out=vdst, in_=vsrc),
                       reads=[("psv",), ("vones",)], writes=[("vbuf",)])
                for p_i in range(NPAIR):
                    sch.dma("gpsimd", f"vsp{p_i}",
                            v_d[p_i, t * 128:(t + 1) * 128, :],
                            vbuf[:, p_i * 130:(p_i + 1) * 130],
                            reads=[("vbuf",)])
            xv_h = xfm_h[:].rearrange("p (i c k) -> p c i k", i=4, c=8)
            xv_l = xfm_l[:].rearrange("p (i c k) -> p c i k", i=4, c=8)
            for c8 in range(8):
                kb = c8 % 2
                lhs_pairs = [(wk_h[:, c * 1024 + c8 * 128: c * 1024 + c8 * 128 + 128],
                              wk_l[:, c * 1024 + c8 * 128: c * 1024 + c8 * 128 + 128])
                             for c in range(8)]
                emit_mm3(ps_k[kb], ("psk", kb), lhs_pairs,
                         lambda c, n, which: (xv_h if which == "h" else xv_l)[:, c],
                         [("pw", "wk", "h"), ("pw", "wk", "l")] +
                         [k for ks in xf_keys for k in ks], nhalves=1)
                sch.op("scalar", lambda e, kb=kb: e.activation(out=ksb[kb][:],
                                                               in_=ps_k[kb],
                                                               func=AF.Copy),
                       reads=[("psk", kb)], writes=[("ksb", kb)])
                sch.dma("gpsimd", f"ksp{kb}",
                        k_d[c8 * 128:(c8 + 1) * 128, g * 512:(g + 1) * 512],
                        ksb[kb][:], reads=[("ksb", kb)])
            for c8 in range(8):
                qb = c8 % 2
                lhs_pairs = [(wq_h[:, c * 1024 + c8 * 128: c * 1024 + c8 * 128 + 128],
                              wq_l[:, c * 1024 + c8 * 128: c * 1024 + c8 * 128 + 128])
                             for c in range(8)]
                emit_mm3(ps_q[:, 0:256], ("psq",), lhs_pairs,
                         lambda c, n, which: (xv_h if which == "h" else xv_l)[:, c, 0::2, :],
                         [("pw", "wq", "h"), ("pw", "wq", "l"),
                          xf_keys[0][0], xf_keys[0][1], xf_keys[2][0], xf_keys[2][1]],
                         nhalves=1, hw=256)
                sch.op("scalar", lambda e, qb=qb: e.activation(out=qsb[qb][:],
                                                               in_=ps_q[:, 0:256],
                                                               func=AF.Copy),
                       reads=[("psq",)], writes=[("qsb", qb)])
                sch.dma("gpsimd", f"qsp{qb}",
                        q_d[c8 * 128:(c8 + 1) * 128, g * 256:(g + 1) * 256],
                        qsb[qb][:], reads=[("qsb", qb)])
        fs.close()
        sch.phase()


    if "p2" in stages:
        fs = contextlib.ExitStack()
        fs2 = contextlib.ExitStack()
        kpair2 = [fs2.enter_context(nc.sbuf_tensor(f"kpair{i}", [128, S], F32))
                  for i in range(2)]
        qpair2 = [fs2.enter_context(nc.sbuf_tensor(f"qpair{i}", [128, NOWN * 128], F32))
                  for i in range(2)]
        vpair2 = [fs2.enter_context(nc.sbuf_tensor("vpair0", [128, 32 * 130], F32))] * 2
        kh2 = [fs2.enter_context(nc.sbuf_tensor(f"kh{i}", [128, S], BF16))
               for i in range(2)]
        kl2 = [fs2.enter_context(nc.sbuf_tensor(f"kl{i}", [128, S], BF16))
               for i in range(2)]
        qh2 = [fs2.enter_context(nc.sbuf_tensor(f"qh{i}", [128, NOWN * 128], BF16))
               for i in range(2)]
        ql2 = [fs2.enter_context(nc.sbuf_tensor(f"ql{i}", [128, NOWN * 128], BF16))
               for i in range(2)]
        att_st = [fs2.enter_context(nc.sbuf_tensor(f"attst{i}", [128, 256], F32))
                  for i in range(2)]
        pt = [[fs2.enter_context(nc.sbuf_tensor(f"pt{h}_{b}", [128, 512], F32))
               for b in range(2)] for h in range(2)]
        rsum = [fs2.enter_context(nc.sbuf_tensor(f"rsum{i}", [1, 256], F32))
                for i in range(4)]
        recs = [fs2.enter_context(nc.sbuf_tensor(f"recs{i}", [1, 256], F32))
                for i in range(4)]
        tmpo = fs2.enter_context(nc.sbuf_tensor("tmpo", [64, 256], F32))
        def pair_load(p_i):
            pb = p_i % 2
            sch.dma("sync", f"kpld{pb}", kpair2[pb][:],
                    k_d[p_i * 128:(p_i + 1) * 128, :], writes=[("kpair", pb)])
            sch.dma("sync", f"qpld{pb}", qpair2[pb][:],
                    q_d[p_i * 128:(p_i + 1) * 128, :], writes=[("qpair", pb)])

        pair_load(0)
        final_norm = [None]
        for p_i in range(NPAIR):
            pb = p_i % 2
            kpair, qpair, vpair = kpair2[pb], qpair2[pb], vpair2[pb]
            kh, kl, qh, ql = kh2[pb], kl2[pb], qh2[pb], ql2[pb]
            sch.op("vector", lambda e, kh=kh, kpair=kpair: e.tensor_copy(
                   out=kh[:], in_=kpair[:]),
                   reads=[("kpair", pb)], writes=[("kh", pb)])
            sch.op("vector", lambda e, kl=kl, kh=kh, kpair=kpair: e.tensor_sub(
                   out=kl[:], in0=kpair[:], in1=kh[:]),
                   reads=[("kpair", pb), ("kh", pb)], writes=[("kl", pb)])
            sch.op("vector", lambda e, qh=qh, qpair=qpair: e.tensor_copy(
                   out=qh[:], in_=qpair[:]),
                   reads=[("qpair", pb)], writes=[("qh", pb)])
            sch.op("vector", lambda e, ql=ql, qh=qh, qpair=qpair: e.tensor_sub(
                   out=ql[:], in0=qpair[:], in1=qh[:]),
                   reads=[("qpair", pb), ("qh", pb)], writes=[("ql", pb)])
            sch.dma("sync", "vpld",
                    vpair[:].rearrange("pt (m c) -> pt m c", c=130),
                    v_d[p_i].rearrange("(m pt) c -> pt m c", pt=128),
                    writes=[("vpair", 0), ("vpair", 1)])
            if p_i + 1 < NPAIR:
                pair_load(p_i + 1)
            if final_norm[0] is not None:
                final_norm[0]()
            norm_pending = []

            def emit_norm():
                for (hp_, ab_, pw_, pi_, t_rs_) in norm_pending:
                    pokey_ = ("pso", hp_)
                    sl_ = ab_ * 2 + hp_
                    sch.op("tensor", lambda e, sl_=sl_: e.matmul(
                           PS[3][0:64, 0:256], ones64[:], recs[sl_][:],
                           start=True, stop=True),
                           reads=[("recs", sl_), ("ones64",)], writes=[("psbc",)])
                    sch.op("scalar", lambda e, hp=hp_: e.activation(
                           out=tmpo[:], in_=PS[2][0:64, hp * 512:hp * 512 + 256],
                           func=AF.Copy),
                           reads=[pokey_], writes=[("tmpo",)], after=[t_rs_])
                    dst = att_st[ab_][hp_ * 64:(hp_ + 1) * 64, :]
                    sch.op("vector", lambda e, dst=dst: e.tensor_mul(
                           out=dst, in0=tmpo[:], in1=PS[3][0:64, 0:256]),
                           reads=[("tmpo",), ("psbc",)],
                           writes=[("attst", ab_, hp_)])
                if norm_pending:
                    _, ab_, pw_, pi_, _t = norm_pending[0]
                    sch.dma("gpsimd", f"atsp{ab_}",
                            attn_d[:, pi_ * 2048 + pw_ * 256:
                                   pi_ * 2048 + (pw_ + 1) * 256],
                            att_st[ab_][:],
                            reads=[("attst", ab_, 0), ("attst", ab_, 1)],
                            writes=[("afm", pi_, pw_)])
                norm_pending.clear()

            final_norm[0] = emit_norm
            for w in range(8):
                n_m = 4 * w + 4
                for hp in range(2):
                    tp = (0, 0) if hp == 0 else (64, 0)
                    pbase = hp * 64
                    po = PS[2][0:65, hp * 512:hp * 512 + 256]
                    pokey = ("pso", hp)

                    def sco(mp, hp=hp, w=w, tp=tp, pbase=pbase, pb=pb, kh=kh,
                            kl=kl, qh=qh, ql=ql):
                        buf = mp % 2
                        skey = ("pss", hp, buf)
                        for m in (2 * mp, 2 * mp + 1):
                            sps = PS[hp][:, buf * 512 + (m % 2) * 256:
                                         buf * 512 + (m % 2) * 256 + 256]
                            plan3 = [(kh, qh), (kh, ql), (kl, qh)]
                            for pi3, (ka, qa) in enumerate(plan3):
                                sch.op("tensor", lambda e, sps=sps, m=m, w=w, tp=tp,
                                       pbase=pbase, ka=ka, qa=qa, pi3=pi3:
                                       e.matmul(sps,
                                                ka[pbase:pbase + 64,
                                                   m * 128:(m + 1) * 128],
                                                qa[pbase:pbase + 64,
                                                   w * 256:(w + 1) * 256],
                                                start=(pi3 == 0), stop=(pi3 == 2),
                                                tile_position=tp),
                                       reads=[("kh", pb), ("kl", pb), ("qh", pb),
                                              ("ql", pb)],
                                       writes=([skey] if pi3 in (0, 2) else []))
                        for m in (2 * mp, 2 * mp + 1):
                            km = m - (4 * w)
                            if 0 <= km <= 3:
                                sps = PS[hp][:, buf * 512 + (m % 2) * 256:
                                             buf * 512 + (m % 2) * 256 + 256]
                                sch.op("vector", lambda e, sps=sps, km=km: e.tensor_add(
                                       out=sps, in0=sps,
                                       in1=maskc[:, km * 256:(km + 1) * 256]),
                                       reads=[skey, ("maskc",)], writes=[skey])
                        ptb = pt[hp][buf]
                        sch.op("scalar", lambda e, ptb=ptb, hp=hp, buf=buf:
                               e.activation(out=ptb[:], in_=PS[hp][:, buf * 512:
                                                                   buf * 512 + 512],
                                            func=AF.Exp, scale=0.125, bias=negC_ap),
                               reads=[skey, ("sconst",)], writes=[("pt", hp, buf)])

                    sco(0)
                    if hp == 0:
                        emit_norm()
                    for mp in range(n_m // 2):
                        buf = mp % 2
                        if mp + 1 < n_m // 2:
                            sco(mp + 1)
                        for m in (2 * mp, 2 * mp + 1):
                            vst = vpair[:, m * 130 + hp * 65: m * 130 + hp * 65 + 65]
                            mv = pt[hp][buf][:, (m % 2) * 256:(m % 2) * 256 + 256]
                            st, sp = (m == 0), (m == n_m - 1)
                            sch.op("tensor", lambda e, po=po, vst=vst, mv=mv, st=st,
                                   sp=sp:
                                   e.matmul(po, vst, mv, start=st, stop=sp),
                                   reads=[("vpair", pb), ("pt", hp, buf)] +
                                         ([pokey] if st else []),
                                   writes=([pokey] if st or sp else []))
                    ab = w % 2
                    sl = ab * 2 + hp
                    t_rs = sch.op("vector", lambda e, hp=hp, sl=sl: e.tensor_copy(
                           out=rsum[sl][:],
                           in_=PS[2][64:65, hp * 512:hp * 512 + 256]),
                           reads=[pokey], writes=[("rsum", sl)], small=True)
                    sch.op("vector", lambda e, sl=sl: e.reciprocal(
                           out=recs[sl][:], in_=rsum[sl][:]),
                           reads=[("rsum", sl)], writes=[("recs", sl)], small=True)
                    norm_pending.append((hp, ab, w, p_i, t_rs))

        if final_norm[0] is not None:
            final_norm[0]()
        fs2.close()
        sch.phase()
        if "p3a" in stages:
            wo_sb = fs.enter_context(nc.sbuf_tensor("wo_sb", [128, 8 * D], F32))
            x1b = [fs.enter_context(nc.sbuf_tensor(f"x1b{i}", [128, D], F32))
                   for i in range(2)]
            x2b = [fs.enter_context(nc.sbuf_tensor(f"x2b{i}", [128, D], F32))
                   for i in range(2)]
            afm_t = [fs.enter_context(nc.sbuf_tensor(f"afmt{i}", [128, D], F32))
                     for i in range(2)]
            sch.dma("sync", "wold", wo_sb[:].rearrange("p (c d) -> p c d", c=8),
                    wo_in.rearrange("(c p) d -> p c d", p=128), writes=[("wo",)])
            afm_all = [("afm", c_, w_) for c_ in range(NPAIR) for w_ in range(8)]
            for j in range(NOWN):
                bs = j % 2
                l = 2 * j
                sch.dma("sync", f"x1ld{bs}", x1b[bs][:],
                        x1_d[l * 128:(l + 1) * 128, :], writes=[("x1b", bs)])
                sch.dma("sync", f"afld{bs}",
                        afm_t[bs][:].rearrange("p (c q) -> p c q", c=8),
                        attn_d.rearrange("p (c q) -> p c q", c=8)[
                            :, :, j * 128:(j + 1) * 128],
                        reads=[("afm", c_, j // 2) for c_ in range(NPAIR)],
                        writes=[("afmt", bs)])
                psw = PS[0] if bs == 0 else PS[1]
                wkey = ("psw", bs)
                for n in range(2):
                    for c in range(8):
                        st, sp = (c == 0), (c == 7)
                        lhs = afm_t[bs][:, c * 128:(c + 1) * 128]
                        rhs = wo_sb[:, c * 1024 + n * 512: c * 1024 + n * 512 + 512]
                        sch.op("tensor", lambda e, psw=psw, lhs=lhs, rhs=rhs, st=st,
                               sp=sp, n=n:
                               e.matmul(psw[:, n * 512:(n + 1) * 512], lhs, rhs,
                                        start=st, stop=sp),
                               reads=[("wo",), ("afmt", bs)],
                               writes=([wkey] if (st or sp) else []))
                sch.op("vector", lambda e, bs=bs: e.tensor_add(
                       out=x2b[bs][:], in0=x1b[bs][:], in1=(PS[0] if bs == 0
                                                            else PS[1])[:]),
                       reads=[wkey, ("x1b", bs)], writes=[("x2b", bs)])
                sch.dma("gpsimd", f"x2sp{bs}", x2_d[j * 128:(j + 1) * 128, :],
                        x2b[bs][:], reads=[("x2b", bs)])
        fs.close()
        sch.phase()

    if "p3b" in stages:
        emit_ffn_sparse([x2_d[j * 128:(j + 1) * 128, :] for j in range(NOWN)],
                        [("out", j) for j in range(NOWN)], "fb", CAP2, base2[:],
                        ("base2",))

    finals = [o.seq for o in sch.all if o.is_dma and o.stream and
              (o.stream.startswith("osp") or o.stream.startswith("x2sp") or
               o.stream.startswith("x1sp") or o.stream.startswith("adump"))]
    sch.finish("sync", finals)
    sch.emit()
    return nc


def stage_inputs(inputs):
    x = np.asarray(inputs["L0_attn_x"], dtype=np.float32)
    wq = np.asarray(inputs["L0_attn_W_Q"], np.float32)
    wk = np.asarray(inputs["L0_attn_W_K"], np.float32)
    wv = np.asarray(inputs["L0_attn_W_V"], np.float32)
    wo = np.asarray(inputs["L0_attn_W_o"], np.float32)
    wg = np.asarray(inputs["L0_ffn_Wg"], np.float32)
    bgv = np.asarray(inputs["L0_ffn_bg"], np.float32)
    we = np.asarray(inputs["L0_ffn_We"], np.float32)
    bev = np.asarray(inputs["L0_ffn_be"], np.float32)

    perm = head_col_perm()
    wq_p = np.ascontiguousarray(wq[:, perm])
    wk_p = np.ascontiguousarray(wk[:, perm])
    wv_p = np.ascontiguousarray(wv[:, perm])
    wo_p = np.ascontiguousarray(wo[perm, :])
    bg_bc = np.ascontiguousarray(np.broadcast_to(bgv[None, :], (128, D)))
    be_bc = np.ascontiguousarray(np.broadcast_to(bev.reshape(1, E * D), (128, E * D)))

    in_maps = []
    for c in range(8):
        b, r = c // 2, c % 2
        xb = x[b].reshape(NT, 128, D)
        if r == 0:
            xl = xb
        else:
            order = [l + 1 - 2 * (l % 2) for l in range(NT)]
            xl = xb[order]
        in_maps.append({
            "x_in": np.ascontiguousarray(xl.reshape(S, D)),
            "wg_in": wg, "bg_b": bg_bc, "we_in": we, "be_b": be_bc,
            "wq_in": wq_p, "wk_in": wk_p, "wv_in": wv_p, "wo_in": wo_p,
            "maskc_in": build_masks(r),
        })
    return in_maps


def gather_output(results):
    out = np.empty((B, S, D), np.float32)
    for c in range(8):
        b, r = c // 2, c % 2
        sh = np.asarray(results[c]["out_d"]).reshape(NOWN, 128, D)
        for j in range(NOWN):
            g = 2 * j + r
            out[b, g * 128:(g + 1) * 128, :] = sh[j]
    return out


def kernel(**inputs):
    nc = build_program()
    in_maps = stage_inputs(inputs)
    res = run_bass_kernel_spmd(nc, in_maps, core_ids=list(range(8)))
    return gather_output(res.results)
